# version 13
# baseline (speedup 1.0000x reference)
import contextlib
import dataclasses
import os
import time
import numpy as np
import ml_dtypes

import concourse.bass as bass
import concourse.mybir as mybir
from concourse.bass_utils import run_bass_kernel_spmd

F32 = mybir.dt.float32
BF16 = mybir.dt.bfloat16
I32 = mybir.dt.int32
U8 = mybir.dt.uint8
AF = mybir.ActivationFunctionType
ALU = mybir.AluOpType
AX = mybir.AxisListType
nbf = ml_dtypes.bfloat16

D = 1024
DS = 64
TOK = 4096
NSEG = 4
C = 128
NCH = TOK // C
GRP = 512
NG = TOK // GRP
NPASS = 4
GPP = NG // NPASS
QTOK = TOK // NPASS
TH_ERR = float((1.0 + 1e-6) * np.log(np.float64(np.float32(0.7)) /
                                     np.float64(np.float32(0.3))))
BIG = 100.0


def _fresh_nonce():
    return ((int(time.time() * 1e6) ^ (os.getpid() << 16)) & 0x7FFFFF00) | 0x1100

ENGINES = ("PE", "DVE", "ACT", "POOL", "SP")


class _Op:
    __slots__ = ("idx", "engine", "emit", "reads", "writes", "is_dma",
                 "deps", "marked", "count", "dma_val", "tag",
                 "self_drain", "drain_val")

    def __init__(self, idx, engine, emit, reads, writes, is_dma, tag=""):
        self.idx = idx; self.engine = engine; self.emit = emit
        self.reads = tuple(reads); self.writes = tuple(writes)
        self.is_dma = is_dma; self.deps = []
        self.marked = False; self.count = None; self.dma_val = None
        self.tag = tag; self.self_drain = 0; self.drain_val = 0


class Sched:

    def __init__(self, nc):
        self.nc = nc
        self.ops = []
        self.last_writer = {}
        self.readers = {}
        self.bar = ()

    def add(self, engine, emit, reads=(), writes=(), is_dma=False, tag="",
            dep_on=()):
        reads = tuple(reads)
        writes = tuple(writes)
        psr = tuple(b for b in reads if b.startswith("ps") and b not in writes)
        if psr:
            writes = writes + psr
            reads = tuple(b for b in reads if not b.startswith("ps"))
        op = _Op(len(self.ops), engine, emit, reads, writes, is_dma, tag)
        deps = {}
        for d in tuple(dep_on) + tuple(self.bar):
            if d is not None:
                deps[d.idx] = d
        for b in op.reads:
            w = self.last_writer.get(b)
            if w is not None:
                deps[w.idx] = w
        for b in op.writes:
            w = self.last_writer.get(b)
            if w is not None:
                deps[w.idx] = w
            for rr in self.readers.get(b, ()):
                deps[rr.idx] = rr
        deps.pop(op.idx, None)
        op.deps = list(deps.values())
        for b in op.reads:
            self.readers.setdefault(b, []).append(op)
        for b in op.writes:
            self.last_writer[b] = op
            self.readers[b] = []
        self.ops.append(op)
        return op

    def set_barrier(self):
        last_op = {}
        last_dma = {}
        for op in self.ops:
            last_op[op.engine] = op
            if op.is_dma:
                last_dma[op.engine] = op
        bars = []
        for e in ENGINES:
            deps = []
            if e in last_op:
                deps.append(last_op[e])
            if e in last_dma and last_dma[e] is not last_op.get(e):
                deps.append(last_dma[e])
            b = self.add(e, (lambda eng: eng.nop()), tag="phase_bar",
                         dep_on=tuple(deps))
            bars.append(b)
        self.bar = tuple(bars)

    def prepare(self):
        issued = {e: 0 for e in ENGINES}
        drains = {e: None for e in ENGINES}
        new_ops = []
        nidx = 10 ** 7
        for op in self.ops:
            ndeps = []
            self_drain = 0
            for d in op.deps:
                if not d.is_dma:
                    ndeps.append(d)
                    continue
                q = d.engine
                if q == op.engine:
                    self_drain = max(self_drain, 16 * issued[q])
                else:
                    dr = drains[q]
                    if dr is None or dr[0] < d.dma_val:
                        dop = _Op(nidx, q, (lambda e: e.nop()), (), (),
                                  False, tag="drain")
                        nidx += 1
                        dop.deps = []
                        dop.drain_val = 16 * issued[q]
                        dop.self_drain = 0
                        new_ops.append(dop)
                        dr = (16 * issued[q], dop)
                        drains[q] = dr
                    ndeps.append(dr[1])
            op.deps = ndeps
            op.self_drain = self_drain
            new_ops.append(op)
            if op.is_dma:
                issued[op.engine] += 1
                op.dma_val = 16 * issued[op.engine]
        self.ops = new_ops
        for op in self.ops:
            for d in op.deps:
                d.marked = True
        counts = {e: 0 for e in ENGINES}
        for op in self.ops:
            if not op.is_dma and op.marked:
                counts[op.engine] += 1
                op.count = counts[op.engine]

    def emit_engine(self, ename, eng, sems, dsems):
        waited = {e: 0 for e in ENGINES}
        dwaited = 0
        for op in self.ops:
            if op.engine != ename:
                continue
            sd = max(getattr(op, "self_drain", 0),
                     getattr(op, "drain_val", 0))
            if sd > dwaited:
                eng.wait_ge(dsems[ename], sd)
                dwaited = sd
            for d in op.deps:
                if d.count is not None and d.count > waited[d.engine]:
                    eng.wait_ge(sems[d.engine], d.count)
                    waited[d.engine] = d.count
            inst = op.emit(eng)
            if op.is_dma:
                inst.then_inc(dsems[ename], 16)
            elif op.marked:
                inst.then_inc(sems[ename], 1)

    def run(self):
        nc = self.nc
        self.prepare()
        stack = contextlib.ExitStack()
        sems = {e: stack.enter_context(nc.semaphore("S" + e)) for e in ENGINES}
        dsems = {e: stack.enter_context(nc.semaphore("D" + e)) for e in ENGINES}
        self._sem_stack = stack
        with nc.Block() as block:
            block.tensor(lambda eng: self.emit_engine("PE", eng, sems, dsems))
            block.vector(lambda eng: self.emit_engine("DVE", eng, sems, dsems))
            block.scalar(lambda eng: self.emit_engine("ACT", eng, sems, dsems))
            block.gpsimd(lambda eng: self.emit_engine("POOL", eng, sems, dsems))
            block.sync(lambda eng: self.emit_engine("SP", eng, sems, dsems))


def _bpar(dram_ap, n=128):
    return dataclasses.replace(dram_ap, ap=[[0, n]] + list(dram_ap.ap[1:]))


def _rep(ap, n):
    return dataclasses.replace(ap, ap=list(ap.ap[:1]) + [[0, n]] + list(ap.ap[1:]))


def _repf(ap, n):
    return dataclasses.replace(ap, ap=list(ap.ap[:1]) + [[0, n]])


def build_all():
    nc = bass.Bass("TRN2", target_bir_lowering=False, detect_race_conditions=False)
    ctx = contextlib.ExitStack()
    ctxA = contextlib.ExitStack()

    def din(name, shape, dtype=F32):
        return nc.dram_tensor(name, shape, dtype, kind="ExternalInput").ap()

    def sbuf(name, shape, dtype=F32):
        return ctx.enter_context(nc.sbuf_tensor(name, shape, dtype)).ap()

    def sbufA(name, shape, dtype=F32):
        return ctxA.enter_context(nc.sbuf_tensor(name, shape, dtype)).ap()

    def psumA(name, shape, dtype=F32):
        return ctxA.enter_context(nc.psum_tensor(name, shape, dtype)).ap()

    dXC = din("xC", [NSEG * TOK // C * 128, D])
    dWkT = din("WkT", [D, DS]); dWvT = din("WvT", [D, DS])
    dWqT = din("WqT", [D, DS]); dIR1T = din("ir1T", [D, DS])
    dIR2T = din("ir2T", [DS, 1]); dPWT = din("pwT", [D, 3])
    dNM1T = din("nm1T", [DS, 2 * DS]); dNM2T = din("nm2T", [2 * DS, DS])
    dNMB1 = din("nmb1", [2 * DS, 1]); dNMB2 = din("nmb2", [DS, 1])
    dIRB1 = din("irb1", [DS, 1]); dIRB2 = din("irb2", [1, 1])
    dWoT = din("WoT", [DS, D]); dNW = din("normw", [1, DS])
    dGWT = din("gwT", [D, 8]); dLB = din("lb", [1, 8])
    dFA = din("fa", [1, 1]); dSA = din("sa", [1, 1])
    dID = din("identb", [128, 128], BF16)
    dEX = din("exTS", [128, 128]); dEXT = din("exST", [128, 128])
    dMSL = din("mskSL", [128, 128]); dMSU = din("mskSU", [128, 128])
    dIOR = din("iotaRow", [1, 128]); dIOC = din("iotaCol", [128, 1])
    dIOCR = din("iotaColRev", [128, 1])
    dRELAY = din("relay", [1, 4], I32)
    dFQ = din("fq", [1, 1], I32)
    dWG = din("wgT", [10, D, GRP])
    dWU = din("wuT", [10, D, GRP])
    dWD = din("wdT", [10, GRP, D])

    oOut = nc.dram_tensor("outT", [TOK // C * 128, D], F32,
                          kind="ExternalOutput").ap()

    x1Scr = nc.dram_tensor("x1Scr", [NSEG * TOK // C * 128, D], F32,
                           kind="Internal").ap()
    statsScr = nc.dram_tensor("statsScr", [NSEG, 2 * TOK], F32,
                              kind="Internal").ap()
    selsScr = nc.dram_tensor("selsScr", [NSEG, 8 * TOK], BF16,
                             kind="Internal").ap()
    scrWG = nc.dram_tensor("scrWG", [10, 128, 8 * GRP], BF16, kind="Internal").ap()
    scrWU = nc.dram_tensor("scrWU", [10, 128, 8 * GRP], BF16, kind="Internal").ap()
    scrWD = nc.dram_tensor("scrWD", [10, 128, 4 * D], BF16, kind="Internal").ap()
    wmScr = nc.dram_tensor("wmScr", [1, 16], F32, kind="Internal").ap()
    x1Own = nc.dram_tensor("x1Own", [TOK // C * 128, D], F32,
                           kind="Internal").ap()
    statsOwn = nc.dram_tensor("statsOwn", [1, 2 * TOK], F32,
                              kind="Internal").ap()
    selsOwn = nc.dram_tensor("selsOwn", [1, 8 * TOK], BF16,
                             kind="Internal").ap()
    shPart = nc.dram_tensor("shPart", [2, 16], F32, kind="Internal",
                            addr_space="Shared").ap()
    shFlag = nc.dram_tensor("shFlag", [2, 2], I32, kind="Internal",
                            addr_space="Shared").ap()

    def xchunk(gc):
        return dXC[gc * 128:(gc + 1) * 128, :]

    def x1w2(seg, i):
        gc = seg * NCH + i
        return x1Scr[gc * 128:(gc + 1) * 128, :]

    def x1rd3(p_i):
        return dataclasses.replace(
            x1Own, offset=x1Own.offset + p_i * 8 * 128 * D,
            ap=[[D, 128], [128 * D, 8], [1, D]])

    def outw3(p_i):
        return dataclasses.replace(
            oOut, offset=oOut.offset + p_i * 8 * 128 * D,
            ap=[[D, 128], [128 * D, 8], [1, D]])

    def selsw3(seg, g4):
        return dataclasses.replace(
            selsScr, offset=selsScr.offset + seg * (8 * TOK) + g4 * 512,
            ap=[[128, 4], [TOK, 8], [1, 128]])

    sRelay = sbuf("sRelay", [1, 4], I32)
    sNonceS = sbuf("sNonceS", [1, 1], I32)
    sFlagRx = sbuf("sFlagRx", [1, 1], I32)
    sPart = sbuf("sPart", [1, 16])
    sTxPart = sbuf("sTxPart", [1, 16])
    sRxPart = sbuf("sRxPart", [1, 16])
    sWm = sbuf("sWm", [1, 16])
    sWmbc = sbuf("sWmbc", [128, 16])
    sFq = sbuf("sFq", [1, 1], I32)

    sWall = sbufA("sWall", [128, 8 * 195], BF16)
    sIR1T = sbufA("sIR1T", [128, 8 * DS], BF16)
    sGWT = sbufA("sGWT", [128, 8 * 8], BF16)
    sNM1T = sbufA("sNM1T", [DS, 2 * DS], BF16)
    sNM2T = sbufA("sNM2T", [2 * DS, DS], BF16)
    sIR2T = sbufA("sIR2T", [DS, 1], BF16)
    sWoT = sbufA("sWoT", [DS, D], BF16)
    sNW = sbufA("sNW", [128, DS])
    sLB = sbufA("sLB", [128, 8])
    sID = sbufA("sID", [128, 128], BF16)
    sIDF = sbufA("sIDF", [128, 128])
    sEX = sbufA("sEX", [128, 128]); sEXT = sbufA("sEXT", [128, 128])
    sMSL = sbufA("sMSL", [128, 128]); sMSU = sbufA("sMSU", [128, 128])
    sIOR = sbufA("sIOR", [1, 128]); sIOC = sbufA("sIOC", [128, 1])
    sIOCR = sbufA("sIOCR", [128, 1]); sIORB = sbufA("sIORB", [128, 128])
    sONE1 = sbufA("sONE1", [1, 128], BF16)
    sONE128 = sbufA("sONE128", [128, 1], BF16)
    sFA = sbufA("sFA", [128, 1]); sSA = sbufA("sSA", [128, 1])
    sLNFA = sbufA("sLNFA", [128, 1]); sLNSA = sbufA("sLNSA", [128, 1])
    sBF = sbufA("sBF", [128, 1]); sBS = sbufA("sBS", [128, 1])
    sNBF = sbufA("sNBF", [128, 1]); sNBS = sbufA("sNBS", [128, 1])
    sFAC = sbufA("sFAC", [128, 1]); sSAC = sbufA("sSAC", [128, 1])
    sDQFC = sbufA("sDQFC", [128, 1]); sDQSC = sbufA("sDQSC", [128, 1])
    sDKFC = sbufA("sDKFC", [128, 1]); sDKSC = sbufA("sDKSC", [128, 1])
    sNMB1 = sbufA("sNMB1", [2 * DS, 1]); sNMB2 = sbufA("sNMB2", [DS, 1])
    sIRB1 = sbufA("sIRB1", [DS, 1]); sIRB2 = sbufA("sIRB2", [128, 1])
    sDQF = sbufA("sDQF", [128, 128]); sDQS = sbufA("sDQS", [128, 128])
    sEt = sbufA("sEt", [128, 128])
    sNegE_f = sbufA("sNegE_f", [128, 128], BF16)
    sNegET_f = sbufA("sNegET_f", [128, 128], BF16)
    sPosET_f = sbufA("sPosET_f", [128, 128])
    sNegE_s = sbufA("sNegE_s", [128, 128], BF16)
    sNegET_s = sbufA("sNegET_s", [128, 128], BF16)
    sPosET_s = sbufA("sPosET_s", [128, 128])
    sHH = sbufA("sHH", [DS, 2 * DS], BF16)
    sNS0 = sbufA("sNS0", [DS, 1])
    sZeroHH = sbufA("sZeroHH", [DS, 2 * DS])
    sOHSG = sbufA("sOHSG", [128, 32], BF16)
    sSelT = sbufA("sSelT", [32, 128], BF16)
    sIOTA8 = sbufA("sIOTA8", [128, 8])
    sIOTA8B = sbufA("sIOTA8B", [128, 8])
    sNWrowF = sbufA("sNWrowF", [1, DS])
    sEPS5 = sbufA("sEPS5", [1, 1])
    sNWrowB = sbufA("sNWrowB", [1, DS], BF16)

    aMhT = sbufA("aMhT", [DS, NCH * 2 * C], BF16)
    aD0 = sbufA("aD0", [128, NCH * 128])
    aGqTf = sbufA("aGqTf", [128, NCH * C], BF16)
    aGqTs = sbufA("aGqTs", [128, NCH * C], BF16)
    aQhTf = sbufA("aQhTf", [DS, NCH * C], BF16)
    aQhTs = sbufA("aQhTs", [DS, NCH * C], BF16)
    aKpf = sbufA("aKpf", [128, NCH * DS], BF16)
    aKps = sbufA("aKps", [128, NCH * DS], BF16)
    aVtok = sbufA("aVtok", [128, NCH * DS])
    aNuT = sbufA("aNuT", [DS, NCH * C])
    aPwt = sbufA("aPwt", [128, NCH * 4])
    aQkf = sbufA("aQkf", [128, NCH])
    aQks = sbufA("aQks", [128, NCH])
    aHH = sbufA("aHH", [DS, NCH * 2 * DS], BF16)
    aDb = sbufA("aDb", [128, NCH * 128], BF16)
    aRbf = sbufA("aRbf", [128, NCH * DS], BF16)
    aRbs = sbufA("aRbs", [128, NCH * DS], BF16)
    aUnb = sbufA("aUnb", [128, NCH], BF16)
    aNsTok = sbufA("aNsTok", [128, NCH * DS])

    R = 2
    rg = {}
    def ring(nm, shape, dtype=F32):
        rg[nm] = [sbufA(f"r{r}_{nm}", shape, dtype) for r in range(R)]
    ring("xch", [128, 8 * C])
    ring("xtl", [128, 8 * C])
    ring("xhat", [128, 8 * C], BF16)
    ring("lnt", [128, 8 * C])
    ring("xbq", [128, 16 * C], BF16)
    ring("mrow", [1, C]); ring("rrow", [1, C])
    ring("mrr", [1, 2 * C], BF16)
    ring("kT", [DS, C], BF16); ring("qT", [DS, C], BF16)
    ring("ktT", [DS, C], BF16); ring("vT", [DS, C], BF16)
    ring("ktok", [C, DS]); ring("qtok", [C, DS])
    ring("kttok", [C, DS])
    ring("kb", [C, DS], BF16); ring("qb", [C, DS], BF16)
    ring("ktb", [C, DS], BF16); ring("vb", [C, DS], BF16)
    ring("mb", [C, DS], BF16)
    ring("Rf", [C, 128], BF16); ring("Rs", [C, 128], BF16)
    ring("Ntf", [128, 128], BF16); ring("Nnf", [128, 128], BF16)
    ring("Nts", [128, 128], BF16); ring("Nns", [128, 128], BF16)
    ring("Sp1", [128, 256], BF16); ring("Sp2", [128, 256], BF16)
    ring("Sp3", [128, 256], BF16); ring("Sp4", [128, 256], BF16)
    ring("Xa", [C, 128], BF16); ring("Xb", [C, 128], BF16)
    ring("Xc", [C, 128], BF16); ring("Xd", [C, 128], BF16)
    ring("nu1b", [128, C], BF16)
    ring("z1b", [DS, C], BF16)
    ring("ucol", [C, 1])
    ring("sc1", [C, DS]); ring("sc2", [C, DS])
    ring("errc", [C, 1])
    ring("urow", [1, C], BF16)
    ring("cbc", [DS, C]); ring("wns", [DS, C]); ring("nsT", [DS, C])
    ring("rf", [C, DS]); ring("rs", [C, DS])
    ring("o1", [C, DS]); ring("o2", [C, DS]); ring("msc", [C, 1])
    ring("ohat", [C, DS], BF16); ring("ohT", [DS, C], BF16)
    ring("scr", [128, 8]); ring("sc8a", [128, 8]); ring("sc8b", [128, 8])
    ring("c1", [128, 1]); ring("c2", [128, 1]); ring("c3", [128, 1])
    ring("ohw", [128, 8], BF16); ring("ohs", [128, 8])

    psTok = psumA("psTok", [128, 512])
    psFeat = psumA("psFeat", [128, 384])
    psSq = psumA("psSq", [128, 256])
    psSq2 = psumA("psSq2", [128, 256])
    psX = psumA("psX", [C, 128])
    psX2 = psumA("psX2", [C, 128])
    psChO = psumA("psChO", [128, 512])
    psTr = psumA("psTr", [128, 512], BF16)

    spin_sem = ctx.enter_context(nc.semaphore("SPIN"))

    s = Sched(nc)
    A = s.add
    out_dmas = []

    def N(ap_):
        return ap_.tensor.name

    for dst, src in [(sIR1T, dIR1T), (sGWT, dGWT)]:
        A("POOL", (lambda d_, s_: lambda e: e.dma_start(
            out=d_, in_=s_.rearrange("(g p) i -> p g i", p=128)))(dst, src),
          writes=[N(dst)], is_dma=True)
    for src, off, w in [(dWkT, 0, DS), (dWvT, DS, DS), (dWqT, 2 * DS, DS),
                        (dPWT, 192, 3)]:
        dstv = dataclasses.replace(
            sWall, offset=sWall.offset + off,
            ap=[list(sWall.ap[0]), [195, 8], [1, w]])
        A("POOL", (lambda d_, s_: lambda e: e.dma_start(
            out=d_, in_=s_.rearrange("(g p) i -> p g i", p=128)))(dstv, src),
          writes=["sWall"], is_dma=True)
    for dst, src in [(sNM1T, dNM1T), (sNM2T, dNM2T), (sIR2T, dIR2T),
                     (sWoT, dWoT)]:
        A("POOL", (lambda d_, s_: lambda e: e.dma_start(out=d_, in_=s_))(dst, src),
          writes=[N(dst)], is_dma=True)
    for dst, src in [(sID, dID), (sEX, dEX), (sEXT, dEXT), (sMSL, dMSL),
                     (sMSU, dMSU), (sIOR, dIOR), (sIOC, dIOC), (sIOCR, dIOCR),
                     (sNMB1, dNMB1), (sNMB2, dNMB2), (sIRB1, dIRB1),
                     (sRelay, dRELAY), (sNWrowF, dNW), (sFq, dFQ)]:
        A("SP", (lambda d_, s_: lambda e: e.dma_start(out=d_, in_=s_))(dst, src),
          writes=[N(dst)], is_dma=True)
    A("SP", lambda e: e.dma_start(out=sFA, in_=_bpar(dFA)), writes=["sFA"], is_dma=True)
    A("SP", lambda e: e.dma_start(out=sSA, in_=_bpar(dSA)), writes=["sSA"], is_dma=True)
    A("SP", lambda e: e.dma_start(out=sIRB2, in_=_bpar(dIRB2)), writes=["sIRB2"], is_dma=True)
    A("SP", lambda e: e.dma_start(out=sLB, in_=_bpar(dLB)), writes=["sLB"], is_dma=True)
    A("SP", lambda e: e.dma_start(out=sNonceS, in_=dRELAY[0:1, 3:4]), writes=["sNonceS"], is_dma=True)
    A("SP", lambda e: e.dma_start(out=sIORB, in_=_bpar(dIOR)), writes=["sIORB"], is_dma=True)
    A("SP", lambda e: e.dma_start(out=sIOTA8, in_=dataclasses.replace(
        dIOR, ap=[[0, 128], [1, 8]])), writes=["sIOTA8"], is_dma=True)

    fqreg = {}

    def ld_fq(e):
        fqreg[0] = e.value_load(sFq)
        return e.nop()
    A("SP", ld_fq, reads=["sFq"], writes=["fqreg"])

    for e_i in range(10):
        for kind, src, dst, pat in [
                ("g", dWG, scrWG, "(g p) i -> p g i"),
                ("u", dWU, scrWU, "(g p) i -> p g i"),
                ("d", dWD, scrWD, "(c p) h -> p c h")]:
            A("POOL", (lambda s_, d_, ei_, pat_: lambda e: e.dma_start(
                out=d_[ei_], in_=s_[ei_].rearrange(pat_, p=128)))(
                    src, dst, e_i, pat),
              writes=[f"cast{kind}{e_i}"], is_dma=True)

    A("DVE", lambda e: e.memset(sONE1, 1.0), writes=["sONE1"])
    A("DVE", lambda e: e.memset(sONE128, 1.0), writes=["sONE128"])
    A("DVE", lambda e: e.memset(sPart, 0.0), writes=["sPart"])
    A("DVE", lambda e: e.memset(sEPS5, 1e-5), writes=["sEPS5"])
    A("DVE", lambda e: e.memset(sZeroHH, 0.0), writes=["sZeroHH"])
    A("DVE", lambda e: e.tensor_scalar(sIOTA8B, sIOTA8, BIG, None, ALU.add),
      reads=["sIOTA8"], writes=["sIOTA8B"])
    A("DVE", lambda e: e.tensor_copy(out=sNWrowB, in_=sNWrowF),
      reads=["sNWrowF"], writes=["sNWrowB"])
    A("PE", lambda e: e.matmul(psFeat[:, 0:DS], sONE1, sNWrowB, start=True, stop=True),
      reads=["sONE1", "sNWrowB"], writes=["psFeat"])
    A("DVE", lambda e: e.tensor_copy(out=sNW, in_=psFeat[:, 0:DS]),
      reads=["psFeat"], writes=["sNW"])
    A("DVE", lambda e: e.tensor_copy(out=sIDF, in_=sID),
      reads=["sID"], writes=["sIDF"])

    A("ACT", lambda e: e.activation(sLNFA, sFA, AF.Ln), reads=["sFA"], writes=["sLNFA"])
    A("ACT", lambda e: e.activation(sLNSA, sSA, AF.Ln), reads=["sSA"], writes=["sLNSA"])
    A("DVE", lambda e: e.tensor_scalar(sBF, sFA, -1.0, 1.0, ALU.mult, ALU.add),
      reads=["sFA"], writes=["sBF"])
    A("DVE", lambda e: e.tensor_scalar(sBS, sSA, -1.0, 1.0, ALU.mult, ALU.add),
      reads=["sSA"], writes=["sBS"])
    A("DVE", lambda e: e.tensor_scalar(sNBF, sBF, -1.0, None, ALU.mult),
      reads=["sBF"], writes=["sNBF"])
    A("DVE", lambda e: e.tensor_scalar(sNBS, sBS, -1.0, None, ALU.mult),
      reads=["sBS"], writes=["sNBS"])
    A("ACT", lambda e: e.activation(sFAC, sLNFA, AF.Exp, scale=float(C)),
      reads=["sLNFA"], writes=["sFAC"])
    A("ACT", lambda e: e.activation(sSAC, sLNSA, AF.Exp, scale=float(C)),
      reads=["sLNSA"], writes=["sSAC"])
    A("ACT", lambda e: e.activation(sDQFC, sIOC, AF.Exp, scale=sLNFA),
      reads=["sIOC", "sLNFA"], writes=["sDQFC"])
    A("ACT", lambda e: e.activation(sDQSC, sIOC, AF.Exp, scale=sLNSA),
      reads=["sIOC", "sLNSA"], writes=["sDQSC"])
    A("ACT", lambda e: e.activation(sDKFC, sIOCR, AF.Exp, scale=sLNFA),
      reads=["sIOCR", "sLNFA"], writes=["sDKFC"])
    A("ACT", lambda e: e.activation(sDKSC, sIOCR, AF.Exp, scale=sLNSA),
      reads=["sIOCR", "sLNSA"], writes=["sDKSC"])
    A("DVE", lambda e: e.tensor_scalar(sDKFC, sDKFC, sBF, None, ALU.mult),
      reads=["sBF"], writes=["sDKFC"])
    A("DVE", lambda e: e.tensor_scalar(sDKSC, sDKSC, sBS, None, ALU.mult),
      reads=["sBS"], writes=["sDKSC"])
    A("ACT", lambda e: e.activation(sDQF, sIORB, AF.Exp, scale=sLNFA),
      reads=["sIORB", "sLNFA"], writes=["sDQF"])
    A("ACT", lambda e: e.activation(sDQS, sIORB, AF.Exp, scale=sLNSA),
      reads=["sIORB", "sLNSA"], writes=["sDQS"])
    for tag, lncol, bcol, nbcol, negE, negET, posET in [
        ("f", sLNFA, sBF, sNBF, sNegE_f, sNegET_f, sPosET_f),
        ("s", sLNSA, sBS, sNBS, sNegE_s, sNegET_s, sPosET_s),
    ]:
        A("ACT", (lambda ln_: lambda e: e.activation(sEt, sEX, AF.Exp, scale=ln_))(lncol),
          reads=["sEX", N(lncol)], writes=["sEt"])
        A("DVE", (lambda nb_, d_: lambda e: e.scalar_tensor_tensor(
            d_, sEt, nb_, sMSL, ALU.mult, ALU.mult))(nbcol, negE),
          reads=["sEt", N(nbcol), "sMSL"], writes=[N(negE)])
        A("ACT", (lambda ln_: lambda e: e.activation(sEt, sEXT, AF.Exp, scale=ln_))(lncol),
          reads=["sEXT", N(lncol)], writes=["sEt"])
        A("DVE", (lambda nb_, d_: lambda e: e.scalar_tensor_tensor(
            d_, sEt, nb_, sMSU, ALU.mult, ALU.mult))(nbcol, negET),
          reads=["sEt", N(nbcol), "sMSU"], writes=[N(negET)])
        A("DVE", (lambda b_, d_: lambda e: e.scalar_tensor_tensor(
            d_, sEt, b_, sMSU, ALU.mult, ALU.mult))(bcol, posET),
          reads=["sEt", N(bcol), "sMSU"], writes=[N(posET)])

    def emit_ln(xsrc, xkey, r, i, stats_seg):
        xv = xsrc.rearrange("p (g t) -> p g t", g=8)
        xbq, lnt = rg["xbq"][r], rg["lnt"][r]
        mrow, rrow = rg["mrow"][r], rg["rrow"][r]
        mrr = rg["mrr"][r]
        mrowb, rrowb = mrr[:, 0:C], mrr[:, C:2 * C]
        xhat = rg["xhat"][r]
        bfv = dataclasses.replace(xbq, ap=[list(xbq.ap[0]), [2 * C, 8], [1, C]])
        sqv = dataclasses.replace(xbq, offset=xbq.offset + C,
                                  ap=[list(xbq.ap[0]), [2 * C, 8], [1, C]])
        A("DVE", lambda e: e.tensor_copy(out=bfv, in_=xv),
          reads=[xkey], writes=[N(xbq)])
        A("ACT", lambda e: e.activation(sqv, xv, AF.Square),
          reads=[xkey], writes=[N(xbq)])
        def mm_sums(e):
            last = None
            for g in range(8):
                last = e.matmul(psFeat[0:1, 128:384], sONE128,
                                xbq[:, g * 2 * C:(g + 1) * 2 * C],
                                start=(g == 0), stop=(g == 7))
            return last
        A("PE", mm_sums, reads=[N(xbq), "sONE128"], writes=["psFeat"])
        A("DVE", lambda e: e.tensor_scalar(mrow, psFeat[0:1, 128:128 + C],
                                           1.0 / D, None, ALU.mult),
          reads=["psFeat"], writes=[N(mrow)])
        A("DVE", lambda e: e.scalar_tensor_tensor(
            rrow, mrow, -1.0, mrow, ALU.mult, ALU.mult),
          reads=[N(mrow)], writes=[N(rrow)])
        A("DVE", lambda e: e.scalar_tensor_tensor(
            rrow, psFeat[0:1, 256:256 + C], 1.0 / D, rrow, ALU.mult, ALU.add),
          reads=["psFeat", N(rrow)], writes=[N(rrow)])
        A("ACT", lambda e: e.activation(rrow, rrow, AF.Ln, bias=sEPS5),
          reads=[N(rrow), "sEPS5"], writes=[N(rrow)])
        A("ACT", lambda e: e.activation(rrow, rrow, AF.Exp, scale=-0.5),
          reads=[N(rrow)], writes=[N(rrow)])
        if stats_seg is not None:
            cs = i * C
            sg = stats_seg
            op = A("SP", (lambda cs_, sg_, m_: lambda e: e.dma_start(
                out=statsScr[sg_:sg_ + 1, cs_:cs_ + C], in_=m_))(cs, sg, mrow),
                reads=[N(mrow)], is_dma=True)
            out_dmas.append(op)
            op = A("SP", (lambda cs_, sg_, r_: lambda e: e.dma_start(
                out=statsScr[sg_:sg_ + 1, TOK + cs_:TOK + cs_ + C],
                in_=r_))(cs, sg, rrow),
                reads=[N(rrow)], writes=[f"dstats@{i}"], is_dma=True)
            out_dmas.append(op)
        A("POOL", lambda e: e.tensor_copy(out=mrowb, in_=mrow),
          reads=[N(mrow)], writes=[N(mrr)])
        A("POOL", lambda e: e.tensor_copy(out=rrowb, in_=rrow),
          reads=[N(rrow)], writes=[N(mrr)])
        A("PE", lambda e: e.matmul(psFeat[:, 128:384], sONE1, mrr,
                                   start=True, stop=True),
          reads=[N(mrr), "sONE1"], writes=["psFeat"])
        A("DVE", lambda e: e.tensor_tensor(
            lnt.rearrange("p (g t) -> p g t", g=8), xv,
            _rep(psFeat[:, 128:128 + C], 8), ALU.subtract),
          reads=[xkey, "psFeat"], writes=[N(lnt)])
        A("DVE", lambda e: e.tensor_tensor(
            xhat.rearrange("p (g t) -> p g t", g=8),
            lnt.rearrange("p (g t) -> p g t", g=8),
            _rep(psFeat[:, 256:256 + C], 8), ALU.mult),
          reads=[N(lnt), "psFeat"], writes=[N(xhat)])
        return xhat

    def emit_prep(seg, i):
        r = i % R
        cs = i * C
        xch = rg["xch"][r]
        gc = seg * NCH + i
        A("POOL", (lambda gc_, x_: lambda e: e.dma_start(
            out=x_, in_=xchunk(gc_)))(gc, xch),
          writes=[N(xch)], is_dma=True)
        xhat = emit_ln(xch, N(xch), r, i, stats_seg=None)
        def mm_proj(e):
            last = None
            for g in range(8):
                last = e.matmul(psTok[:, 0:195], xhat[:, g * C:(g + 1) * C],
                                sWall[:, g * 195:(g + 1) * 195],
                                start=(g == 0), stop=(g == 7))
            return last
        A("PE", mm_proj, reads=[N(xhat), "sWall"], writes=["psTok"])
        def mm_imp(e):
            last = None
            for g in range(8):
                last = e.matmul(psFeat[0:DS, 0:C],
                                sIR1T[:, g * DS:(g + 1) * DS],
                                xhat[:, g * C:(g + 1) * C],
                                start=(g == 0), stop=(g == 7))
            return last
        A("PE", mm_imp, reads=[N(xhat), "sIR1T"], writes=["psFeat"])
        ktok, qtok = rg["ktok"][r], rg["qtok"][r]
        vtok = aVtok[:, i * DS:(i + 1) * DS]
        for nm, off, dst in [("k", 0, ktok), ("q", 2 * DS, qtok)]:
            c1, c2 = rg["c1"][r], rg["c2"][r]
            A("ACT", (lambda o_, c_: lambda e: e.activation(
                rg["sc1"][r], psTok[:, o_:o_ + DS], AF.Square, accum_out=c_))(off, c1),
              reads=["psTok"], writes=[N(rg["sc1"][r]), N(c1)])
            A("ACT", (lambda c_: lambda e: e.activation(c_, c_, AF.Ln))(c1),
              reads=[], writes=[N(c1)])
            A("ACT", (lambda c_: lambda e: e.activation(c_, c_, AF.Exp, scale=0.5))(c1),
              reads=[], writes=[N(c1)])
            A("DVE", (lambda c_: lambda e: e.tensor_scalar(
                c_, c_, 1e-12, None, ALU.max))(c1),
              reads=[], writes=[N(c1)])
            A("DVE", (lambda a_, b_: lambda e: e.reciprocal(b_, a_))(c1, c2),
              reads=[N(c1)], writes=[N(c2)])
            A("DVE", (lambda o_, d_, c_: lambda e: e.tensor_scalar(
                d_, psTok[:, o_:o_ + DS], c_, None, ALU.mult))(off, dst, c2),
              reads=["psTok", N(c2)], writes=[N(dst)])
        A("ACT", (lambda v_: lambda e: e.activation(
            v_, psTok[:, DS:2 * DS], AF.Copy))(vtok),
          reads=["psTok"], writes=[f"aVtok@{i}"])
        pwt = aPwt[:, i * 4:i * 4 + 4]
        c3 = rg["c3"][r]
        A("DVE", lambda e: e.reduce_max(c3, psTok[:, 192:195], AX.X),
          reads=["psTok"], writes=[N(c3)])
        A("DVE", lambda e: e.tensor_scalar(rg["c1"][r], c3, -1.0, None, ALU.mult),
          reads=[N(c3)], writes=[N(rg["c1"][r])])
        A("ACT", (lambda p_: lambda e: e.activation(
            p_[:, 0:3], psTok[:, 192:195], AF.Exp, bias=rg["c1"][r]))(pwt),
          reads=["psTok", N(rg["c1"][r])], writes=[f"aPwt@{i}"])
        A("DVE", (lambda p_: lambda e: e.reduce_sum(c3, p_[:, 0:3], AX.X))(pwt),
          reads=[f"aPwt@{i}"], writes=[N(c3)])
        A("DVE", lambda e: e.reciprocal(c3, c3), reads=[], writes=[N(c3)])
        A("DVE", (lambda p_: lambda e: e.tensor_scalar(
            p_[:, 0:3], p_[:, 0:3], c3, None, ALU.mult))(pwt),
          reads=[N(c3)], writes=[f"aPwt@{i}"])
        ucol = rg["ucol"][r]
        A("ACT", lambda e: e.activation(rg["z1b"][r], psFeat[0:DS, 0:C], AF.Gelu,
                                        bias=sIRB1),
          reads=["psFeat", "sIRB1"], writes=[N(rg["z1b"][r])])
        A("PE", lambda e: e.matmul(psFeat[:, 380:381], rg["z1b"][r], sIR2T,
                                   start=True, stop=True),
          reads=[N(rg["z1b"][r]), "sIR2T"], writes=["psFeat"])
        A("DVE", lambda e: e.tensor_scalar(ucol, psFeat[:, 380:381], sIRB2, 0.0,
                                           ALU.add, ALU.is_gt),
          reads=["psFeat", "sIRB2"], writes=[N(ucol)])
        kttok = rg["kttok"][r]
        A("DVE", lambda e: e.tensor_scalar(kttok, ktok, ucol, None, ALU.mult),
          reads=[N(ktok), N(ucol)], writes=[N(kttok)])
        A("DVE", (lambda kp_: lambda e: e.tensor_scalar(
            kp_, ktok, sDKFC, None, ALU.mult))(aKpf[:, i * DS:(i + 1) * DS]),
          reads=[N(ktok), "sDKFC"], writes=[f"aKpf@{i}"])
        A("DVE", (lambda kp_: lambda e: e.tensor_scalar(
            kp_, kttok, sDKSC, None, ALU.mult))(aKps[:, i * DS:(i + 1) * DS]),
          reads=[N(kttok), "sDKSC"], writes=[f"aKps@{i}"])
        A("DVE", lambda e: e.tensor_tensor(rg["sc1"][r], qtok, ktok, ALU.mult),
          reads=[N(qtok), N(ktok)], writes=[N(rg["sc1"][r])])
        A("DVE", (lambda q_: lambda e: e.reduce_sum(q_, rg["sc1"][r], AX.X))(
            aQkf[:, i:i + 1]),
          reads=[N(rg["sc1"][r])], writes=[f"aQkf@{i}"])
        A("DVE", lambda e: e.tensor_tensor(rg["sc2"][r], qtok, kttok, ALU.mult),
          reads=[N(qtok), N(kttok)], writes=[N(rg["sc2"][r])])
        A("DVE", (lambda q_: lambda e: e.reduce_sum(q_, rg["sc2"][r], AX.X))(
            aQks[:, i:i + 1]),
          reads=[N(rg["sc2"][r])], writes=[f"aQks@{i}"])
        Rf, Rs = rg["Rf"][r], rg["Rs"][r]
        A("POOL", (lambda v_: lambda e: e.tensor_copy(out=Rf[:, 0:DS], in_=v_))(vtok),
          reads=[f"aVtok@{i}"], writes=[N(Rf)])
        A("DVE", lambda e: e.tensor_scalar(Rf[:, DS:128], ktok, sDQFC, None, ALU.mult),
          reads=[N(ktok), "sDQFC"], writes=[N(Rf)])
        A("POOL", (lambda v_: lambda e: e.tensor_copy(out=Rs[:, 0:DS], in_=v_))(vtok),
          reads=[f"aVtok@{i}"], writes=[N(Rs)])
        A("DVE", lambda e: e.tensor_scalar(Rs[:, DS:128], kttok, sDQSC, None, ALU.mult),
          reads=[N(kttok), "sDQSC"], writes=[N(Rs)])
        for nm, srcap, srck in [("kb", ktok, N(ktok)), ("qb", qtok, N(qtok)),
                                ("ktb", kttok, N(kttok)),
                                ("vb", vtok, f"aVtok@{i}")]:
            A("DVE", (lambda d_, s_: lambda e: e.tensor_copy(out=d_, in_=s_))(
                rg[nm][r], srcap),
              reads=[srck], writes=[N(rg[nm][r])])
        def mm_tr1(e):
            e.transpose(psTr[0:DS, 0:C], rg["kb"][r], sID)
            return e.transpose(psTr[0:DS, C:2 * C], rg["qb"][r], sID)
        A("PE", mm_tr1, reads=[N(rg["kb"][r]), N(rg["qb"][r]), "sID"],
          writes=["psTr"])
        A("ACT", lambda e: e.activation(rg["kT"][r], psTr[0:DS, 0:C], AF.Copy),
          reads=["psTr"], writes=[N(rg["kT"][r])])
        A("ACT", lambda e: e.activation(rg["qT"][r], psTr[0:DS, C:2 * C], AF.Copy),
          reads=["psTr"], writes=[N(rg["qT"][r])])
        def mm_tr2(e):
            e.transpose(psTr[0:DS, 2 * C:3 * C], rg["ktb"][r], sID)
            return e.transpose(psTr[0:DS, 3 * C:4 * C], rg["vb"][r], sID)
        A("PE", mm_tr2, reads=[N(rg["ktb"][r]), N(rg["vb"][r]), "sID"],
          writes=["psTr"])
        A("ACT", lambda e: e.activation(rg["ktT"][r], psTr[0:DS, 2 * C:3 * C], AF.Copy),
          reads=["psTr"], writes=[N(rg["ktT"][r])])
        A("ACT", lambda e: e.activation(rg["vT"][r], psTr[0:DS, 3 * C:4 * C], AF.Copy),
          reads=["psTr"], writes=[N(rg["vT"][r])])
        A("DVE", (lambda q_: lambda e: e.tensor_tensor(
            q_, rg["qT"][r], sDQF[0:DS, :], ALU.mult))(aQhTf[:, cs:cs + C]),
          reads=[N(rg["qT"][r]), "sDQF"], writes=[f"aQhTf@{i}"])
        A("DVE", (lambda q_: lambda e: e.tensor_tensor(
            q_, rg["qT"][r], sDQS[0:DS, :], ALU.mult))(aQhTs[:, cs:cs + C]),
          reads=[N(rg["qT"][r]), "sDQS"], writes=[f"aQhTs@{i}"])
        def mm_A(e):
            e.matmul(psSq[:, 0:128], rg["kT"][r], rg["kT"][r], start=True, stop=True)
            return e.matmul(psSq[:, 128:256], rg["kT"][r], rg["qT"][r],
                            start=True, stop=True)
        A("PE", mm_A, reads=[N(rg["kT"][r]), N(rg["qT"][r])], writes=["psSq"])
        A("PE", lambda e: e.matmul(psSq2[:, 0:128], rg["ktT"][r], rg["ktT"][r],
                                   start=True, stop=True),
          reads=[N(rg["ktT"][r])], writes=["psSq2"])
        A("DVE", lambda e: e.tensor_tensor(rg["Ntf"][r], psSq[:, 0:128],
                                           sNegET_f, ALU.mult),
          reads=["psSq", "sNegET_f"], writes=[N(rg["Ntf"][r])])
        A("DVE", lambda e: e.tensor_tensor(rg["Nnf"][r], psSq[:, 0:128],
                                           sNegE_f, ALU.mult),
          reads=["psSq", "sNegE_f"], writes=[N(rg["Nnf"][r])])
        A("DVE", (lambda g_: lambda e: e.tensor_tensor(
            g_, psSq[:, 128:256], sPosET_f, ALU.mult))(aGqTf[:, cs:cs + C]),
          reads=["psSq", "sPosET_f"], writes=[f"aGqTf@{i}"])
        A("DVE", (lambda g_: lambda e: e.scalar_tensor_tensor(
            g_, psSq[:, 128:256], rg["ucol"][r], sPosET_s,
            ALU.mult, ALU.mult))(aGqTs[:, cs:cs + C]),
          reads=["psSq", N(rg["ucol"][r]), "sPosET_s"], writes=[f"aGqTs@{i}"])
        A("DVE", lambda e: e.tensor_tensor(rg["Nts"][r], psSq2[:, 0:128],
                                           sNegET_s, ALU.mult),
          reads=["psSq2", "sNegET_s"], writes=[N(rg["Nts"][r])])
        A("DVE", lambda e: e.tensor_tensor(rg["Nns"][r], psSq2[:, 0:128],
                                           sNegE_s, ALU.mult),
          reads=["psSq2", "sNegE_s"], writes=[N(rg["Nns"][r])])
        A("PE", lambda e: e.matmul(psFeat[:, 0:C], sNM1T, rg["vT"][r],
                                   start=True, stop=True),
          reads=[N(rg["vT"][r]), "sNM1T"], writes=["psFeat"])
        A("ACT", lambda e: e.activation(rg["nu1b"][r], psFeat[:, 0:C], AF.Gelu,
                                        bias=sNMB1),
          reads=["psFeat", "sNMB1"], writes=[N(rg["nu1b"][r])])
        A("PE", lambda e: e.matmul(psFeat[0:DS, 128:128 + C], sNM2T,
                                   rg["nu1b"][r], start=True, stop=True),
          reads=[N(rg["nu1b"][r]), "sNM2T"], writes=["psFeat"])
        A("DVE", (lambda n_: lambda e: e.tensor_scalar(
            n_, psFeat[0:DS, 128:128 + C], sNMB2, None, ALU.add))(
            aNuT[:, cs:cs + C]),
          reads=["psFeat", "sNMB2"], writes=[f"aNuT@{i}"])
        wy = {}
        for st in ("f", "s"):
            wy[st] = dict(
                Sp=([rg["Sp1"][r], rg["Sp2"][r]] if st == "f"
                    else [rg["Sp3"][r], rg["Sp4"][r]]),
                Xp=([rg["Xa"][r], rg["Xb"][r]] if st == "f"
                    else [rg["Xc"][r], rg["Xd"][r]]),
                cur_t=rg["Ntf" if st == "f" else "Nts"][r],
                cur_n=rg["Nnf" if st == "f" else "Nns"][r],
                psX=(psX if st == "f" else psX2),
                pxk=("psX" if st == "f" else "psX2"),
                psSq=(psSq if st == "f" else psSq2),
                pqk=("psSq" if st == "f" else "psSq2"),
                src=rg["Rf" if st == "f" else "Rs"][r],
            )
            wy[st]["ct_key"] = N(wy[st]["cur_t"])
            wy[st]["cn_key"] = N(wy[st]["cur_n"])
            wy[st]["src_key"] = N(wy[st]["src"])
        for lev in range(7):
            for st in ("f", "s"):
                w = wy[st]
                A("PE", (lambda t_, s_, p_: lambda e: e.matmul(
                    p_, t_, s_, start=True, stop=True))(
                        w["cur_t"], w["src"], w["psX"]),
                  reads=[w["ct_key"], w["src_key"]], writes=[w["pxk"]])
                dst = w["Xp"][lev % 2]
                A("DVE", (lambda d_, s_, p_: lambda e: e.tensor_tensor(
                    d_, p_, s_, ALU.add))(dst, w["src"], w["psX"]),
                  reads=[w["pxk"], w["src_key"]], writes=[N(dst)])
                w["src"], w["src_key"] = dst, N(dst)
                if lev < 6:
                    so = w["Sp"][lev % 2]
                    def mm_sq(e, t_=w["cur_t"], n_=w["cur_n"], ps_=w["psSq"]):
                        e.matmul(ps_[:, 0:128], t_, n_, start=True, stop=True)
                        return e.matmul(ps_[:, 128:256], n_, t_,
                                        start=True, stop=True)
                    A("PE", mm_sq, reads=[w["ct_key"], w["cn_key"]],
                      writes=[w["pqk"]])
                    A("ACT", (lambda o_, ps_: lambda e: e.activation(
                        o_, ps_, AF.Copy))(so, w["psSq"]),
                      reads=[w["pqk"]], writes=[N(so)])
                    w["cur_t"] = so[:, 0:128]
                    w["cur_n"] = so[:, 128:256]
                    w["ct_key"] = w["cn_key"] = N(so)
        for st in ("f", "s"):
            w = wy[st]
            src, src_key = w["src"], w["src_key"]
            off = 0 if st == "f" else DS
            A("DVE", (lambda s_, o_: lambda e: e.tensor_copy(
                out=aD0[:, i * 128 + o_:i * 128 + o_ + DS], in_=s_[:, 0:DS]))(
                src, off),
              reads=[src_key], writes=[f"aD0@{i}"])
            mb = rg["mb"][r]
            A("DVE", (lambda s_: lambda e: e.tensor_copy(
                out=mb, in_=s_[:, DS:128]))(src),
              reads=[src_key], writes=[N(mb)])
            poff = 0 if st == "f" else C
            A("PE", (lambda po_: lambda e: e.transpose(
                psTr[0:DS, po_:po_ + C], rg["mb"][r], sID))(poff),
              reads=[N(mb), "sID"], writes=["psTr"])
            moff = i * 2 * C + (0 if st == "f" else C)
            A("DVE", (lambda mo_, po_: lambda e: e.tensor_scalar(
                aMhT[:, mo_:mo_ + C], psTr[0:DS, po_:po_ + C],
                -1.0, None, ALU.mult))(moff, poff),
              reads=["psTr"], writes=[f"aMhT@{i}"])

    def emit_B1(seg, i):
        if True:
            A("POOL", (lambda i_: lambda e: e.tensor_copy(
                out=aHH[:, i_ * 2 * DS:(i_ + 1) * 2 * DS], in_=sHH))(i),
              reads=["sHH"], writes=[f"aHH@{i}"])
            def mm_d(e, i=i):
                e.matmul(psChO[0:C, 0:DS], aMhT[:, i * 2 * C:i * 2 * C + C],
                         sHH[:, 0:DS], start=True, stop=True)
                return e.matmul(psChO[0:C, DS:2 * DS],
                                aMhT[:, i * 2 * C + C:i * 2 * C + 2 * C],
                                sHH[:, DS:2 * DS], start=True, stop=True)
            A("PE", mm_d, reads=[f"aMhT@{i}", "sHH"], writes=["psChO"])
            A("DVE", (lambda i_: lambda e: e.tensor_tensor(
                aDb[:, i_ * 128:(i_ + 1) * 128], psChO[0:C, 0:2 * DS],
                aD0[:, i_ * 128:(i_ + 1) * 128], ALU.add))(i),
              reads=["psChO", f"aD0@{i}"], writes=[f"aDb@{i}"])
            def mm_H(e, i=i):
                e.matmul(psChO[0:DS, 2 * DS:3 * DS], aKpf[:, i * DS:(i + 1) * DS],
                         aDb[:, i * 128:i * 128 + DS], start=True, stop=True)
                return e.matmul(psChO[0:DS, 3 * DS:4 * DS],
                                aKps[:, i * DS:(i + 1) * DS],
                                aDb[:, i * 128 + DS:(i + 1) * 128],
                                start=True, stop=True)
            A("PE", mm_H, reads=[f"aKpf@{i}", f"aKps@{i}", f"aDb@{i}"],
              writes=["psChO"])
            A("DVE", lambda e: e.scalar_tensor_tensor(
                sHH[:, 0:DS], sHH[:, 0:DS], sFAC[0:DS, :],
                psChO[0:DS, 2 * DS:3 * DS], ALU.mult, ALU.add),
              reads=["psChO", "sFAC"], writes=["sHH"])
            A("DVE", lambda e: e.scalar_tensor_tensor(
                sHH[:, DS:2 * DS], sHH[:, DS:2 * DS], sSAC[0:DS, :],
                psChO[0:DS, 3 * DS:4 * DS], ALU.mult, ALU.add),
              reads=["psChO", "sSAC"], writes=["sHH"])

    def emit_B3a(seg, i):
        if True:
            def mm_rd(e, i=i):
                e.matmul(psTok[:, 256:256 + DS], aQhTf[:, i * C:(i + 1) * C],
                         aHH[:, i * 2 * DS:i * 2 * DS + DS], start=True, stop=False)
                e.matmul(psTok[:, 256:256 + DS], aGqTf[:, i * C:(i + 1) * C],
                         aDb[:, i * 128:i * 128 + DS], start=False, stop=True)
                e.matmul(psTok[:, 320:320 + DS], aQhTs[:, i * C:(i + 1) * C],
                         aHH[:, i * 2 * DS + DS:(i + 1) * 2 * DS],
                         start=True, stop=False)
                return e.matmul(psTok[:, 320:320 + DS], aGqTs[:, i * C:(i + 1) * C],
                                aDb[:, i * 128 + DS:(i + 1) * 128],
                                start=False, stop=True)
            A("PE", mm_rd, reads=[f"aQhTf@{i}", f"aQhTs@{i}", f"aGqTf@{i}",
                                  f"aGqTs@{i}", f"aDb@{i}", f"aHH@{i}"],
              writes=["psTok"])
            A("ACT", (lambda i_: lambda e: e.activation(
                aRbf[:, i_ * DS:(i_ + 1) * DS], psTok[:, 256:256 + DS], AF.Copy))(i),
              reads=["psTok"], writes=[f"aRbf@{i}"])
            A("ACT", (lambda i_: lambda e: e.activation(
                aRbs[:, i_ * DS:(i_ + 1) * DS], psTok[:, 320:320 + DS], AF.Copy))(i),
              reads=["psTok"], writes=[f"aRbs@{i}"])
            r = i % R
            A("DVE", (lambda i_, r_: lambda e: e.scalar_tensor_tensor(
                rg["sc1"][r_], aRbf[:, i_ * DS:(i_ + 1) * DS], -0.5,
                aVtok[:, i_ * DS:(i_ + 1) * DS], ALU.mult, ALU.add))(i, r),
              reads=[f"aRbf@{i}", f"aVtok@{i}"], writes=[N(rg["sc1"][r])])
            A("DVE", (lambda i_, r_: lambda e: e.scalar_tensor_tensor(
                rg["sc2"][r_], aRbs[:, i_ * DS:(i_ + 1) * DS], -0.5,
                rg["sc1"][r_], ALU.mult, ALU.add))(i, r),
              reads=[f"aRbs@{i}", N(rg["sc1"][r])], writes=[N(rg["sc2"][r])])
            A("DVE", (lambda r_: lambda e: e.tensor_tensor(
                rg["sc1"][r_], rg["sc2"][r_], rg["sc2"][r_], ALU.mult))(r),
              reads=[N(rg["sc2"][r])], writes=[N(rg["sc1"][r])])
            A("DVE", (lambda r_: lambda e: e.reduce_sum(
                rg["errc"][r_], rg["sc1"][r_], AX.X))(r),
              reads=[N(rg["sc1"][r])], writes=[N(rg["errc"][r])])
            A("DVE", (lambda i_, r_: lambda e: e.tensor_scalar(
                aUnb[:, i_:i_ + 1], rg["errc"][r_], TH_ERR, None, ALU.is_gt))(i, r),
              reads=[N(rg["errc"][r])], writes=[f"aUnb@{i}"])

    def emit_ns(seg, i):
        if True:
            r = i % R
            A("PE", (lambda i_: lambda e: e.matmul(
                psChO[0:1, 0:C], aUnb[:, i_:i_ + 1], sID,
                start=True, stop=True))(i),
              reads=[f"aUnb@{i}", "sID"], writes=["psChO"])
            A("DVE", lambda e: e.tensor_copy(out=rg["urow"][r],
                                             in_=psChO[0:1, 0:C]),
              reads=["psChO"], writes=[N(rg["urow"][r])])
            A("PE", lambda e: e.matmul(psChO[0:DS, 4 * DS:4 * DS + C],
                                       sONE1[:, 0:DS], rg["urow"][r],
                                       start=True, stop=True),
              reads=[N(rg["urow"][r]), "sONE1"], writes=["psChO"])
            A("DVE", lambda e: e.tensor_scalar(rg["cbc"][r],
                                               psChO[0:DS, 4 * DS:4 * DS + C],
                                               -0.1, 1.0, ALU.mult, ALU.add),
              reads=["psChO"], writes=[N(rg["cbc"][r])])
            A("DVE", (lambda i_: lambda e: e.scalar_tensor_tensor(
                rg["wns"][r], aNuT[:, i_ * C:(i_ + 1) * C], 0.1,
                psChO[0:DS, 4 * DS:4 * DS + C], ALU.mult, ALU.mult))(i),
              reads=[f"aNuT@{i}", "psChO"], writes=[N(rg["wns"][r])])
            A("DVE", lambda e: e.tensor_tensor_scan(
                rg["nsT"][r], rg["cbc"][r], rg["wns"][r], sNS0, ALU.mult, ALU.add),
              reads=[N(rg["cbc"][r]), N(rg["wns"][r]), "sNS0"],
              writes=[N(rg["nsT"][r])])
            A("DVE", lambda e: e.tensor_copy(out=sNS0, in_=rg["nsT"][r][:, C - 1:C]),
              reads=[N(rg["nsT"][r])], writes=["sNS0"])
            A("PE", lambda e: e.transpose(psChO[0:C, 6 * DS:7 * DS], rg["nsT"][r],
                                          sIDF[0:DS, 0:DS]),
              reads=[N(rg["nsT"][r]), "sIDF"], writes=["psChO"])
            A("DVE", (lambda i_: lambda e: e.tensor_copy(
                out=aNsTok[:, i_ * DS:(i_ + 1) * DS],
                in_=psChO[0:C, 6 * DS:7 * DS]))(i),
              reads=["psChO"], writes=[f"aNsTok@{i}"])

    def emit_tailc(seg, i):
        r = i % R
        cs = i * C
        xtl = rg["xtl"][r]
        gc = seg * NCH + i
        A("SP", (lambda gc_, x_: lambda e: e.dma_start(
            out=x_, in_=xchunk(gc_)))(gc, xtl),
          writes=[N(xtl)], is_dma=True)
        A("DVE", (lambda i_: lambda e: e.tensor_scalar(
            rg["sc1"][r], aDb[:, i_ * 128:i_ * 128 + DS], aQkf[:, i_:i_ + 1],
            None, ALU.mult))(i),
          reads=[f"aDb@{i}", f"aQkf@{i}"], writes=[N(rg["sc1"][r])])
        A("DVE", lambda e: e.tensor_scalar(rg["sc1"][r], rg["sc1"][r], sBF,
                                           None, ALU.mult),
          reads=["sBF"], writes=[N(rg["sc1"][r])])
        A("DVE", (lambda i_: lambda e: e.scalar_tensor_tensor(
            rg["rf"][r], aRbf[:, i_ * DS:(i_ + 1) * DS], sFA, rg["sc1"][r],
            ALU.mult, ALU.add))(i),
          reads=[f"aRbf@{i}", N(rg["sc1"][r]), "sFA"], writes=[N(rg["rf"][r])])
        A("DVE", (lambda i_: lambda e: e.tensor_scalar(
            rg["sc2"][r], aDb[:, i_ * 128 + DS:(i_ + 1) * 128],
            aQks[:, i_:i_ + 1], None, ALU.mult))(i),
          reads=[f"aDb@{i}", f"aQks@{i}"], writes=[N(rg["sc2"][r])])
        A("DVE", lambda e: e.tensor_scalar(rg["sc2"][r], rg["sc2"][r], sBS,
                                           None, ALU.mult),
          reads=["sBS"], writes=[N(rg["sc2"][r])])
        A("DVE", (lambda i_: lambda e: e.scalar_tensor_tensor(
            rg["rs"][r], aRbs[:, i_ * DS:(i_ + 1) * DS], sSA, rg["sc2"][r],
            ALU.mult, ALU.add))(i),
          reads=[f"aRbs@{i}", N(rg["sc2"][r]), "sSA"], writes=[N(rg["rs"][r])])
        pwt = aPwt[:, i * 4:i * 4 + 4]
        A("DVE", (lambda p_: lambda e: e.tensor_scalar(
            rg["o1"][r], rg["rf"][r], p_[:, 0:1], None, ALU.mult))(pwt),
          reads=[N(rg["rf"][r]), f"aPwt@{i}"], writes=[N(rg["o1"][r])])
        A("DVE", (lambda p_: lambda e: e.scalar_tensor_tensor(
            rg["o2"][r], rg["rs"][r], p_[:, 1:2], rg["o1"][r],
            ALU.mult, ALU.add))(pwt),
          reads=[N(rg["rs"][r]), f"aPwt@{i}", N(rg["o1"][r])],
          writes=[N(rg["o2"][r])])
        A("DVE", (lambda p_, i_: lambda e: e.scalar_tensor_tensor(
            rg["o1"][r], aNsTok[:, i_ * DS:(i_ + 1) * DS], p_[:, 2:3],
            rg["o2"][r], ALU.mult, ALU.add))(pwt, i),
          reads=[f"aNsTok@{i}", f"aPwt@{i}", N(rg["o2"][r])],
          writes=[N(rg["o1"][r])])
        A("ACT", lambda e: e.activation(rg["sc1"][r], rg["o1"][r], AF.Square,
                                        accum_out=rg["msc"][r]),
          reads=[N(rg["o1"][r])], writes=[N(rg["sc1"][r]), N(rg["msc"][r])])
        A("DVE", lambda e: e.tensor_scalar(rg["msc"][r], rg["msc"][r],
                                           1.0 / DS, 1e-6, ALU.mult, ALU.add),
          reads=[], writes=[N(rg["msc"][r])])
        A("ACT", lambda e: e.activation(rg["msc"][r], rg["msc"][r], AF.Ln),
          reads=[], writes=[N(rg["msc"][r])])
        A("ACT", lambda e: e.activation(rg["msc"][r], rg["msc"][r], AF.Exp,
                                        scale=-0.5),
          reads=[], writes=[N(rg["msc"][r])])
        A("DVE", lambda e: e.scalar_tensor_tensor(
            rg["ohat"][r], rg["o1"][r], rg["msc"][r], sNW, ALU.mult, ALU.mult),
          reads=[N(rg["o1"][r]), N(rg["msc"][r]), "sNW"],
          writes=[N(rg["ohat"][r])])
        A("PE", lambda e: e.transpose(psTr[0:DS, 3 * C:4 * C],
                                      rg["ohat"][r], sID),
          reads=[N(rg["ohat"][r]), "sID"], writes=["psTr"])
        A("DVE", lambda e: e.tensor_copy(out=rg["ohT"][r],
                                         in_=psTr[0:DS, 3 * C:4 * C]),
          reads=["psTr"], writes=[N(rg["ohT"][r])])
        for hf in range(2):
            def mm_wo(e, hf=hf, r=r):
                last = None
                for g in range(4):
                    gg = hf * 4 + g
                    last = e.matmul(psChO[:, g * C:(g + 1) * C],
                                    sWoT[:, gg * C:(gg + 1) * C], rg["ohT"][r],
                                    start=True, stop=True)
                return last
            A("PE", mm_wo, reads=[N(rg["ohT"][r]), "sWoT"], writes=["psChO"])
            xvh = xtl.rearrange("p (g t) -> p g t", g=8)[:, hf * 4:(hf + 1) * 4, :]
            A("DVE", (lambda xv_: lambda e: e.tensor_tensor(
                xv_, xv_, psChO.rearrange("p (g t) -> p g t", g=4), ALU.add))(xvh),
              reads=["psChO"], writes=[N(xtl)])
        op = A("SP", (lambda seg_, i_, x_: lambda e: e.dma_start(
            out=x1w2(seg_, i_), in_=x_))(seg, i, xtl),
            reads=[N(xtl)], writes=[f"dx1@{i}"], is_dma=True)
        out_dmas.append(op)
        xhat = emit_ln(xtl, N(xtl), r, i, stats_seg=seg)
        def mm_rt(e):
            last = None
            for g in range(8):
                last = e.matmul(psTok[:, 200:208], xhat[:, g * C:(g + 1) * C],
                                sGWT[:, g * 8:(g + 1) * 8],
                                start=(g == 0), stop=(g == 7))
            return last
        A("PE", mm_rt, reads=[N(xhat), "sGWT"], writes=["psTok"])
        scr, a8, b8 = rg["scr"][r], rg["sc8a"][r], rg["sc8b"][r]
        c1, c2, c3 = rg["c1"][r], rg["c2"][r], rg["c3"][r]
        A("DVE", lambda e: e.tensor_tensor(scr, psTok[:, 200:208], sLB, ALU.add),
          reads=["psTok", "sLB"], writes=[N(scr)])
        A("ACT", lambda e: e.activation(scr, scr, AF.Sigmoid),
          reads=[], writes=[N(scr)])
        A("DVE", lambda e: e.reduce_max(c1, scr, AX.X),
          reads=[N(scr)], writes=[N(c1)])
        A("DVE", lambda e: e.tensor_scalar(a8, scr, c1, None, ALU.is_ge),
          reads=[N(scr), N(c1)], writes=[N(a8)])
        A("DVE", lambda e: e.scalar_tensor_tensor(a8, a8, -BIG, sIOTA8B,
                                                  ALU.mult, ALU.add),
          reads=["sIOTA8B"], writes=[N(a8)])
        A("DVE", lambda e: e.tensor_reduce(c2, a8, AX.X, ALU.min),
          reads=[N(a8)], writes=[N(c2)])
        A("DVE", lambda e: e.tensor_scalar(a8, sIOTA8, c2, None, ALU.is_equal),
          reads=[N(c2), "sIOTA8"], writes=[N(a8)])
        A("DVE", lambda e: e.scalar_tensor_tensor(b8, a8, -BIG, scr,
                                                  ALU.mult, ALU.add),
          reads=[N(a8), N(scr)], writes=[N(b8)])
        A("DVE", lambda e: e.reduce_max(c2, b8, AX.X),
          reads=[N(b8)], writes=[N(c2)])
        A("DVE", lambda e: e.tensor_scalar(b8, b8, c2, None, ALU.is_ge),
          reads=[N(c2)], writes=[N(b8)])
        A("DVE", lambda e: e.scalar_tensor_tensor(b8, b8, -BIG, sIOTA8B,
                                                  ALU.mult, ALU.add),
          reads=["sIOTA8B"], writes=[N(b8)])
        A("DVE", lambda e: e.tensor_reduce(c3, b8, AX.X, ALU.min),
          reads=[N(b8)], writes=[N(c3)])
        A("DVE", lambda e: e.tensor_scalar(b8, sIOTA8, c3, None, ALU.is_equal),
          reads=[N(c3), "sIOTA8"], writes=[N(b8)])
        A("DVE", lambda e: e.tensor_tensor(c3, c1, c2, ALU.add),
          reads=[N(c1), N(c2)], writes=[N(c3)])
        A("DVE", lambda e: e.tensor_scalar(c3, c3, 1e-8, None, ALU.add),
          reads=[], writes=[N(c3)])
        A("DVE", lambda e: e.reciprocal(c3, c3), reads=[], writes=[N(c3)])
        A("DVE", lambda e: e.tensor_scalar(c1, c1, c3, None, ALU.mult),
          reads=[N(c3)], writes=[N(c1)])
        A("DVE", lambda e: e.tensor_scalar(c2, c2, c3, None, ALU.mult),
          reads=[N(c3)], writes=[N(c2)])
        A("DVE", lambda e: e.tensor_tensor(rg["ohs"][r], a8, b8, ALU.add),
          reads=[N(a8), N(b8)], writes=[N(rg["ohs"][r])])
        A("DVE", lambda e: e.tensor_scalar(b8, b8, c2, None, ALU.mult),
          reads=[N(c2)], writes=[N(b8)])
        A("DVE", lambda e: e.scalar_tensor_tensor(rg["ohw"][r], a8, c1, b8,
                                                  ALU.mult, ALU.add),
          reads=[N(a8), N(c1), N(b8)], writes=[N(rg["ohw"][r])])
        A("DVE", (lambda i_: lambda e: e.tensor_copy(
            out=sOHSG[:, (i_ % 4) * 8:(i_ % 4 + 1) * 8], in_=rg["ohs"][r]))(i),
          reads=[N(rg["ohs"][r])], writes=[f"sOHSG@{i % 4}"])
        def mm_part(e, i=i, r=r):
            e.matmul(psTok[0:1, 208:216], sONE128, rg["ohw"][r],
                     start=True, stop=True)
            return e.matmul(psTok[0:1, 216:224], sONE128,
                            sOHSG[:, (i % 4) * 8:(i % 4 + 1) * 8],
                            start=True, stop=True)
        A("PE", mm_part, reads=[N(rg["ohw"][r]), f"sOHSG@{i % 4}", "sONE128"],
          writes=["psTok"])
        A("DVE", lambda e: e.tensor_tensor(sPart, sPart, psTok[0:1, 208:224],
                                           ALU.add),
          reads=["psTok"], writes=["sPart"])
        if i % 4 == 3:
            g4 = i // 4
            A("PE", lambda e: e.transpose(psTr[0:32, 2 * C:3 * C], sOHSG, sID),
              reads=["sOHSG@0", "sOHSG@1", "sOHSG@2", "sOHSG@3", "sID"],
              writes=["psTr"])
            A("DVE", lambda e: e.tensor_copy(out=sSelT,
                                             in_=psTr[0:32, 2 * C:3 * C]),
              reads=["psTr"], writes=["sSelT"])
            op = A("SP", (lambda seg_, g4_: lambda e: e.dma_start(
                out=selsw3(seg_, g4_), in_=sSelT))(seg, g4),
                reads=["sSelT"], writes=[f"dsels@{g4}"], is_dma=True)
            out_dmas.append(op)

    for seg in range(NSEG):
        if seg % 2 == 0:
            A("DVE", lambda e: e.tensor_copy(out=sHH, in_=sZeroHH),
              reads=["sZeroHH"], writes=["sHH"])
            A("DVE", lambda e: e.tensor_copy(out=sNS0, in_=sZeroHH[:, 0:1]),
              reads=["sZeroHH"], writes=["sNS0"])
        for i in range(NCH):
            emit_prep(seg, i)
        for i in range(NCH + 2):
            if i < NCH:
                emit_B1(seg, i)
            if 1 <= i < NCH + 1:
                emit_B3a(seg, i - 1)
            if i >= 2:
                emit_ns(seg, i - 2)
        for i in range(NCH):
            emit_tailc(seg, i)

    A("DVE", lambda e: e.tensor_copy(out=sTxPart, in_=sPart),
      reads=["sPart"], writes=["sTxPart"])
    def emit_txp(e):
        slot = e.value_load(sRelay[0:1, 0:1])
        return e.dma_start(out=shPart[bass.ts(slot, 1), :], in_=sTxPart)
    A("SP", emit_txp, reads=["sTxPart", "sRelay"], writes=["txp"], is_dma=True)
    def emit_txf(e):
        slot = e.value_load(sRelay[0:1, 0:1])
        return e.dma_start(out=shFlag[bass.ts(slot, 1), 0:1], in_=sNonceS)
    A("SP", emit_txf, reads=["txp", "sNonceS", "sRelay"], writes=["txf"],
      is_dma=True)
    def emit_spin(e):
        part = e.value_load(sRelay[0:1, 1:2])
        nonce = e.value_load(sRelay[0:1, 3:4])
        with e.register("r_cnt") as r_cnt, e.register("r_cond") as r_cond, \
             e.register("r_del") as r_del:
            e.reg_mov(r_cnt, 0)
            e.reg_mov(r_cond, 1)
            e.reg_mov(r_del, 0)
            with e.While(lambda: r_cond):
                for _ in range(400):
                    e.reg_add(r_del, r_del, 1)
                e.dma_start(out=sFlagRx,
                            in_=shFlag[bass.ts(part, 1), 0:1]
                            ).then_inc(spin_sem, 16)
                e.reg_add(r_cnt, r_cnt, 16)
                e.wait_ge(spin_sem, r_cnt)
                v = e.value_load(sFlagRx)
                e.reg_alu(r_cond, v, nonce, ALU.not_equal)
                with e.If_cmp(r_cnt, 16 * 4000000, "IS_GE"):
                    e.reg_mov(r_cond, 0)
        return e.nop()
    A("SP", emit_spin, reads=["sRelay", "txf"], writes=["spun", "sFlagRx"],
      tag="spin")
    def emit_rxp(e):
        part = e.value_load(sRelay[0:1, 1:2])
        return e.dma_start(out=sRxPart, in_=shPart[bass.ts(part, 1), :])
    A("SP", emit_rxp, reads=["spun", "sRelay"], writes=["sRxPart"], is_dma=True)
    A("DVE", lambda e: e.tensor_tensor(sWm, sPart, sRxPart, ALU.add),
      reads=["sPart", "sRxPart"], writes=["sWm"])
    A("DVE", lambda e: e.tensor_scalar(sWm[:, 8:16], sWm[:, 8:16], 1.0, None,
                                       ALU.max), reads=[], writes=["sWm"])
    A("DVE", lambda e: e.reciprocal(sWm[:, 8:16], sWm[:, 8:16]),
      reads=[], writes=["sWm"])
    A("DVE", lambda e: e.tensor_tensor(sWm[:, 0:8], sWm[:, 0:8], sWm[:, 8:16],
                                       ALU.mult), reads=[], writes=["sWm"])
    A("SP", lambda e: e.dma_start(out=wmScr, in_=sWm), reads=["sWm"],
      writes=["wmScr"], is_dma=True)
    A("SP", lambda e: e.dma_start(out=sWmbc, in_=_bpar(wmScr)),
      reads=["wmScr"], writes=["sWmbc"], is_dma=True)

    QC = D * TOK // 4
    for j in range(4):
        def cp_x1(e, j_=j):
            fq = fqreg[0]
            return e.dma_start(
                out=dataclasses.replace(
                    x1Own, offset=x1Own.offset + j_ * QC, ap=[[1, QC]]),
                in_=dataclasses.replace(
                    x1Scr, offset=x1Scr.offset + j_ * QC,
                    ap=[[D * TOK, NSEG], [1, QC]])[bass.ts(fq, 1), :])
        op = A("SP", cp_x1, reads=["fqreg"] + [f"dx1@{i}" for i in range(NCH)],
               writes=["x1Own"], is_dma=True)
        out_dmas.append(op)
    def cp_stats(e):
        fq = fqreg[0]
        return e.dma_start(out=statsOwn,
                           in_=statsScr[bass.ts(fq, 1), :])
    op = A("SP", cp_stats,
           reads=["fqreg"] + [f"dstats@{i}" for i in range(NCH)],
           writes=["statsOwn"], is_dma=True)
    out_dmas.append(op)
    def cp_sels(e):
        fq = fqreg[0]
        return e.dma_start(out=selsOwn,
                           in_=selsScr[bass.ts(fq, 1), :])
    op = A("SP", cp_sels,
           reads=["fqreg"] + [f"dsels@{i}" for i in range(NG)],
           writes=["selsOwn"], is_dma=True)
    out_dmas.append(op)

    s.set_barrier()
    ctxA.pop_all().close()

    x_q = sbuf("x_q", [128, 8 * QTOK])
    xhatF = sbuf("xhatF", [128, 8 * QTOK], BF16)
    slWG = [sbuf(f"slWG{j}", [128, 8 * GRP], BF16) for j in range(2)]
    slWU = [sbuf(f"slWU{j}", [128, 8 * GRP], BF16) for j in range(2)]
    slWD = [sbuf(f"slWD{j}", [128, 4 * D], BF16) for j in range(2)]
    sSilu = sbuf("sSilu", [128, 4 * GRP], BF16)
    sT1 = sbuf("sT1", [128, 4 * GRP], BF16)
    sSel = sbuf("sSel", [128, GRP], BF16)
    sDn = sbuf("sDn", [128, 4 * GRP], BF16)
    sMbc = sbuf("sMbc", [128, GRP])
    sRbc = sbuf("sRbc", [128, GRP])
    sLnT = sbuf("sLnT", [128, 8 * GRP])
    psG = ctx.enter_context(nc.psum_tensor("psG", [128, 2048], F32)).ap()
    psU = ctx.enter_context(nc.psum_tensor("psU", [128, 2048], F32)).ap()

    for p_i in range(NPASS):
        q0 = p_i * QTOK
        A("SP", (lambda pi_: lambda e: e.dma_start(
            out=x_q, in_=x1rd3(pi_)))(p_i),
          reads=["x1Own"],
          writes=["x_q"], is_dma=True)
        for gq in range(GPP):
            g = p_i * GPP + gq
            cs = g * GRP
            xv = x_q.rearrange("p (c g t) -> p c g t", c=8, g=8, t=128)[
                :, gq * 4:(gq + 1) * 4, :, :]
            A("SP", (lambda cs_: lambda e: e.dma_start(
                out=sMbc, in_=_bpar(statsOwn[0:1, cs_:cs_ + GRP])))(cs),
              reads=["statsOwn"],
              writes=["sMbc"], is_dma=True)
            A("SP", (lambda cs_: lambda e: e.dma_start(
                out=sRbc, in_=_bpar(statsOwn[0:1, TOK + cs_:TOK + cs_ + GRP])))(cs),
              reads=["statsOwn"],
              writes=["sRbc"], is_dma=True)
            xho = xhatF.rearrange("p (g c t) -> p c g t", g=8, c=8, t=128)[
                :, gq * 4:(gq + 1) * 4, :, :]
            ln4 = sLnT.rearrange("p (g c t) -> p c g t", g=8, c=4, t=128)
            def bc4(bc):
                return dataclasses.replace(
                    bc, ap=[list(bc.ap[0]), [128, 4], [0, 8], [1, 128]])
            A("DVE", (lambda xv_: lambda e: e.tensor_tensor(
                ln4, xv_, bc4(sMbc), ALU.subtract))(xv),
              reads=["x_q", "sMbc"], writes=["sLnT"])
            A("DVE", (lambda xho_: lambda e: e.tensor_tensor(
                xho_, ln4, bc4(sRbc), ALU.mult))(xho),
              reads=["sLnT", "sRbc"], writes=[f"xhatF@{gq}"])
        for e_i in range(10):
            sl = e_i % 2
            A("SP", (lambda ei_, sl_: lambda e: e.dma_start(
                out=slWG[sl_], in_=scrWG[ei_]))(e_i, sl),
              reads=[f"castg{e_i}"], writes=[f"slWG{sl}"], is_dma=True)
            A("SP", (lambda ei_, sl_: lambda e: e.dma_start(
                out=slWU[sl_], in_=scrWU[ei_]))(e_i, sl),
              reads=[f"castu{e_i}"], writes=[f"slWU{sl}"], is_dma=True)
            A("SP", (lambda ei_, sl_: lambda e: e.dma_start(
                out=slWD[sl_], in_=scrWD[ei_]))(e_i, sl),
              reads=[f"castd{e_i}"], writes=[f"slWD{sl}"], is_dma=True)
            for gq in range(GPP):
                g = p_i * GPP + gq
                cs = g * GRP
                if e_i >= 2:
                    A("SP", (lambda ei_, cs_: lambda e: e.dma_start(
                        out=sSel, in_=_bpar(selsOwn[0:1, (ei_ - 2) * TOK + cs_:
                                                    (ei_ - 2) * TOK + cs_ + GRP])))(e_i, cs),
                      reads=["selsOwn"],
                      writes=["sSel"], is_dma=True)
                def mm_gate(e, sl=sl, gq=gq):
                    last = None
                    for it in range(4):
                        for g8 in range(8):
                            last = e.matmul(
                                psG[:, it * GRP:(it + 1) * GRP],
                                slWG[sl][:, g8 * GRP + it * 128:
                                         g8 * GRP + (it + 1) * 128],
                                xhatF[:, g8 * QTOK + gq * GRP:
                                      g8 * QTOK + (gq + 1) * GRP],
                                start=(g8 == 0), stop=(g8 == 7))
                    return last
                A("PE", mm_gate, reads=[f"slWG{sl}", f"xhatF@{gq}"], writes=["psG"])
                A("ACT", lambda e: e.activation(sSilu, psG, AF.Silu),
                  reads=["psG"], writes=["sSilu"])
                def mm_up(e, sl=sl, gq=gq):
                    last = None
                    for it in range(4):
                        for g8 in range(8):
                            last = e.matmul(
                                psU[:, it * GRP:(it + 1) * GRP],
                                slWU[sl][:, g8 * GRP + it * 128:
                                         g8 * GRP + (it + 1) * 128],
                                xhatF[:, g8 * QTOK + gq * GRP:
                                      g8 * QTOK + (gq + 1) * GRP],
                                start=(g8 == 0), stop=(g8 == 7))
                    return last
                A("PE", mm_up, reads=[f"slWU{sl}", f"xhatF@{gq}"], writes=["psU"])
                A("DVE", lambda e: e.tensor_tensor(sT1, psU, sSilu, ALU.mult),
                  reads=["psU", "sSilu"], writes=["sT1"])
                if e_i >= 2:
                    A("DVE", (lambda ei_: lambda e: e.scalar_tensor_tensor(
                        sT1.rearrange("p (i t) -> p i t", i=4),
                        sT1.rearrange("p (i t) -> p i t", i=4),
                        sWmbc[:, ei_ - 2:ei_ - 1], _rep(sSel, 4),
                        ALU.mult, ALU.mult))(e_i),
                      reads=["sWmbc", "sSel"], writes=["sT1"])
                for hf in range(2):
                    ps = psG if hf == 0 else psU
                    pk = "psG" if hf == 0 else "psU"
                    def mm_dn(e, sl=sl, hf=hf, ps=ps):
                        last = None
                        for ht in range(4):
                            gg = hf * 4 + ht
                            for ic in range(4):
                                last = e.matmul(
                                    ps[:, ht * GRP:(ht + 1) * GRP],
                                    slWD[sl][:, ic * D + gg * 128:
                                             ic * D + (gg + 1) * 128],
                                    sT1[:, ic * GRP:(ic + 1) * GRP],
                                    start=(ic == 0), stop=(ic == 3))
                        return last
                    A("PE", mm_dn, reads=[f"slWD{sl}", "sT1"], writes=[pk])
                    xo = x_q.rearrange("p (c g t) -> p g c t",
                                       c=8, g=8, t=128)[
                        :, hf * 4:(hf + 1) * 4, gq * 4:(gq + 1) * 4, :]
                    if hf == 0:
                        A("DVE", (lambda xo_, ps_: lambda e: e.tensor_tensor(
                            xo_, xo_,
                            ps_.rearrange("p (g c t) -> p g c t",
                                          g=4, c=4, t=128),
                            ALU.add))(xo, ps),
                          reads=[pk], writes=["x_q"])
                    else:
                        A("ACT", (lambda ps_: lambda e: e.activation(
                            sDn, ps_, AF.Copy))(ps),
                          reads=[pk], writes=["sDn"])
                        A("POOL", (lambda xo_: lambda e: e.tensor_tensor(
                            xo_, xo_,
                            sDn.rearrange("p (g c t) -> p g c t",
                                          g=4, c=4, t=128),
                            ALU.add))(xo),
                          reads=["sDn"], writes=["x_q"])
        op = A("SP", (lambda pi_: lambda e: e.dma_start(
            out=outw3(pi_), in_=x_q))(p_i),
            reads=["x_q"], is_dma=True)
        out_dmas.append(op)
    A("SP", lambda e: e.nop(), dep_on=[o for o in out_dmas if o is not None])
    if os.environ.get("MK_PRCNT"):
        print("opcount:", len(s.ops))
    s.run()
    ctx.pop_all()
    return nc


_CACHE = {}


def _consts():
    t = np.arange(128, dtype=np.float32)
    TT, SS = np.meshgrid(t, t, indexing="ij")
    exTS = (TT - 1 - SS).astype(np.float32)
    return dict(
        identb=np.eye(128, dtype=nbf),
        exTS=exTS,
        exST=np.ascontiguousarray(exTS.T),
        mskSL=(SS < TT).astype(np.float32),
        mskSU=np.ascontiguousarray((SS < TT).astype(np.float32).T),
        iotaRow=np.ascontiguousarray(t.reshape(1, 128)),
        iotaCol=np.ascontiguousarray(t.reshape(128, 1)),
        iotaColRev=np.ascontiguousarray((127.0 - t).reshape(128, 1)),
    )


def kernel(L0_attn_x, L0_attn_Wk, L0_attn_Wv, L0_attn_Wq, L0_attn_fast_alpha,
           L0_attn_slow_alpha, L0_attn_nm_w1, L0_attn_nm_b1, L0_attn_nm_w2,
           L0_attn_nm_b2, L0_attn_ir_w1, L0_attn_ir_b1, L0_attn_ir_w2,
           L0_attn_ir_b2, L0_attn_pw, L0_attn_Wo, L0_attn_norm_w,
           L0_ffn_gate_w, L0_ffn_lb_bias, L0_ffn_expert_gate_w,
           L0_ffn_expert_up_w, L0_ffn_expert_down_w, L0_ffn_shared_gate_w,
           L0_ffn_shared_up_w, L0_ffn_shared_down_w):
    f = np.float32
    asn = np.ascontiguousarray
    x = np.asarray(L0_attn_x, f)
    consts = _consts()

    if "all" not in _CACHE:
        _CACHE["all"] = build_all()
    nc = _CACHE["all"]

    sgT = np.asarray(L0_ffn_shared_gate_w, f).T
    suT = np.asarray(L0_ffn_shared_up_w, f).T
    sdT = np.asarray(L0_ffn_shared_down_w, f).T
    eg = np.asarray(L0_ffn_expert_gate_w, f)
    eu = np.asarray(L0_ffn_expert_up_w, f)
    ed = np.asarray(L0_ffn_expert_down_w, f)
    wg = np.empty((10, D, GRP), f); wu = np.empty((10, D, GRP), f)
    wd = np.empty((10, GRP, D), f)
    wg[0] = sgT[:, 0:GRP]; wg[1] = sgT[:, GRP:2 * GRP]
    wu[0] = suT[:, 0:GRP]; wu[1] = suT[:, GRP:2 * GRP]
    wd[0] = sdT[0:GRP, :]; wd[1] = sdT[GRP:2 * GRP, :]
    for e_i in range(8):
        wg[2 + e_i] = eg[e_i].T
        wu[2 + e_i] = eu[e_i].T
        wd[2 + e_i] = ed[e_i].T

    common = dict(
        WkT=asn(np.asarray(L0_attn_Wk, f).T), WvT=asn(np.asarray(L0_attn_Wv, f).T),
        WqT=asn(np.asarray(L0_attn_Wq, f).T), ir1T=asn(np.asarray(L0_attn_ir_w1, f).T),
        ir2T=asn(np.asarray(L0_attn_ir_w2, f).T), pwT=asn(np.asarray(L0_attn_pw, f).T),
        nm1T=asn(np.asarray(L0_attn_nm_w1, f).T), nm2T=asn(np.asarray(L0_attn_nm_w2, f).T),
        nmb1=np.asarray(L0_attn_nm_b1, f).reshape(-1, 1),
        nmb2=np.asarray(L0_attn_nm_b2, f).reshape(-1, 1),
        irb1=np.asarray(L0_attn_ir_b1, f).reshape(-1, 1),
        irb2=np.asarray(L0_attn_ir_b2, f).reshape(1, 1),
        WoT=asn(np.asarray(L0_attn_Wo, f).T),
        normw=np.asarray(L0_attn_norm_w, f).reshape(1, -1),
        gwT=asn(np.asarray(L0_ffn_gate_w, f).T),
        lb=np.asarray(L0_ffn_lb_bias, f).reshape(1, -1),
        fa=np.asarray(L0_attn_fast_alpha, f).reshape(1, 1),
        sa=np.asarray(L0_attn_slow_alpha, f).reshape(1, 1),
        wgT=wg, wuT=wu, wdT=wd,
        **consts,
    )
    nonce = _fresh_nonce()
    in_maps = []
    for c in range(8):
        parity = c % 2
        p = c // 2
        b0 = 2 * parity
        m = dict(common)
        xpair = np.concatenate([x[b0], x[b0 + 1]], axis=0)
        m["xC"] = asn(xpair.reshape(128, 128, 8, 128).transpose(0, 3, 2, 1)
                      .reshape(16384, D))
        m["fq"] = np.array([[p]], np.int32)
        m["relay"] = np.array([[parity, 1 - parity, 1, nonce + c // 2]],
                              np.int32)
        in_maps.append(m)
    res = run_bass_kernel_spmd(nc, in_maps, core_ids=list(range(8)))
    r = res.results

    out = np.empty((4, 8192, D), f)
    for c in range(8):
        parity = c % 2
        p = c // 2
        b = 2 * parity + p // 2
        h = p % 2
        a = np.asarray(r[c]["outT"], f)
        out[b, h * TOK:(h + 1) * TOK, :] = (
            a.reshape(32, 128, 8, 128).transpose(0, 3, 2, 1).reshape(TOK, D))
    return out


# version 14
# speedup vs baseline: 1.0343x; 1.0343x over previous
import contextlib
import dataclasses
import os
import time
import numpy as np
import ml_dtypes

import concourse.bass as bass
import concourse.mybir as mybir
from concourse.bass_utils import run_bass_kernel_spmd

F32 = mybir.dt.float32
BF16 = mybir.dt.bfloat16
I32 = mybir.dt.int32
U8 = mybir.dt.uint8
AF = mybir.ActivationFunctionType
ALU = mybir.AluOpType
AX = mybir.AxisListType
nbf = ml_dtypes.bfloat16

D = 1024
DS = 64
TOK = 4096
NSEG = 4
C = 128
NCH = TOK // C
GRP = 512
NG = TOK // GRP
NPASS = 4
GPP = NG // NPASS
QTOK = TOK // NPASS
TH_ERR = float((1.0 + 1e-6) * np.log(np.float64(np.float32(0.7)) /
                                     np.float64(np.float32(0.3))))
BIG = 100.0


def _fresh_nonce():
    return ((int(time.time() * 1e6) ^ (os.getpid() << 16)) & 0x7FFFFF00) | 0x1100

ENGINES = ("PE", "DVE", "ACT", "POOL", "SP")


class _Op:
    __slots__ = ("idx", "engine", "emit", "reads", "writes", "is_dma",
                 "deps", "marked", "count", "dma_val", "tag",
                 "self_drain", "drain_val")

    def __init__(self, idx, engine, emit, reads, writes, is_dma, tag=""):
        self.idx = idx; self.engine = engine; self.emit = emit
        self.reads = tuple(reads); self.writes = tuple(writes)
        self.is_dma = is_dma; self.deps = []
        self.marked = False; self.count = None; self.dma_val = None
        self.tag = tag; self.self_drain = 0; self.drain_val = 0


class Sched:

    def __init__(self, nc):
        self.nc = nc
        self.ops = []
        self.last_writer = {}
        self.readers = {}
        self.bar = ()

    def add(self, engine, emit, reads=(), writes=(), is_dma=False, tag="",
            dep_on=()):
        reads = tuple(reads)
        writes = tuple(writes)
        psr = tuple(b for b in reads if b.startswith("ps") and b not in writes)
        if psr:
            writes = writes + psr
            reads = tuple(b for b in reads if not b.startswith("ps"))
        op = _Op(len(self.ops), engine, emit, reads, writes, is_dma, tag)
        deps = {}
        for d in tuple(dep_on) + tuple(self.bar):
            if d is not None:
                deps[d.idx] = d
        for b in op.reads:
            w = self.last_writer.get(b)
            if w is not None:
                deps[w.idx] = w
        for b in op.writes:
            w = self.last_writer.get(b)
            if w is not None:
                deps[w.idx] = w
            for rr in self.readers.get(b, ()):
                deps[rr.idx] = rr
        deps.pop(op.idx, None)
        op.deps = list(deps.values())
        for b in op.reads:
            self.readers.setdefault(b, []).append(op)
        for b in op.writes:
            self.last_writer[b] = op
            self.readers[b] = []
        self.ops.append(op)
        return op

    def set_barrier(self):
        last_op = {}
        last_dma = {}
        for op in self.ops:
            last_op[op.engine] = op
            if op.is_dma:
                last_dma[op.engine] = op
        bars = []
        for e in ENGINES:
            deps = []
            if e in last_op:
                deps.append(last_op[e])
            if e in last_dma and last_dma[e] is not last_op.get(e):
                deps.append(last_dma[e])
            b = self.add(e, (lambda eng: eng.nop()), tag="phase_bar",
                         dep_on=tuple(deps))
            bars.append(b)
        self.bar = tuple(bars)

    def prepare(self):
        issued = {e: 0 for e in ENGINES}
        drains = {e: None for e in ENGINES}
        new_ops = []
        nidx = 10 ** 7
        for op in self.ops:
            ndeps = []
            self_drain = 0
            for d in op.deps:
                if not d.is_dma:
                    ndeps.append(d)
                    continue
                q = d.engine
                if q == op.engine:
                    self_drain = max(self_drain, 16 * issued[q])
                else:
                    dr = drains[q]
                    if dr is None or dr[0] < d.dma_val:
                        dop = _Op(nidx, q, (lambda e: e.nop()), (), (),
                                  False, tag="drain")
                        nidx += 1
                        dop.deps = []
                        dop.drain_val = 16 * issued[q]
                        dop.self_drain = 0
                        new_ops.append(dop)
                        dr = (16 * issued[q], dop)
                        drains[q] = dr
                    ndeps.append(dr[1])
            op.deps = ndeps
            op.self_drain = self_drain
            new_ops.append(op)
            if op.is_dma:
                issued[op.engine] += 1
                op.dma_val = 16 * issued[op.engine]
        self.ops = new_ops
        for op in self.ops:
            for d in op.deps:
                d.marked = True
        counts = {e: 0 for e in ENGINES}
        for op in self.ops:
            if not op.is_dma and op.marked:
                counts[op.engine] += 1
                op.count = counts[op.engine]

    def emit_engine(self, ename, eng, sems, dsems):
        waited = {e: 0 for e in ENGINES}
        dwaited = 0
        for op in self.ops:
            if op.engine != ename:
                continue
            sd = max(getattr(op, "self_drain", 0),
                     getattr(op, "drain_val", 0))
            if sd > dwaited:
                eng.wait_ge(dsems[ename], sd)
                dwaited = sd
            for d in op.deps:
                if d.count is not None and d.count > waited[d.engine]:
                    eng.wait_ge(sems[d.engine], d.count)
                    waited[d.engine] = d.count
            inst = op.emit(eng)
            if op.is_dma:
                inst.then_inc(dsems[ename], 16)
            elif op.marked:
                inst.then_inc(sems[ename], 1)

    def run(self):
        nc = self.nc
        self.prepare()
        stack = contextlib.ExitStack()
        sems = {e: stack.enter_context(nc.semaphore("S" + e)) for e in ENGINES}
        dsems = {e: stack.enter_context(nc.semaphore("D" + e)) for e in ENGINES}
        self._sem_stack = stack
        with nc.Block() as block:
            block.tensor(lambda eng: self.emit_engine("PE", eng, sems, dsems))
            block.vector(lambda eng: self.emit_engine("DVE", eng, sems, dsems))
            block.scalar(lambda eng: self.emit_engine("ACT", eng, sems, dsems))
            block.gpsimd(lambda eng: self.emit_engine("POOL", eng, sems, dsems))
            block.sync(lambda eng: self.emit_engine("SP", eng, sems, dsems))


def _bpar(dram_ap, n=128):
    return dataclasses.replace(dram_ap, ap=[[0, n]] + list(dram_ap.ap[1:]))


def _rep(ap, n):
    return dataclasses.replace(ap, ap=list(ap.ap[:1]) + [[0, n]] + list(ap.ap[1:]))


def _repf(ap, n):
    return dataclasses.replace(ap, ap=list(ap.ap[:1]) + [[0, n]])


def build_all():
    nc = bass.Bass("TRN2", target_bir_lowering=False, detect_race_conditions=False)
    ctx = contextlib.ExitStack()
    ctxA = contextlib.ExitStack()

    def din(name, shape, dtype=F32):
        return nc.dram_tensor(name, shape, dtype, kind="ExternalInput").ap()

    def sbuf(name, shape, dtype=F32):
        return ctx.enter_context(nc.sbuf_tensor(name, shape, dtype)).ap()

    def sbufA(name, shape, dtype=F32):
        return ctxA.enter_context(nc.sbuf_tensor(name, shape, dtype)).ap()

    def psumA(name, shape, dtype=F32):
        return ctxA.enter_context(nc.psum_tensor(name, shape, dtype)).ap()

    dXC = din("xC", [NSEG * TOK // C * 128, D])
    dWkT = din("WkT", [D, DS]); dWvT = din("WvT", [D, DS])
    dWqT = din("WqT", [D, DS]); dIR1T = din("ir1T", [D, DS])
    dIR2T = din("ir2T", [DS, 1]); dPWT = din("pwT", [D, 3])
    dNM1T = din("nm1T", [DS, 2 * DS]); dNM2T = din("nm2T", [2 * DS, DS])
    dNMB1 = din("nmb1", [2 * DS, 1]); dNMB2 = din("nmb2", [DS, 1])
    dIRB1 = din("irb1", [DS, 1]); dIRB2 = din("irb2", [1, 1])
    dWoT = din("WoT", [DS, D]); dNW = din("normw", [1, DS])
    dGWT = din("gwT", [D, 8]); dLB = din("lb", [1, 8])
    dFA = din("fa", [1, 1]); dSA = din("sa", [1, 1])
    dID = din("identb", [128, 128], BF16)
    dEX = din("exTS", [128, 128]); dEXT = din("exST", [128, 128])
    dMSL = din("mskSL", [128, 128]); dMSU = din("mskSU", [128, 128])
    dIOR = din("iotaRow", [1, 128]); dIOC = din("iotaCol", [128, 1])
    dIOCR = din("iotaColRev", [128, 1])
    dRELAY = din("relay", [1, 4], I32)
    dFQ = din("fq", [1, 1], I32)
    dWG = din("wgT", [10, D, GRP])
    dWU = din("wuT", [10, D, GRP])
    dWD = din("wdT", [10, GRP, D])

    oOut = nc.dram_tensor("outT", [TOK // C * 128, D], F32,
                          kind="ExternalOutput").ap()

    x1Scr = nc.dram_tensor("x1Scr", [NSEG * TOK // C * 128, D], F32,
                           kind="Internal").ap()
    statsScr = nc.dram_tensor("statsScr", [NSEG, 2 * TOK], F32,
                              kind="Internal").ap()
    selsScr = nc.dram_tensor("selsScr", [NSEG, 8 * TOK], BF16,
                             kind="Internal").ap()
    scrWG = nc.dram_tensor("scrWG", [10, 128, 8 * GRP], BF16, kind="Internal").ap()
    scrWU = nc.dram_tensor("scrWU", [10, 128, 8 * GRP], BF16, kind="Internal").ap()
    scrWD = nc.dram_tensor("scrWD", [10, 128, 4 * D], BF16, kind="Internal").ap()
    wmScr = nc.dram_tensor("wmScr", [1, 16], F32, kind="Internal").ap()
    x1Own = nc.dram_tensor("x1Own", [TOK // C * 128, D], F32,
                           kind="Internal").ap()
    statsOwn = nc.dram_tensor("statsOwn", [1, 2 * TOK], F32,
                              kind="Internal").ap()
    selsOwn = nc.dram_tensor("selsOwn", [1, 8 * TOK], BF16,
                             kind="Internal").ap()
    shPart = nc.dram_tensor("shPart", [2, 16], F32, kind="Internal",
                            addr_space="Shared").ap()
    shFlag = nc.dram_tensor("shFlag", [2, 2], I32, kind="Internal",
                            addr_space="Shared").ap()

    def xchunk(gc):
        return dXC[gc * 128:(gc + 1) * 128, :]

    def x1w2(seg, i):
        gc = seg * NCH + i
        return x1Scr[gc * 128:(gc + 1) * 128, :]

    def x1rd3(p_i):
        return dataclasses.replace(
            x1Own, offset=x1Own.offset + p_i * 8 * 128 * D,
            ap=[[D, 128], [128 * D, 8], [1, D]])

    def outw3(p_i):
        return dataclasses.replace(
            oOut, offset=oOut.offset + p_i * 8 * 128 * D,
            ap=[[D, 128], [128 * D, 8], [1, D]])

    def selsw3(seg, g4):
        return dataclasses.replace(
            selsScr, offset=selsScr.offset + seg * (8 * TOK) + g4 * 512,
            ap=[[128, 4], [TOK, 8], [1, 128]])

    sRelay = sbuf("sRelay", [1, 4], I32)
    sNonceS = sbuf("sNonceS", [1, 1], I32)
    sFlagRx = sbuf("sFlagRx", [1, 1], I32)
    sPart = sbuf("sPart", [1, 16])
    sTxPart = sbuf("sTxPart", [1, 16])
    sRxPart = sbuf("sRxPart", [1, 16])
    sWm = sbuf("sWm", [1, 16])
    sWmbc = sbuf("sWmbc", [128, 16])
    sFq = sbuf("sFq", [1, 1], I32)

    sWall = sbufA("sWall", [128, 8 * 195], BF16)
    sIR1T = sbufA("sIR1T", [128, 8 * DS], BF16)
    sGWT = sbufA("sGWT", [128, 8 * 8], BF16)
    sNM1T = sbufA("sNM1T", [DS, 2 * DS], BF16)
    sNM2T = sbufA("sNM2T", [2 * DS, DS], BF16)
    sIR2T = sbufA("sIR2T", [DS, 1], BF16)
    sWoT = sbufA("sWoT", [DS, D], BF16)
    sNW = sbufA("sNW", [128, DS])
    sLB = sbufA("sLB", [128, 8])
    sID = sbufA("sID", [128, 128], BF16)
    sIDF = sbufA("sIDF", [128, 128])
    sEX = sbufA("sEX", [128, 128]); sEXT = sbufA("sEXT", [128, 128])
    sMSL = sbufA("sMSL", [128, 128]); sMSU = sbufA("sMSU", [128, 128])
    sIOR = sbufA("sIOR", [1, 128]); sIOC = sbufA("sIOC", [128, 1])
    sIOCR = sbufA("sIOCR", [128, 1]); sIORB = sbufA("sIORB", [128, 128])
    sONE1 = sbufA("sONE1", [1, 128], BF16)
    sONE128 = sbufA("sONE128", [128, 1], BF16)
    sFA = sbufA("sFA", [128, 1]); sSA = sbufA("sSA", [128, 1])
    sLNFA = sbufA("sLNFA", [128, 1]); sLNSA = sbufA("sLNSA", [128, 1])
    sBF = sbufA("sBF", [128, 1]); sBS = sbufA("sBS", [128, 1])
    sNBF = sbufA("sNBF", [128, 1]); sNBS = sbufA("sNBS", [128, 1])
    sFAC = sbufA("sFAC", [128, 1]); sSAC = sbufA("sSAC", [128, 1])
    sDQFC = sbufA("sDQFC", [128, 1]); sDQSC = sbufA("sDQSC", [128, 1])
    sDKFC = sbufA("sDKFC", [128, 1]); sDKSC = sbufA("sDKSC", [128, 1])
    sNMB1 = sbufA("sNMB1", [2 * DS, 1]); sNMB2 = sbufA("sNMB2", [DS, 1])
    sIRB1 = sbufA("sIRB1", [DS, 1]); sIRB2 = sbufA("sIRB2", [128, 1])
    sDQF = sbufA("sDQF", [128, 128]); sDQS = sbufA("sDQS", [128, 128])
    sEt = sbufA("sEt", [128, 128])
    sNegE_f = sbufA("sNegE_f", [128, 128], BF16)
    sNegET_f = sbufA("sNegET_f", [128, 128], BF16)
    sPosET_f = sbufA("sPosET_f", [128, 128])
    sNegE_s = sbufA("sNegE_s", [128, 128], BF16)
    sNegET_s = sbufA("sNegET_s", [128, 128], BF16)
    sPosET_s = sbufA("sPosET_s", [128, 128])
    sHH = sbufA("sHH", [DS, 2 * DS], BF16)
    sNS0 = sbufA("sNS0", [DS, 1])
    sZeroHH = sbufA("sZeroHH", [DS, 2 * DS])
    sOHSG = sbufA("sOHSG", [128, 32], BF16)
    sSelT = sbufA("sSelT", [32, 128], BF16)
    sIOTA8 = sbufA("sIOTA8", [128, 8])
    sIOTA8B = sbufA("sIOTA8B", [128, 8])
    sNWrowF = sbufA("sNWrowF", [1, DS])
    sEPS5 = sbufA("sEPS5", [1, 1])
    sNWrowB = sbufA("sNWrowB", [1, DS], BF16)

    aMhT = sbufA("aMhT", [DS, NCH * 2 * C], BF16)
    aD0 = sbufA("aD0", [128, NCH * 128])
    aGqTf = sbufA("aGqTf", [128, NCH * C], BF16)
    aGqTs = sbufA("aGqTs", [128, NCH * C], BF16)
    aQhTf = sbufA("aQhTf", [DS, NCH * C], BF16)
    aQhTs = sbufA("aQhTs", [DS, NCH * C], BF16)
    aKpf = sbufA("aKpf", [128, NCH * DS], BF16)
    aKps = sbufA("aKps", [128, NCH * DS], BF16)
    aVtok = sbufA("aVtok", [128, NCH * DS])
    aNuT = sbufA("aNuT", [DS, NCH * C])
    aPwt = sbufA("aPwt", [128, NCH * 4])
    aQkf = sbufA("aQkf", [128, NCH])
    aQks = sbufA("aQks", [128, NCH])
    aHH = sbufA("aHH", [DS, NCH * 2 * DS], BF16)
    aDb = sbufA("aDb", [128, NCH * 128], BF16)
    aRbf = sbufA("aRbf", [128, NCH * DS], BF16)
    aRbs = sbufA("aRbs", [128, NCH * DS], BF16)
    aUnb = sbufA("aUnb", [128, NCH], BF16)
    aNsTok = sbufA("aNsTok", [128, NCH * DS])

    R = 2
    rg = {}
    def ring(nm, shape, dtype=F32):
        rg[nm] = [sbufA(f"r{r}_{nm}", shape, dtype) for r in range(R)]
    ring("xch", [128, 8 * C])
    ring("xtl", [128, 8 * C])
    ring("xhat", [128, 8 * C], BF16)
    ring("lnt", [128, 8 * C])
    ring("xbq", [128, 16 * C], BF16)
    ring("mrow", [1, C]); ring("rrow", [1, C])
    ring("mrr", [1, 2 * C], BF16)
    ring("kT", [DS, C], BF16); ring("qT", [DS, C], BF16)
    ring("ktT", [DS, C], BF16); ring("vT", [DS, C], BF16)
    ring("ktok", [C, DS]); ring("qtok", [C, DS])
    ring("kttok", [C, DS])
    ring("kb", [C, DS], BF16); ring("qb", [C, DS], BF16)
    ring("ktb", [C, DS], BF16); ring("vb", [C, DS], BF16)
    ring("mb", [C, DS], BF16)
    ring("Rf", [C, 128], BF16); ring("Rs", [C, 128], BF16)
    ring("Ntf", [128, 128], BF16); ring("Nnf", [128, 128], BF16)
    ring("Nts", [128, 128], BF16); ring("Nns", [128, 128], BF16)
    ring("Sp1", [128, 256], BF16); ring("Sp2", [128, 256], BF16)
    ring("Sp3", [128, 256], BF16); ring("Sp4", [128, 256], BF16)
    ring("Xa", [C, 128], BF16); ring("Xb", [C, 128], BF16)
    ring("Xc", [C, 128], BF16); ring("Xd", [C, 128], BF16)
    ring("nu1b", [128, C], BF16)
    ring("z1b", [DS, C], BF16)
    ring("ucol", [C, 1])
    ring("sc1", [C, DS]); ring("sc2", [C, DS])
    ring("errc", [C, 1])
    ring("urow", [1, C], BF16)
    ring("cbc", [DS, C]); ring("wns", [DS, C]); ring("nsT", [DS, C])
    ring("rf", [C, DS]); ring("rs", [C, DS])
    ring("o1", [C, DS]); ring("o2", [C, DS]); ring("msc", [C, 1])
    ring("ohat", [C, DS], BF16); ring("ohT", [DS, C], BF16)
    ring("scr", [128, 8]); ring("sc8a", [128, 8]); ring("sc8b", [128, 8])
    ring("c1", [128, 1]); ring("c2", [128, 1]); ring("c3", [128, 1])
    ring("ohw", [128, 8], BF16); ring("ohs", [128, 8])

    psTok = psumA("psTok", [128, 512])
    psFeat = psumA("psFeat", [128, 384])
    psSq = psumA("psSq", [128, 256])
    psSq2 = psumA("psSq2", [128, 256])
    psX = psumA("psX", [C, 128])
    psX2 = psumA("psX2", [C, 128])
    psChO = psumA("psChO", [128, 512])
    psTr = psumA("psTr", [128, 512], BF16)

    spin_sem = ctx.enter_context(nc.semaphore("SPIN"))

    s = Sched(nc)
    A = s.add
    out_dmas = []

    def N(ap_):
        return ap_.tensor.name

    for dst, src in [(sIR1T, dIR1T), (sGWT, dGWT)]:
        A("POOL", (lambda d_, s_: lambda e: e.dma_start(
            out=d_, in_=s_.rearrange("(g p) i -> p g i", p=128)))(dst, src),
          writes=[N(dst)], is_dma=True)
    for src, off, w in [(dWkT, 0, DS), (dWvT, DS, DS), (dWqT, 2 * DS, DS),
                        (dPWT, 192, 3)]:
        dstv = dataclasses.replace(
            sWall, offset=sWall.offset + off,
            ap=[list(sWall.ap[0]), [195, 8], [1, w]])
        A("POOL", (lambda d_, s_: lambda e: e.dma_start(
            out=d_, in_=s_.rearrange("(g p) i -> p g i", p=128)))(dstv, src),
          writes=["sWall"], is_dma=True)
    for dst, src in [(sNM1T, dNM1T), (sNM2T, dNM2T), (sIR2T, dIR2T),
                     (sWoT, dWoT)]:
        A("POOL", (lambda d_, s_: lambda e: e.dma_start(out=d_, in_=s_))(dst, src),
          writes=[N(dst)], is_dma=True)
    for dst, src in [(sID, dID), (sEX, dEX), (sEXT, dEXT), (sMSL, dMSL),
                     (sMSU, dMSU), (sIOR, dIOR), (sIOC, dIOC), (sIOCR, dIOCR),
                     (sNMB1, dNMB1), (sNMB2, dNMB2), (sIRB1, dIRB1),
                     (sRelay, dRELAY), (sNWrowF, dNW), (sFq, dFQ)]:
        A("SP", (lambda d_, s_: lambda e: e.dma_start(out=d_, in_=s_))(dst, src),
          writes=[N(dst)], is_dma=True)
    A("SP", lambda e: e.dma_start(out=sFA, in_=_bpar(dFA)), writes=["sFA"], is_dma=True)
    A("SP", lambda e: e.dma_start(out=sSA, in_=_bpar(dSA)), writes=["sSA"], is_dma=True)
    A("SP", lambda e: e.dma_start(out=sIRB2, in_=_bpar(dIRB2)), writes=["sIRB2"], is_dma=True)
    A("SP", lambda e: e.dma_start(out=sLB, in_=_bpar(dLB)), writes=["sLB"], is_dma=True)
    A("SP", lambda e: e.dma_start(out=sNonceS, in_=dRELAY[0:1, 3:4]), writes=["sNonceS"], is_dma=True)
    A("SP", lambda e: e.dma_start(out=sIORB, in_=_bpar(dIOR)), writes=["sIORB"], is_dma=True)
    A("SP", lambda e: e.dma_start(out=sIOTA8, in_=dataclasses.replace(
        dIOR, ap=[[0, 128], [1, 8]])), writes=["sIOTA8"], is_dma=True)

    fqreg = {}

    def ld_fq(e):
        fqreg[0] = e.value_load(sFq)
        return e.nop()
    A("SP", ld_fq, reads=["sFq"], writes=["fqreg"])

    for e_i in range(10):
        for kind, src, dst, pat in [
                ("g", dWG, scrWG, "(g p) i -> p g i"),
                ("u", dWU, scrWU, "(g p) i -> p g i"),
                ("d", dWD, scrWD, "(c p) h -> p c h")]:
            A("POOL", (lambda s_, d_, ei_, pat_: lambda e: e.dma_start(
                out=d_[ei_], in_=s_[ei_].rearrange(pat_, p=128)))(
                    src, dst, e_i, pat),
              writes=[f"cast{kind}{e_i}"], is_dma=True)

    A("DVE", lambda e: e.memset(sONE1, 1.0), writes=["sONE1"])
    A("DVE", lambda e: e.memset(sONE128, 1.0), writes=["sONE128"])
    A("DVE", lambda e: e.memset(sPart, 0.0), writes=["sPart"])
    A("DVE", lambda e: e.memset(sEPS5, 1e-5), writes=["sEPS5"])
    A("DVE", lambda e: e.memset(sZeroHH, 0.0), writes=["sZeroHH"])
    A("DVE", lambda e: e.tensor_scalar(sIOTA8B, sIOTA8, BIG, None, ALU.add),
      reads=["sIOTA8"], writes=["sIOTA8B"])
    A("DVE", lambda e: e.tensor_copy(out=sNWrowB, in_=sNWrowF),
      reads=["sNWrowF"], writes=["sNWrowB"])
    A("PE", lambda e: e.matmul(psFeat[:, 0:DS], sONE1, sNWrowB, start=True, stop=True),
      reads=["sONE1", "sNWrowB"], writes=["psFeat"])
    A("DVE", lambda e: e.tensor_copy(out=sNW, in_=psFeat[:, 0:DS]),
      reads=["psFeat"], writes=["sNW"])
    A("DVE", lambda e: e.tensor_copy(out=sIDF, in_=sID),
      reads=["sID"], writes=["sIDF"])

    A("ACT", lambda e: e.activation(sLNFA, sFA, AF.Ln), reads=["sFA"], writes=["sLNFA"])
    A("ACT", lambda e: e.activation(sLNSA, sSA, AF.Ln), reads=["sSA"], writes=["sLNSA"])
    A("DVE", lambda e: e.tensor_scalar(sBF, sFA, -1.0, 1.0, ALU.mult, ALU.add),
      reads=["sFA"], writes=["sBF"])
    A("DVE", lambda e: e.tensor_scalar(sBS, sSA, -1.0, 1.0, ALU.mult, ALU.add),
      reads=["sSA"], writes=["sBS"])
    A("DVE", lambda e: e.tensor_scalar(sNBF, sBF, -1.0, None, ALU.mult),
      reads=["sBF"], writes=["sNBF"])
    A("DVE", lambda e: e.tensor_scalar(sNBS, sBS, -1.0, None, ALU.mult),
      reads=["sBS"], writes=["sNBS"])
    A("ACT", lambda e: e.activation(sFAC, sLNFA, AF.Exp, scale=float(C)),
      reads=["sLNFA"], writes=["sFAC"])
    A("ACT", lambda e: e.activation(sSAC, sLNSA, AF.Exp, scale=float(C)),
      reads=["sLNSA"], writes=["sSAC"])
    A("ACT", lambda e: e.activation(sDQFC, sIOC, AF.Exp, scale=sLNFA),
      reads=["sIOC", "sLNFA"], writes=["sDQFC"])
    A("ACT", lambda e: e.activation(sDQSC, sIOC, AF.Exp, scale=sLNSA),
      reads=["sIOC", "sLNSA"], writes=["sDQSC"])
    A("ACT", lambda e: e.activation(sDKFC, sIOCR, AF.Exp, scale=sLNFA),
      reads=["sIOCR", "sLNFA"], writes=["sDKFC"])
    A("ACT", lambda e: e.activation(sDKSC, sIOCR, AF.Exp, scale=sLNSA),
      reads=["sIOCR", "sLNSA"], writes=["sDKSC"])
    A("DVE", lambda e: e.tensor_scalar(sDKFC, sDKFC, sBF, None, ALU.mult),
      reads=["sBF"], writes=["sDKFC"])
    A("DVE", lambda e: e.tensor_scalar(sDKSC, sDKSC, sBS, None, ALU.mult),
      reads=["sBS"], writes=["sDKSC"])
    A("ACT", lambda e: e.activation(sDQF, sIORB, AF.Exp, scale=sLNFA),
      reads=["sIORB", "sLNFA"], writes=["sDQF"])
    A("ACT", lambda e: e.activation(sDQS, sIORB, AF.Exp, scale=sLNSA),
      reads=["sIORB", "sLNSA"], writes=["sDQS"])
    for tag, lncol, bcol, nbcol, negE, negET, posET in [
        ("f", sLNFA, sBF, sNBF, sNegE_f, sNegET_f, sPosET_f),
        ("s", sLNSA, sBS, sNBS, sNegE_s, sNegET_s, sPosET_s),
    ]:
        A("ACT", (lambda ln_: lambda e: e.activation(sEt, sEX, AF.Exp, scale=ln_))(lncol),
          reads=["sEX", N(lncol)], writes=["sEt"])
        A("DVE", (lambda nb_, d_: lambda e: e.scalar_tensor_tensor(
            d_, sEt, nb_, sMSL, ALU.mult, ALU.mult))(nbcol, negE),
          reads=["sEt", N(nbcol), "sMSL"], writes=[N(negE)])
        A("ACT", (lambda ln_: lambda e: e.activation(sEt, sEXT, AF.Exp, scale=ln_))(lncol),
          reads=["sEXT", N(lncol)], writes=["sEt"])
        A("DVE", (lambda nb_, d_: lambda e: e.scalar_tensor_tensor(
            d_, sEt, nb_, sMSU, ALU.mult, ALU.mult))(nbcol, negET),
          reads=["sEt", N(nbcol), "sMSU"], writes=[N(negET)])
        A("DVE", (lambda b_, d_: lambda e: e.scalar_tensor_tensor(
            d_, sEt, b_, sMSU, ALU.mult, ALU.mult))(bcol, posET),
          reads=["sEt", N(bcol), "sMSU"], writes=[N(posET)])

    def emit_ln(xsrc, xkey, r, i, stats_seg):
        xv = xsrc.rearrange("p (g t) -> p g t", g=8)
        xbq, lnt = rg["xbq"][r], rg["lnt"][r]
        mrow, rrow = rg["mrow"][r], rg["rrow"][r]
        mrr = rg["mrr"][r]
        mrowb, rrowb = mrr[:, 0:C], mrr[:, C:2 * C]
        xhat = rg["xhat"][r]
        bfv = dataclasses.replace(xbq, ap=[list(xbq.ap[0]), [2 * C, 8], [1, C]])
        sqv = dataclasses.replace(xbq, offset=xbq.offset + C,
                                  ap=[list(xbq.ap[0]), [2 * C, 8], [1, C]])
        A("POOL", lambda e: e.tensor_copy(out=bfv, in_=xv),
          reads=[xkey], writes=[N(xbq)])
        A("ACT", lambda e: e.activation(sqv, xv, AF.Square),
          reads=[xkey], writes=[N(xbq)])
        def mm_sums(e):
            last = None
            for g in range(8):
                last = e.matmul(psFeat[0:1, 128:384], sONE128,
                                xbq[:, g * 2 * C:(g + 1) * 2 * C],
                                start=(g == 0), stop=(g == 7))
            return last
        A("PE", mm_sums, reads=[N(xbq), "sONE128"], writes=["psFeat"])
        A("DVE", lambda e: e.tensor_scalar(mrow, psFeat[0:1, 128:128 + C],
                                           1.0 / D, None, ALU.mult),
          reads=["psFeat"], writes=[N(mrow)])
        A("DVE", lambda e: e.scalar_tensor_tensor(
            rrow, mrow, -1.0, mrow, ALU.mult, ALU.mult),
          reads=[N(mrow)], writes=[N(rrow)])
        A("DVE", lambda e: e.scalar_tensor_tensor(
            rrow, psFeat[0:1, 256:256 + C], 1.0 / D, rrow, ALU.mult, ALU.add),
          reads=["psFeat", N(rrow)], writes=[N(rrow)])
        A("ACT", lambda e: e.activation(rrow, rrow, AF.Ln, bias=sEPS5),
          reads=[N(rrow), "sEPS5"], writes=[N(rrow)])
        A("ACT", lambda e: e.activation(rrow, rrow, AF.Exp, scale=-0.5),
          reads=[N(rrow)], writes=[N(rrow)])
        if stats_seg is not None:
            cs = i * C
            sg = stats_seg
            op = A("SP", (lambda cs_, sg_, m_: lambda e: e.dma_start(
                out=statsScr[sg_:sg_ + 1, cs_:cs_ + C], in_=m_))(cs, sg, mrow),
                reads=[N(mrow)], is_dma=True)
            out_dmas.append(op)
            op = A("SP", (lambda cs_, sg_, r_: lambda e: e.dma_start(
                out=statsScr[sg_:sg_ + 1, TOK + cs_:TOK + cs_ + C],
                in_=r_))(cs, sg, rrow),
                reads=[N(rrow)], writes=[f"dstats@{i}"], is_dma=True)
            out_dmas.append(op)
        A("POOL", lambda e: e.tensor_copy(out=mrowb, in_=mrow),
          reads=[N(mrow)], writes=[N(mrr)])
        A("POOL", lambda e: e.tensor_copy(out=rrowb, in_=rrow),
          reads=[N(rrow)], writes=[N(mrr)])
        A("PE", lambda e: e.matmul(psFeat[:, 128:384], sONE1, mrr,
                                   start=True, stop=True),
          reads=[N(mrr), "sONE1"], writes=["psFeat"])
        A("DVE", lambda e: e.tensor_tensor(
            lnt.rearrange("p (g t) -> p g t", g=8), xv,
            _rep(psFeat[:, 128:128 + C], 8), ALU.subtract),
          reads=[xkey, "psFeat"], writes=[N(lnt)])
        A("DVE", lambda e: e.tensor_tensor(
            xhat.rearrange("p (g t) -> p g t", g=8),
            lnt.rearrange("p (g t) -> p g t", g=8),
            _rep(psFeat[:, 256:256 + C], 8), ALU.mult),
          reads=[N(lnt), "psFeat"], writes=[N(xhat)])
        return xhat

    def emit_prep(seg, i):
        r = i % R
        cs = i * C
        xch = rg["xch"][r]
        gc = seg * NCH + i
        A("POOL", (lambda gc_, x_: lambda e: e.dma_start(
            out=x_, in_=xchunk(gc_)))(gc, xch),
          writes=[N(xch)], is_dma=True)
        xhat = emit_ln(xch, N(xch), r, i, stats_seg=None)
        def mm_proj(e):
            last = None
            for g in range(8):
                last = e.matmul(psTok[:, 0:195], xhat[:, g * C:(g + 1) * C],
                                sWall[:, g * 195:(g + 1) * 195],
                                start=(g == 0), stop=(g == 7))
            return last
        A("PE", mm_proj, reads=[N(xhat), "sWall"], writes=["psTok"])
        def mm_imp(e):
            last = None
            for g in range(8):
                last = e.matmul(psFeat[0:DS, 0:C],
                                sIR1T[:, g * DS:(g + 1) * DS],
                                xhat[:, g * C:(g + 1) * C],
                                start=(g == 0), stop=(g == 7))
            return last
        A("PE", mm_imp, reads=[N(xhat), "sIR1T"], writes=["psFeat"])
        ktok, qtok = rg["ktok"][r], rg["qtok"][r]
        vtok = aVtok[:, i * DS:(i + 1) * DS]
        for nm, off, dst in [("k", 0, ktok), ("q", 2 * DS, qtok)]:
            c1, c2 = rg["c1"][r], rg["c2"][r]
            A("ACT", (lambda o_, c_: lambda e: e.activation(
                rg["sc1"][r], psTok[:, o_:o_ + DS], AF.Square, accum_out=c_))(off, c1),
              reads=["psTok"], writes=[N(rg["sc1"][r]), N(c1)])
            A("ACT", (lambda c_: lambda e: e.activation(c_, c_, AF.Ln))(c1),
              reads=[], writes=[N(c1)])
            A("ACT", (lambda c_: lambda e: e.activation(c_, c_, AF.Exp, scale=0.5))(c1),
              reads=[], writes=[N(c1)])
            A("DVE", (lambda c_: lambda e: e.tensor_scalar(
                c_, c_, 1e-12, None, ALU.max))(c1),
              reads=[], writes=[N(c1)])
            A("DVE", (lambda a_, b_: lambda e: e.reciprocal(b_, a_))(c1, c2),
              reads=[N(c1)], writes=[N(c2)])
            A("DVE", (lambda o_, d_, c_: lambda e: e.tensor_scalar(
                d_, psTok[:, o_:o_ + DS], c_, None, ALU.mult))(off, dst, c2),
              reads=["psTok", N(c2)], writes=[N(dst)])
        A("ACT", (lambda v_: lambda e: e.activation(
            v_, psTok[:, DS:2 * DS], AF.Copy))(vtok),
          reads=["psTok"], writes=[f"aVtok@{i}"])
        pwt = aPwt[:, i * 4:i * 4 + 4]
        c3 = rg["c3"][r]
        A("DVE", lambda e: e.reduce_max(c3, psTok[:, 192:195], AX.X),
          reads=["psTok"], writes=[N(c3)])
        A("DVE", lambda e: e.tensor_scalar(rg["c1"][r], c3, -1.0, None, ALU.mult),
          reads=[N(c3)], writes=[N(rg["c1"][r])])
        A("ACT", (lambda p_: lambda e: e.activation(
            p_[:, 0:3], psTok[:, 192:195], AF.Exp, bias=rg["c1"][r]))(pwt),
          reads=["psTok", N(rg["c1"][r])], writes=[f"aPwt@{i}"])
        A("DVE", (lambda p_: lambda e: e.reduce_sum(c3, p_[:, 0:3], AX.X))(pwt),
          reads=[f"aPwt@{i}"], writes=[N(c3)])
        A("DVE", lambda e: e.reciprocal(c3, c3), reads=[], writes=[N(c3)])
        A("DVE", (lambda p_: lambda e: e.tensor_scalar(
            p_[:, 0:3], p_[:, 0:3], c3, None, ALU.mult))(pwt),
          reads=[N(c3)], writes=[f"aPwt@{i}"])
        ucol = rg["ucol"][r]
        A("ACT", lambda e: e.activation(rg["z1b"][r], psFeat[0:DS, 0:C], AF.Gelu,
                                        bias=sIRB1),
          reads=["psFeat", "sIRB1"], writes=[N(rg["z1b"][r])])
        A("PE", lambda e: e.matmul(psFeat[:, 380:381], rg["z1b"][r], sIR2T,
                                   start=True, stop=True),
          reads=[N(rg["z1b"][r]), "sIR2T"], writes=["psFeat"])
        A("DVE", lambda e: e.tensor_scalar(ucol, psFeat[:, 380:381], sIRB2, 0.0,
                                           ALU.add, ALU.is_gt),
          reads=["psFeat", "sIRB2"], writes=[N(ucol)])
        kttok = rg["kttok"][r]
        A("DVE", lambda e: e.tensor_scalar(kttok, ktok, ucol, None, ALU.mult),
          reads=[N(ktok), N(ucol)], writes=[N(kttok)])
        A("DVE", (lambda kp_: lambda e: e.tensor_scalar(
            kp_, ktok, sDKFC, None, ALU.mult))(aKpf[:, i * DS:(i + 1) * DS]),
          reads=[N(ktok), "sDKFC"], writes=[f"aKpf@{i}"])
        A("DVE", (lambda kp_: lambda e: e.tensor_scalar(
            kp_, kttok, sDKSC, None, ALU.mult))(aKps[:, i * DS:(i + 1) * DS]),
          reads=[N(kttok), "sDKSC"], writes=[f"aKps@{i}"])
        A("DVE", lambda e: e.tensor_tensor(rg["sc1"][r], qtok, ktok, ALU.mult),
          reads=[N(qtok), N(ktok)], writes=[N(rg["sc1"][r])])
        A("DVE", (lambda q_: lambda e: e.reduce_sum(q_, rg["sc1"][r], AX.X))(
            aQkf[:, i:i + 1]),
          reads=[N(rg["sc1"][r])], writes=[f"aQkf@{i}"])
        A("DVE", lambda e: e.tensor_tensor(rg["sc2"][r], qtok, kttok, ALU.mult),
          reads=[N(qtok), N(kttok)], writes=[N(rg["sc2"][r])])
        A("DVE", (lambda q_: lambda e: e.reduce_sum(q_, rg["sc2"][r], AX.X))(
            aQks[:, i:i + 1]),
          reads=[N(rg["sc2"][r])], writes=[f"aQks@{i}"])
        Rf, Rs = rg["Rf"][r], rg["Rs"][r]
        A("POOL", (lambda v_: lambda e: e.tensor_copy(out=Rf[:, 0:DS], in_=v_))(vtok),
          reads=[f"aVtok@{i}"], writes=[N(Rf)])
        A("DVE", lambda e: e.tensor_scalar(Rf[:, DS:128], ktok, sDQFC, None, ALU.mult),
          reads=[N(ktok), "sDQFC"], writes=[N(Rf)])
        A("POOL", (lambda v_: lambda e: e.tensor_copy(out=Rs[:, 0:DS], in_=v_))(vtok),
          reads=[f"aVtok@{i}"], writes=[N(Rs)])
        A("DVE", lambda e: e.tensor_scalar(Rs[:, DS:128], kttok, sDQSC, None, ALU.mult),
          reads=[N(kttok), "sDQSC"], writes=[N(Rs)])
        for nm, srcap, srck in [("kb", ktok, N(ktok)), ("qb", qtok, N(qtok)),
                                ("ktb", kttok, N(kttok)),
                                ("vb", vtok, f"aVtok@{i}")]:
            A("POOL", (lambda d_, s_: lambda e: e.tensor_copy(out=d_, in_=s_))(
                rg[nm][r], srcap),
              reads=[srck], writes=[N(rg[nm][r])])
        def mm_tr1(e):
            e.transpose(psTr[0:DS, 0:C], rg["kb"][r], sID)
            return e.transpose(psTr[0:DS, C:2 * C], rg["qb"][r], sID)
        A("PE", mm_tr1, reads=[N(rg["kb"][r]), N(rg["qb"][r]), "sID"],
          writes=["psTr"])
        A("ACT", lambda e: e.activation(rg["kT"][r], psTr[0:DS, 0:C], AF.Copy),
          reads=["psTr"], writes=[N(rg["kT"][r])])
        A("ACT", lambda e: e.activation(rg["qT"][r], psTr[0:DS, C:2 * C], AF.Copy),
          reads=["psTr"], writes=[N(rg["qT"][r])])
        def mm_tr2(e):
            e.transpose(psTr[0:DS, 2 * C:3 * C], rg["ktb"][r], sID)
            return e.transpose(psTr[0:DS, 3 * C:4 * C], rg["vb"][r], sID)
        A("PE", mm_tr2, reads=[N(rg["ktb"][r]), N(rg["vb"][r]), "sID"],
          writes=["psTr"])
        A("ACT", lambda e: e.activation(rg["ktT"][r], psTr[0:DS, 2 * C:3 * C], AF.Copy),
          reads=["psTr"], writes=[N(rg["ktT"][r])])
        A("ACT", lambda e: e.activation(rg["vT"][r], psTr[0:DS, 3 * C:4 * C], AF.Copy),
          reads=["psTr"], writes=[N(rg["vT"][r])])
        A("DVE", (lambda q_: lambda e: e.tensor_tensor(
            q_, rg["qT"][r], sDQF[0:DS, :], ALU.mult))(aQhTf[:, cs:cs + C]),
          reads=[N(rg["qT"][r]), "sDQF"], writes=[f"aQhTf@{i}"])
        A("DVE", (lambda q_: lambda e: e.tensor_tensor(
            q_, rg["qT"][r], sDQS[0:DS, :], ALU.mult))(aQhTs[:, cs:cs + C]),
          reads=[N(rg["qT"][r]), "sDQS"], writes=[f"aQhTs@{i}"])
        def mm_A(e):
            e.matmul(psSq[:, 0:128], rg["kT"][r], rg["kT"][r], start=True, stop=True)
            return e.matmul(psSq[:, 128:256], rg["kT"][r], rg["qT"][r],
                            start=True, stop=True)
        A("PE", mm_A, reads=[N(rg["kT"][r]), N(rg["qT"][r])], writes=["psSq"])
        A("PE", lambda e: e.matmul(psSq2[:, 0:128], rg["ktT"][r], rg["ktT"][r],
                                   start=True, stop=True),
          reads=[N(rg["ktT"][r])], writes=["psSq2"])
        A("DVE", lambda e: e.tensor_tensor(rg["Ntf"][r], psSq[:, 0:128],
                                           sNegET_f, ALU.mult),
          reads=["psSq", "sNegET_f"], writes=[N(rg["Ntf"][r])])
        A("DVE", lambda e: e.tensor_tensor(rg["Nnf"][r], psSq[:, 0:128],
                                           sNegE_f, ALU.mult),
          reads=["psSq", "sNegE_f"], writes=[N(rg["Nnf"][r])])
        A("DVE", (lambda g_: lambda e: e.tensor_tensor(
            g_, psSq[:, 128:256], sPosET_f, ALU.mult))(aGqTf[:, cs:cs + C]),
          reads=["psSq", "sPosET_f"], writes=[f"aGqTf@{i}"])
        A("DVE", (lambda g_: lambda e: e.scalar_tensor_tensor(
            g_, psSq[:, 128:256], rg["ucol"][r], sPosET_s,
            ALU.mult, ALU.mult))(aGqTs[:, cs:cs + C]),
          reads=["psSq", N(rg["ucol"][r]), "sPosET_s"], writes=[f"aGqTs@{i}"])
        A("DVE", lambda e: e.tensor_tensor(rg["Nts"][r], psSq2[:, 0:128],
                                           sNegET_s, ALU.mult),
          reads=["psSq2", "sNegET_s"], writes=[N(rg["Nts"][r])])
        A("DVE", lambda e: e.tensor_tensor(rg["Nns"][r], psSq2[:, 0:128],
                                           sNegE_s, ALU.mult),
          reads=["psSq2", "sNegE_s"], writes=[N(rg["Nns"][r])])
        A("PE", lambda e: e.matmul(psFeat[:, 0:C], sNM1T, rg["vT"][r],
                                   start=True, stop=True),
          reads=[N(rg["vT"][r]), "sNM1T"], writes=["psFeat"])
        A("ACT", lambda e: e.activation(rg["nu1b"][r], psFeat[:, 0:C], AF.Gelu,
                                        bias=sNMB1),
          reads=["psFeat", "sNMB1"], writes=[N(rg["nu1b"][r])])
        A("PE", lambda e: e.matmul(psFeat[0:DS, 128:128 + C], sNM2T,
                                   rg["nu1b"][r], start=True, stop=True),
          reads=[N(rg["nu1b"][r]), "sNM2T"], writes=["psFeat"])
        A("DVE", (lambda n_: lambda e: e.tensor_scalar(
            n_, psFeat[0:DS, 128:128 + C], sNMB2, None, ALU.add))(
            aNuT[:, cs:cs + C]),
          reads=["psFeat", "sNMB2"], writes=[f"aNuT@{i}"])
        wy = {}
        for st in ("f", "s"):
            wy[st] = dict(
                Sp=([rg["Sp1"][r], rg["Sp2"][r]] if st == "f"
                    else [rg["Sp3"][r], rg["Sp4"][r]]),
                Xp=([rg["Xa"][r], rg["Xb"][r]] if st == "f"
                    else [rg["Xc"][r], rg["Xd"][r]]),
                cur_t=rg["Ntf" if st == "f" else "Nts"][r],
                cur_n=rg["Nnf" if st == "f" else "Nns"][r],
                psX=(psX if st == "f" else psX2),
                pxk=("psX" if st == "f" else "psX2"),
                psSq=(psSq if st == "f" else psSq2),
                pqk=("psSq" if st == "f" else "psSq2"),
                src=rg["Rf" if st == "f" else "Rs"][r],
            )
            wy[st]["ct_key"] = N(wy[st]["cur_t"])
            wy[st]["cn_key"] = N(wy[st]["cur_n"])
            wy[st]["src_key"] = N(wy[st]["src"])
        for lev in range(7):
            for st in ("f", "s"):
                w = wy[st]
                A("PE", (lambda t_, s_, p_: lambda e: e.matmul(
                    p_, t_, s_, start=True, stop=True))(
                        w["cur_t"], w["src"], w["psX"]),
                  reads=[w["ct_key"], w["src_key"]], writes=[w["pxk"]])
                dst = w["Xp"][lev % 2]
                A("DVE", (lambda d_, s_, p_: lambda e: e.tensor_tensor(
                    d_, p_, s_, ALU.add))(dst, w["src"], w["psX"]),
                  reads=[w["pxk"], w["src_key"]], writes=[N(dst)])
                w["src"], w["src_key"] = dst, N(dst)
                if lev < 6:
                    so = w["Sp"][lev % 2]
                    def mm_sq(e, t_=w["cur_t"], n_=w["cur_n"], ps_=w["psSq"]):
                        e.matmul(ps_[:, 0:128], t_, n_, start=True, stop=True)
                        return e.matmul(ps_[:, 128:256], n_, t_,
                                        start=True, stop=True)
                    A("PE", mm_sq, reads=[w["ct_key"], w["cn_key"]],
                      writes=[w["pqk"]])
                    A("ACT", (lambda o_, ps_: lambda e: e.activation(
                        o_, ps_, AF.Copy))(so, w["psSq"]),
                      reads=[w["pqk"]], writes=[N(so)])
                    w["cur_t"] = so[:, 0:128]
                    w["cur_n"] = so[:, 128:256]
                    w["ct_key"] = w["cn_key"] = N(so)
        for st in ("f", "s"):
            w = wy[st]
            src, src_key = w["src"], w["src_key"]
            off = 0 if st == "f" else DS
            A("DVE", (lambda s_, o_: lambda e: e.tensor_copy(
                out=aD0[:, i * 128 + o_:i * 128 + o_ + DS], in_=s_[:, 0:DS]))(
                src, off),
              reads=[src_key], writes=[f"aD0@{i}"])
            mb = rg["mb"][r]
            A("DVE", (lambda s_: lambda e: e.tensor_copy(
                out=mb, in_=s_[:, DS:128]))(src),
              reads=[src_key], writes=[N(mb)])
            poff = 0 if st == "f" else C
            A("PE", (lambda po_: lambda e: e.transpose(
                psTr[0:DS, po_:po_ + C], rg["mb"][r], sID))(poff),
              reads=[N(mb), "sID"], writes=["psTr"])
            moff = i * 2 * C + (0 if st == "f" else C)
            A("DVE", (lambda mo_, po_: lambda e: e.tensor_scalar(
                aMhT[:, mo_:mo_ + C], psTr[0:DS, po_:po_ + C],
                -1.0, None, ALU.mult))(moff, poff),
              reads=["psTr"], writes=[f"aMhT@{i}"])

    def emit_B1(seg, i):
        if True:
            A("POOL", (lambda i_: lambda e: e.tensor_copy(
                out=aHH[:, i_ * 2 * DS:(i_ + 1) * 2 * DS], in_=sHH))(i),
              reads=["sHH"], writes=[f"aHH@{i}"])
            def mm_d(e, i=i):
                e.matmul(psChO[0:C, 0:DS], aMhT[:, i * 2 * C:i * 2 * C + C],
                         sHH[:, 0:DS], start=True, stop=True)
                return e.matmul(psChO[0:C, DS:2 * DS],
                                aMhT[:, i * 2 * C + C:i * 2 * C + 2 * C],
                                sHH[:, DS:2 * DS], start=True, stop=True)
            A("PE", mm_d, reads=[f"aMhT@{i}", "sHH"], writes=["psChO"])
            A("DVE", (lambda i_: lambda e: e.tensor_tensor(
                aDb[:, i_ * 128:(i_ + 1) * 128], psChO[0:C, 0:2 * DS],
                aD0[:, i_ * 128:(i_ + 1) * 128], ALU.add))(i),
              reads=["psChO", f"aD0@{i}"], writes=[f"aDb@{i}"])
            def mm_H(e, i=i):
                e.matmul(psChO[0:DS, 2 * DS:3 * DS], aKpf[:, i * DS:(i + 1) * DS],
                         aDb[:, i * 128:i * 128 + DS], start=True, stop=True)
                return e.matmul(psChO[0:DS, 3 * DS:4 * DS],
                                aKps[:, i * DS:(i + 1) * DS],
                                aDb[:, i * 128 + DS:(i + 1) * 128],
                                start=True, stop=True)
            A("PE", mm_H, reads=[f"aKpf@{i}", f"aKps@{i}", f"aDb@{i}"],
              writes=["psChO"])
            A("DVE", lambda e: e.scalar_tensor_tensor(
                sHH[:, 0:DS], sHH[:, 0:DS], sFAC[0:DS, :],
                psChO[0:DS, 2 * DS:3 * DS], ALU.mult, ALU.add),
              reads=["psChO", "sFAC"], writes=["sHH"])
            A("DVE", lambda e: e.scalar_tensor_tensor(
                sHH[:, DS:2 * DS], sHH[:, DS:2 * DS], sSAC[0:DS, :],
                psChO[0:DS, 3 * DS:4 * DS], ALU.mult, ALU.add),
              reads=["psChO", "sSAC"], writes=["sHH"])

    def emit_B3a(seg, i):
        if True:
            def mm_rd(e, i=i):
                e.matmul(psTok[:, 256:256 + DS], aQhTf[:, i * C:(i + 1) * C],
                         aHH[:, i * 2 * DS:i * 2 * DS + DS], start=True, stop=False)
                e.matmul(psTok[:, 256:256 + DS], aGqTf[:, i * C:(i + 1) * C],
                         aDb[:, i * 128:i * 128 + DS], start=False, stop=True)
                e.matmul(psTok[:, 320:320 + DS], aQhTs[:, i * C:(i + 1) * C],
                         aHH[:, i * 2 * DS + DS:(i + 1) * 2 * DS],
                         start=True, stop=False)
                return e.matmul(psTok[:, 320:320 + DS], aGqTs[:, i * C:(i + 1) * C],
                                aDb[:, i * 128 + DS:(i + 1) * 128],
                                start=False, stop=True)
            A("PE", mm_rd, reads=[f"aQhTf@{i}", f"aQhTs@{i}", f"aGqTf@{i}",
                                  f"aGqTs@{i}", f"aDb@{i}", f"aHH@{i}"],
              writes=["psTok"])
            A("ACT", (lambda i_: lambda e: e.activation(
                aRbf[:, i_ * DS:(i_ + 1) * DS], psTok[:, 256:256 + DS], AF.Copy))(i),
              reads=["psTok"], writes=[f"aRbf@{i}"])
            A("ACT", (lambda i_: lambda e: e.activation(
                aRbs[:, i_ * DS:(i_ + 1) * DS], psTok[:, 320:320 + DS], AF.Copy))(i),
              reads=["psTok"], writes=[f"aRbs@{i}"])
            r = i % R
            A("DVE", (lambda i_, r_: lambda e: e.scalar_tensor_tensor(
                rg["sc1"][r_], aRbf[:, i_ * DS:(i_ + 1) * DS], -0.5,
                aVtok[:, i_ * DS:(i_ + 1) * DS], ALU.mult, ALU.add))(i, r),
              reads=[f"aRbf@{i}", f"aVtok@{i}"], writes=[N(rg["sc1"][r])])
            A("DVE", (lambda i_, r_: lambda e: e.scalar_tensor_tensor(
                rg["sc2"][r_], aRbs[:, i_ * DS:(i_ + 1) * DS], -0.5,
                rg["sc1"][r_], ALU.mult, ALU.add))(i, r),
              reads=[f"aRbs@{i}", N(rg["sc1"][r])], writes=[N(rg["sc2"][r])])
            A("DVE", (lambda r_: lambda e: e.tensor_tensor(
                rg["sc1"][r_], rg["sc2"][r_], rg["sc2"][r_], ALU.mult))(r),
              reads=[N(rg["sc2"][r])], writes=[N(rg["sc1"][r])])
            A("DVE", (lambda r_: lambda e: e.reduce_sum(
                rg["errc"][r_], rg["sc1"][r_], AX.X))(r),
              reads=[N(rg["sc1"][r])], writes=[N(rg["errc"][r])])
            A("DVE", (lambda i_, r_: lambda e: e.tensor_scalar(
                aUnb[:, i_:i_ + 1], rg["errc"][r_], TH_ERR, None, ALU.is_gt))(i, r),
              reads=[N(rg["errc"][r])], writes=[f"aUnb@{i}"])

    def emit_ns(seg, i):
        if True:
            r = i % R
            A("PE", (lambda i_: lambda e: e.matmul(
                psChO[0:1, 0:C], aUnb[:, i_:i_ + 1], sID,
                start=True, stop=True))(i),
              reads=[f"aUnb@{i}", "sID"], writes=["psChO"])
            A("DVE", lambda e: e.tensor_copy(out=rg["urow"][r],
                                             in_=psChO[0:1, 0:C]),
              reads=["psChO"], writes=[N(rg["urow"][r])])
            A("PE", lambda e: e.matmul(psChO[0:DS, 4 * DS:4 * DS + C],
                                       sONE1[:, 0:DS], rg["urow"][r],
                                       start=True, stop=True),
              reads=[N(rg["urow"][r]), "sONE1"], writes=["psChO"])
            A("DVE", lambda e: e.tensor_scalar(rg["cbc"][r],
                                               psChO[0:DS, 4 * DS:4 * DS + C],
                                               -0.1, 1.0, ALU.mult, ALU.add),
              reads=["psChO"], writes=[N(rg["cbc"][r])])
            A("DVE", (lambda i_: lambda e: e.scalar_tensor_tensor(
                rg["wns"][r], aNuT[:, i_ * C:(i_ + 1) * C], 0.1,
                psChO[0:DS, 4 * DS:4 * DS + C], ALU.mult, ALU.mult))(i),
              reads=[f"aNuT@{i}", "psChO"], writes=[N(rg["wns"][r])])
            A("DVE", lambda e: e.tensor_tensor_scan(
                rg["nsT"][r], rg["cbc"][r], rg["wns"][r], sNS0, ALU.mult, ALU.add),
              reads=[N(rg["cbc"][r]), N(rg["wns"][r]), "sNS0"],
              writes=[N(rg["nsT"][r])])
            A("DVE", lambda e: e.tensor_copy(out=sNS0, in_=rg["nsT"][r][:, C - 1:C]),
              reads=[N(rg["nsT"][r])], writes=["sNS0"])
            A("PE", lambda e: e.transpose(psChO[0:C, 6 * DS:7 * DS], rg["nsT"][r],
                                          sIDF[0:DS, 0:DS]),
              reads=[N(rg["nsT"][r]), "sIDF"], writes=["psChO"])
            A("DVE", (lambda i_: lambda e: e.tensor_copy(
                out=aNsTok[:, i_ * DS:(i_ + 1) * DS],
                in_=psChO[0:C, 6 * DS:7 * DS]))(i),
              reads=["psChO"], writes=[f"aNsTok@{i}"])

    def emit_tailc(seg, i):
        r = i % R
        cs = i * C
        xtl = rg["xtl"][r]
        gc = seg * NCH + i
        A("SP", (lambda gc_, x_: lambda e: e.dma_start(
            out=x_, in_=xchunk(gc_)))(gc, xtl),
          writes=[N(xtl)], is_dma=True)
        A("DVE", (lambda i_: lambda e: e.tensor_scalar(
            rg["sc1"][r], aDb[:, i_ * 128:i_ * 128 + DS], aQkf[:, i_:i_ + 1],
            None, ALU.mult))(i),
          reads=[f"aDb@{i}", f"aQkf@{i}"], writes=[N(rg["sc1"][r])])
        A("DVE", lambda e: e.tensor_scalar(rg["sc1"][r], rg["sc1"][r], sBF,
                                           None, ALU.mult),
          reads=["sBF"], writes=[N(rg["sc1"][r])])
        A("DVE", (lambda i_: lambda e: e.scalar_tensor_tensor(
            rg["rf"][r], aRbf[:, i_ * DS:(i_ + 1) * DS], sFA, rg["sc1"][r],
            ALU.mult, ALU.add))(i),
          reads=[f"aRbf@{i}", N(rg["sc1"][r]), "sFA"], writes=[N(rg["rf"][r])])
        A("DVE", (lambda i_: lambda e: e.tensor_scalar(
            rg["sc2"][r], aDb[:, i_ * 128 + DS:(i_ + 1) * 128],
            aQks[:, i_:i_ + 1], None, ALU.mult))(i),
          reads=[f"aDb@{i}", f"aQks@{i}"], writes=[N(rg["sc2"][r])])
        A("DVE", lambda e: e.tensor_scalar(rg["sc2"][r], rg["sc2"][r], sBS,
                                           None, ALU.mult),
          reads=["sBS"], writes=[N(rg["sc2"][r])])
        A("DVE", (lambda i_: lambda e: e.scalar_tensor_tensor(
            rg["rs"][r], aRbs[:, i_ * DS:(i_ + 1) * DS], sSA, rg["sc2"][r],
            ALU.mult, ALU.add))(i),
          reads=[f"aRbs@{i}", N(rg["sc2"][r]), "sSA"], writes=[N(rg["rs"][r])])
        pwt = aPwt[:, i * 4:i * 4 + 4]
        A("DVE", (lambda p_: lambda e: e.tensor_scalar(
            rg["o1"][r], rg["rf"][r], p_[:, 0:1], None, ALU.mult))(pwt),
          reads=[N(rg["rf"][r]), f"aPwt@{i}"], writes=[N(rg["o1"][r])])
        A("DVE", (lambda p_: lambda e: e.scalar_tensor_tensor(
            rg["o2"][r], rg["rs"][r], p_[:, 1:2], rg["o1"][r],
            ALU.mult, ALU.add))(pwt),
          reads=[N(rg["rs"][r]), f"aPwt@{i}", N(rg["o1"][r])],
          writes=[N(rg["o2"][r])])
        A("DVE", (lambda p_, i_: lambda e: e.scalar_tensor_tensor(
            rg["o1"][r], aNsTok[:, i_ * DS:(i_ + 1) * DS], p_[:, 2:3],
            rg["o2"][r], ALU.mult, ALU.add))(pwt, i),
          reads=[f"aNsTok@{i}", f"aPwt@{i}", N(rg["o2"][r])],
          writes=[N(rg["o1"][r])])
        A("ACT", lambda e: e.activation(rg["sc1"][r], rg["o1"][r], AF.Square,
                                        accum_out=rg["msc"][r]),
          reads=[N(rg["o1"][r])], writes=[N(rg["sc1"][r]), N(rg["msc"][r])])
        A("DVE", lambda e: e.tensor_scalar(rg["msc"][r], rg["msc"][r],
                                           1.0 / DS, 1e-6, ALU.mult, ALU.add),
          reads=[], writes=[N(rg["msc"][r])])
        A("ACT", lambda e: e.activation(rg["msc"][r], rg["msc"][r], AF.Ln),
          reads=[], writes=[N(rg["msc"][r])])
        A("ACT", lambda e: e.activation(rg["msc"][r], rg["msc"][r], AF.Exp,
                                        scale=-0.5),
          reads=[], writes=[N(rg["msc"][r])])
        A("DVE", lambda e: e.scalar_tensor_tensor(
            rg["ohat"][r], rg["o1"][r], rg["msc"][r], sNW, ALU.mult, ALU.mult),
          reads=[N(rg["o1"][r]), N(rg["msc"][r]), "sNW"],
          writes=[N(rg["ohat"][r])])
        A("PE", lambda e: e.transpose(psTr[0:DS, 3 * C:4 * C],
                                      rg["ohat"][r], sID),
          reads=[N(rg["ohat"][r]), "sID"], writes=["psTr"])
        A("DVE", lambda e: e.tensor_copy(out=rg["ohT"][r],
                                         in_=psTr[0:DS, 3 * C:4 * C]),
          reads=["psTr"], writes=[N(rg["ohT"][r])])
        for hf in range(2):
            def mm_wo(e, hf=hf, r=r):
                last = None
                for g in range(4):
                    gg = hf * 4 + g
                    last = e.matmul(psChO[:, g * C:(g + 1) * C],
                                    sWoT[:, gg * C:(gg + 1) * C], rg["ohT"][r],
                                    start=True, stop=True)
                return last
            A("PE", mm_wo, reads=[N(rg["ohT"][r]), "sWoT"], writes=["psChO"])
            xvh = xtl.rearrange("p (g t) -> p g t", g=8)[:, hf * 4:(hf + 1) * 4, :]
            A("DVE", (lambda xv_: lambda e: e.tensor_tensor(
                xv_, xv_, psChO.rearrange("p (g t) -> p g t", g=4), ALU.add))(xvh),
              reads=["psChO"], writes=[N(xtl)])
        op = A("POOL", (lambda seg_, i_, x_: lambda e: e.dma_start(
            out=x1w2(seg_, i_), in_=x_))(seg, i, xtl),
            reads=[N(xtl)], writes=[f"dx1@{i}"], is_dma=True)
        out_dmas.append(op)
        xhat = emit_ln(xtl, N(xtl), r, i, stats_seg=seg)
        def mm_rt(e):
            last = None
            for g in range(8):
                last = e.matmul(psTok[:, 200:208], xhat[:, g * C:(g + 1) * C],
                                sGWT[:, g * 8:(g + 1) * 8],
                                start=(g == 0), stop=(g == 7))
            return last
        A("PE", mm_rt, reads=[N(xhat), "sGWT"], writes=["psTok"])
        scr, a8, b8 = rg["scr"][r], rg["sc8a"][r], rg["sc8b"][r]
        c1, c2, c3 = rg["c1"][r], rg["c2"][r], rg["c3"][r]
        A("DVE", lambda e: e.tensor_tensor(scr, psTok[:, 200:208], sLB, ALU.add),
          reads=["psTok", "sLB"], writes=[N(scr)])
        A("ACT", lambda e: e.activation(scr, scr, AF.Sigmoid),
          reads=[], writes=[N(scr)])
        A("DVE", lambda e: e.reduce_max(c1, scr, AX.X),
          reads=[N(scr)], writes=[N(c1)])
        A("DVE", lambda e: e.tensor_scalar(a8, scr, c1, None, ALU.is_ge),
          reads=[N(scr), N(c1)], writes=[N(a8)])
        A("DVE", lambda e: e.scalar_tensor_tensor(a8, a8, -BIG, sIOTA8B,
                                                  ALU.mult, ALU.add),
          reads=["sIOTA8B"], writes=[N(a8)])
        A("DVE", lambda e: e.tensor_reduce(c2, a8, AX.X, ALU.min),
          reads=[N(a8)], writes=[N(c2)])
        A("DVE", lambda e: e.tensor_scalar(a8, sIOTA8, c2, None, ALU.is_equal),
          reads=[N(c2), "sIOTA8"], writes=[N(a8)])
        A("DVE", lambda e: e.scalar_tensor_tensor(b8, a8, -BIG, scr,
                                                  ALU.mult, ALU.add),
          reads=[N(a8), N(scr)], writes=[N(b8)])
        A("DVE", lambda e: e.reduce_max(c2, b8, AX.X),
          reads=[N(b8)], writes=[N(c2)])
        A("DVE", lambda e: e.tensor_scalar(b8, b8, c2, None, ALU.is_ge),
          reads=[N(c2)], writes=[N(b8)])
        A("DVE", lambda e: e.scalar_tensor_tensor(b8, b8, -BIG, sIOTA8B,
                                                  ALU.mult, ALU.add),
          reads=["sIOTA8B"], writes=[N(b8)])
        A("DVE", lambda e: e.tensor_reduce(c3, b8, AX.X, ALU.min),
          reads=[N(b8)], writes=[N(c3)])
        A("DVE", lambda e: e.tensor_scalar(b8, sIOTA8, c3, None, ALU.is_equal),
          reads=[N(c3), "sIOTA8"], writes=[N(b8)])
        A("DVE", lambda e: e.tensor_tensor(c3, c1, c2, ALU.add),
          reads=[N(c1), N(c2)], writes=[N(c3)])
        A("DVE", lambda e: e.tensor_scalar(c3, c3, 1e-8, None, ALU.add),
          reads=[], writes=[N(c3)])
        A("DVE", lambda e: e.reciprocal(c3, c3), reads=[], writes=[N(c3)])
        A("DVE", lambda e: e.tensor_scalar(c1, c1, c3, None, ALU.mult),
          reads=[N(c3)], writes=[N(c1)])
        A("DVE", lambda e: e.tensor_scalar(c2, c2, c3, None, ALU.mult),
          reads=[N(c3)], writes=[N(c2)])
        A("DVE", lambda e: e.tensor_tensor(rg["ohs"][r], a8, b8, ALU.add),
          reads=[N(a8), N(b8)], writes=[N(rg["ohs"][r])])
        A("DVE", lambda e: e.tensor_scalar(b8, b8, c2, None, ALU.mult),
          reads=[N(c2)], writes=[N(b8)])
        A("DVE", lambda e: e.scalar_tensor_tensor(rg["ohw"][r], a8, c1, b8,
                                                  ALU.mult, ALU.add),
          reads=[N(a8), N(c1), N(b8)], writes=[N(rg["ohw"][r])])
        A("DVE", (lambda i_: lambda e: e.tensor_copy(
            out=sOHSG[:, (i_ % 4) * 8:(i_ % 4 + 1) * 8], in_=rg["ohs"][r]))(i),
          reads=[N(rg["ohs"][r])], writes=[f"sOHSG@{i % 4}"])
        def mm_part(e, i=i, r=r):
            e.matmul(psTok[0:1, 208:216], sONE128, rg["ohw"][r],
                     start=True, stop=True)
            return e.matmul(psTok[0:1, 216:224], sONE128,
                            sOHSG[:, (i % 4) * 8:(i % 4 + 1) * 8],
                            start=True, stop=True)
        A("PE", mm_part, reads=[N(rg["ohw"][r]), f"sOHSG@{i % 4}", "sONE128"],
          writes=["psTok"])
        A("DVE", lambda e: e.tensor_tensor(sPart, sPart, psTok[0:1, 208:224],
                                           ALU.add),
          reads=["psTok"], writes=["sPart"])
        if i % 4 == 3:
            g4 = i // 4
            A("PE", lambda e: e.transpose(psTr[0:32, 2 * C:3 * C], sOHSG, sID),
              reads=["sOHSG@0", "sOHSG@1", "sOHSG@2", "sOHSG@3", "sID"],
              writes=["psTr"])
            A("DVE", lambda e: e.tensor_copy(out=sSelT,
                                             in_=psTr[0:32, 2 * C:3 * C]),
              reads=["psTr"], writes=["sSelT"])
            op = A("SP", (lambda seg_, g4_: lambda e: e.dma_start(
                out=selsw3(seg_, g4_), in_=sSelT))(seg, g4),
                reads=["sSelT"], writes=[f"dsels@{g4}"], is_dma=True)
            out_dmas.append(op)

    for seg in range(NSEG):
        if seg % 2 == 0:
            A("DVE", lambda e: e.tensor_copy(out=sHH, in_=sZeroHH),
              reads=["sZeroHH"], writes=["sHH"])
            A("DVE", lambda e: e.tensor_copy(out=sNS0, in_=sZeroHH[:, 0:1]),
              reads=["sZeroHH"], writes=["sNS0"])
        for i in range(NCH):
            emit_prep(seg, i)
        for i in range(NCH + 2):
            if i < NCH:
                emit_B1(seg, i)
            if 1 <= i < NCH + 1:
                emit_B3a(seg, i - 1)
            if i >= 2:
                emit_ns(seg, i - 2)
        for i in range(NCH):
            emit_tailc(seg, i)

    A("DVE", lambda e: e.tensor_copy(out=sTxPart, in_=sPart),
      reads=["sPart"], writes=["sTxPart"])
    def emit_txp(e):
        slot = e.value_load(sRelay[0:1, 0:1])
        return e.dma_start(out=shPart[bass.ts(slot, 1), :], in_=sTxPart)
    A("SP", emit_txp, reads=["sTxPart", "sRelay"], writes=["txp"], is_dma=True)
    def emit_txf(e):
        slot = e.value_load(sRelay[0:1, 0:1])
        return e.dma_start(out=shFlag[bass.ts(slot, 1), 0:1], in_=sNonceS)
    A("SP", emit_txf, reads=["txp", "sNonceS", "sRelay"], writes=["txf"],
      is_dma=True)
    def emit_spin(e):
        part = e.value_load(sRelay[0:1, 1:2])
        nonce = e.value_load(sRelay[0:1, 3:4])
        with e.register("r_cnt") as r_cnt, e.register("r_cond") as r_cond, \
             e.register("r_del") as r_del:
            e.reg_mov(r_cnt, 0)
            e.reg_mov(r_cond, 1)
            e.reg_mov(r_del, 0)
            with e.While(lambda: r_cond):
                for _ in range(400):
                    e.reg_add(r_del, r_del, 1)
                e.dma_start(out=sFlagRx,
                            in_=shFlag[bass.ts(part, 1), 0:1]
                            ).then_inc(spin_sem, 16)
                e.reg_add(r_cnt, r_cnt, 16)
                e.wait_ge(spin_sem, r_cnt)
                v = e.value_load(sFlagRx)
                e.reg_alu(r_cond, v, nonce, ALU.not_equal)
                with e.If_cmp(r_cnt, 16 * 4000000, "IS_GE"):
                    e.reg_mov(r_cond, 0)
        return e.nop()
    A("SP", emit_spin, reads=["sRelay", "txf"], writes=["spun", "sFlagRx"],
      tag="spin")
    def emit_rxp(e):
        part = e.value_load(sRelay[0:1, 1:2])
        return e.dma_start(out=sRxPart, in_=shPart[bass.ts(part, 1), :])
    A("SP", emit_rxp, reads=["spun", "sRelay"], writes=["sRxPart"], is_dma=True)
    A("DVE", lambda e: e.tensor_tensor(sWm, sPart, sRxPart, ALU.add),
      reads=["sPart", "sRxPart"], writes=["sWm"])
    A("DVE", lambda e: e.tensor_scalar(sWm[:, 8:16], sWm[:, 8:16], 1.0, None,
                                       ALU.max), reads=[], writes=["sWm"])
    A("DVE", lambda e: e.reciprocal(sWm[:, 8:16], sWm[:, 8:16]),
      reads=[], writes=["sWm"])
    A("DVE", lambda e: e.tensor_tensor(sWm[:, 0:8], sWm[:, 0:8], sWm[:, 8:16],
                                       ALU.mult), reads=[], writes=["sWm"])
    A("SP", lambda e: e.dma_start(out=wmScr, in_=sWm), reads=["sWm"],
      writes=["wmScr"], is_dma=True)
    A("SP", lambda e: e.dma_start(out=sWmbc, in_=_bpar(wmScr)),
      reads=["wmScr"], writes=["sWmbc"], is_dma=True)

    QC = D * TOK // 4
    for j in range(4):
        def cp_x1(e, j_=j):
            fq = fqreg[0]
            return e.dma_start(
                out=dataclasses.replace(
                    x1Own, offset=x1Own.offset + j_ * QC, ap=[[1, QC]]),
                in_=dataclasses.replace(
                    x1Scr, offset=x1Scr.offset + j_ * QC,
                    ap=[[D * TOK, NSEG], [1, QC]])[bass.ts(fq, 1), :])
        op = A("SP", cp_x1, reads=["fqreg"] + [f"dx1@{i}" for i in range(NCH)],
               writes=["x1Own"], is_dma=True)
        out_dmas.append(op)
    def cp_stats(e):
        fq = fqreg[0]
        return e.dma_start(out=statsOwn,
                           in_=statsScr[bass.ts(fq, 1), :])
    op = A("SP", cp_stats,
           reads=["fqreg"] + [f"dstats@{i}" for i in range(NCH)],
           writes=["statsOwn"], is_dma=True)
    out_dmas.append(op)
    def cp_sels(e):
        fq = fqreg[0]
        return e.dma_start(out=selsOwn,
                           in_=selsScr[bass.ts(fq, 1), :])
    op = A("SP", cp_sels,
           reads=["fqreg"] + [f"dsels@{i}" for i in range(NG)],
           writes=["selsOwn"], is_dma=True)
    out_dmas.append(op)

    s.set_barrier()
    ctxA.pop_all().close()

    x_q = sbuf("x_q", [128, 8 * QTOK])
    xhatF = sbuf("xhatF", [128, 8 * QTOK], BF16)
    slWG = [sbuf(f"slWG{j}", [128, 8 * GRP], BF16) for j in range(2)]
    slWU = [sbuf(f"slWU{j}", [128, 8 * GRP], BF16) for j in range(2)]
    slWD = [sbuf(f"slWD{j}", [128, 4 * D], BF16) for j in range(2)]
    sSilu = sbuf("sSilu", [128, 4 * GRP], BF16)
    sT1 = sbuf("sT1", [128, 4 * GRP], BF16)
    sSel = sbuf("sSel", [128, GRP], BF16)
    sDn = sbuf("sDn", [128, 4 * GRP], BF16)
    sMbc = sbuf("sMbc", [128, GRP])
    sRbc = sbuf("sRbc", [128, GRP])
    sLnT = sbuf("sLnT", [128, 8 * GRP])
    psG = ctx.enter_context(nc.psum_tensor("psG", [128, 2048], F32)).ap()
    psU = ctx.enter_context(nc.psum_tensor("psU", [128, 2048], F32)).ap()

    for p_i in range(NPASS):
        q0 = p_i * QTOK
        A("SP", (lambda pi_: lambda e: e.dma_start(
            out=x_q, in_=x1rd3(pi_)))(p_i),
          reads=["x1Own"],
          writes=["x_q"], is_dma=True)
        for gq in range(GPP):
            g = p_i * GPP + gq
            cs = g * GRP
            xv = x_q.rearrange("p (c g t) -> p c g t", c=8, g=8, t=128)[
                :, gq * 4:(gq + 1) * 4, :, :]
            A("SP", (lambda cs_: lambda e: e.dma_start(
                out=sMbc, in_=_bpar(statsOwn[0:1, cs_:cs_ + GRP])))(cs),
              reads=["statsOwn"],
              writes=["sMbc"], is_dma=True)
            A("SP", (lambda cs_: lambda e: e.dma_start(
                out=sRbc, in_=_bpar(statsOwn[0:1, TOK + cs_:TOK + cs_ + GRP])))(cs),
              reads=["statsOwn"],
              writes=["sRbc"], is_dma=True)
            xho = xhatF.rearrange("p (g c t) -> p c g t", g=8, c=8, t=128)[
                :, gq * 4:(gq + 1) * 4, :, :]
            ln4 = sLnT.rearrange("p (g c t) -> p c g t", g=8, c=4, t=128)
            def bc4(bc):
                return dataclasses.replace(
                    bc, ap=[list(bc.ap[0]), [128, 4], [0, 8], [1, 128]])
            A("DVE", (lambda xv_: lambda e: e.tensor_tensor(
                ln4, xv_, bc4(sMbc), ALU.subtract))(xv),
              reads=["x_q", "sMbc"], writes=["sLnT"])
            A("DVE", (lambda xho_: lambda e: e.tensor_tensor(
                xho_, ln4, bc4(sRbc), ALU.mult))(xho),
              reads=["sLnT", "sRbc"], writes=[f"xhatF@{gq}"])
        for e_i in range(10):
            sl = e_i % 2
            A("POOL", (lambda ei_, sl_: lambda e: e.dma_start(
                out=slWG[sl_], in_=scrWG[ei_]))(e_i, sl),
              reads=[f"castg{e_i}"], writes=[f"slWG{sl}"], is_dma=True)
            A("POOL", (lambda ei_, sl_: lambda e: e.dma_start(
                out=slWU[sl_], in_=scrWU[ei_]))(e_i, sl),
              reads=[f"castu{e_i}"], writes=[f"slWU{sl}"], is_dma=True)
            A("SP", (lambda ei_, sl_: lambda e: e.dma_start(
                out=slWD[sl_], in_=scrWD[ei_]))(e_i, sl),
              reads=[f"castd{e_i}"], writes=[f"slWD{sl}"], is_dma=True)
            for gq in range(GPP):
                g = p_i * GPP + gq
                cs = g * GRP
                if e_i >= 2:
                    A("SP", (lambda ei_, cs_: lambda e: e.dma_start(
                        out=sSel, in_=_bpar(selsOwn[0:1, (ei_ - 2) * TOK + cs_:
                                                    (ei_ - 2) * TOK + cs_ + GRP])))(e_i, cs),
                      reads=["selsOwn"],
                      writes=["sSel"], is_dma=True)
                def mm_gate(e, sl=sl, gq=gq):
                    last = None
                    for it in range(4):
                        for g8 in range(8):
                            last = e.matmul(
                                psG[:, it * GRP:(it + 1) * GRP],
                                slWG[sl][:, g8 * GRP + it * 128:
                                         g8 * GRP + (it + 1) * 128],
                                xhatF[:, g8 * QTOK + gq * GRP:
                                      g8 * QTOK + (gq + 1) * GRP],
                                start=(g8 == 0), stop=(g8 == 7))
                    return last
                A("PE", mm_gate, reads=[f"slWG{sl}", f"xhatF@{gq}"], writes=["psG"])
                A("ACT", lambda e: e.activation(sSilu, psG, AF.Silu),
                  reads=["psG"], writes=["sSilu"])
                def mm_up(e, sl=sl, gq=gq):
                    last = None
                    for it in range(4):
                        for g8 in range(8):
                            last = e.matmul(
                                psU[:, it * GRP:(it + 1) * GRP],
                                slWU[sl][:, g8 * GRP + it * 128:
                                         g8 * GRP + (it + 1) * 128],
                                xhatF[:, g8 * QTOK + gq * GRP:
                                      g8 * QTOK + (gq + 1) * GRP],
                                start=(g8 == 0), stop=(g8 == 7))
                    return last
                A("PE", mm_up, reads=[f"slWU{sl}", f"xhatF@{gq}"], writes=["psU"])
                A("DVE", lambda e: e.tensor_tensor(sT1, psU, sSilu, ALU.mult),
                  reads=["psU", "sSilu"], writes=["sT1"])
                if e_i >= 2:
                    A("DVE", (lambda ei_: lambda e: e.scalar_tensor_tensor(
                        sT1.rearrange("p (i t) -> p i t", i=4),
                        sT1.rearrange("p (i t) -> p i t", i=4),
                        sWmbc[:, ei_ - 2:ei_ - 1], _rep(sSel, 4),
                        ALU.mult, ALU.mult))(e_i),
                      reads=["sWmbc", "sSel"], writes=["sT1"])
                for hf in range(2):
                    ps = psG if hf == 0 else psU
                    pk = "psG" if hf == 0 else "psU"
                    def mm_dn(e, sl=sl, hf=hf, ps=ps):
                        last = None
                        for ht in range(4):
                            gg = hf * 4 + ht
                            for ic in range(4):
                                last = e.matmul(
                                    ps[:, ht * GRP:(ht + 1) * GRP],
                                    slWD[sl][:, ic * D + gg * 128:
                                             ic * D + (gg + 1) * 128],
                                    sT1[:, ic * GRP:(ic + 1) * GRP],
                                    start=(ic == 0), stop=(ic == 3))
                        return last
                    A("PE", mm_dn, reads=[f"slWD{sl}", "sT1"], writes=[pk])
                    xo = x_q.rearrange("p (c g t) -> p g c t",
                                       c=8, g=8, t=128)[
                        :, hf * 4:(hf + 1) * 4, gq * 4:(gq + 1) * 4, :]
                    if hf == 0:
                        A("DVE", (lambda xo_, ps_: lambda e: e.tensor_tensor(
                            xo_, xo_,
                            ps_.rearrange("p (g c t) -> p g c t",
                                          g=4, c=4, t=128),
                            ALU.add))(xo, ps),
                          reads=[pk], writes=["x_q"])
                    else:
                        A("ACT", (lambda ps_: lambda e: e.activation(
                            sDn, ps_, AF.Copy))(ps),
                          reads=[pk], writes=["sDn"])
                        A("POOL", (lambda xo_: lambda e: e.tensor_tensor(
                            xo_, xo_,
                            sDn.rearrange("p (g c t) -> p g c t",
                                          g=4, c=4, t=128),
                            ALU.add))(xo),
                          reads=["sDn"], writes=["x_q"])
        op = A("SP", (lambda pi_: lambda e: e.dma_start(
            out=outw3(pi_), in_=x_q))(p_i),
            reads=["x_q"], is_dma=True)
        out_dmas.append(op)
    A("SP", lambda e: e.nop(), dep_on=[o for o in out_dmas if o is not None])
    if os.environ.get("MK_PRCNT"):
        print("opcount:", len(s.ops))
    s.run()
    ctx.pop_all()
    return nc


_CACHE = {}


def _consts():
    t = np.arange(128, dtype=np.float32)
    TT, SS = np.meshgrid(t, t, indexing="ij")
    exTS = (TT - 1 - SS).astype(np.float32)
    return dict(
        identb=np.eye(128, dtype=nbf),
        exTS=exTS,
        exST=np.ascontiguousarray(exTS.T),
        mskSL=(SS < TT).astype(np.float32),
        mskSU=np.ascontiguousarray((SS < TT).astype(np.float32).T),
        iotaRow=np.ascontiguousarray(t.reshape(1, 128)),
        iotaCol=np.ascontiguousarray(t.reshape(128, 1)),
        iotaColRev=np.ascontiguousarray((127.0 - t).reshape(128, 1)),
    )


def kernel(L0_attn_x, L0_attn_Wk, L0_attn_Wv, L0_attn_Wq, L0_attn_fast_alpha,
           L0_attn_slow_alpha, L0_attn_nm_w1, L0_attn_nm_b1, L0_attn_nm_w2,
           L0_attn_nm_b2, L0_attn_ir_w1, L0_attn_ir_b1, L0_attn_ir_w2,
           L0_attn_ir_b2, L0_attn_pw, L0_attn_Wo, L0_attn_norm_w,
           L0_ffn_gate_w, L0_ffn_lb_bias, L0_ffn_expert_gate_w,
           L0_ffn_expert_up_w, L0_ffn_expert_down_w, L0_ffn_shared_gate_w,
           L0_ffn_shared_up_w, L0_ffn_shared_down_w):
    f = np.float32
    asn = np.ascontiguousarray
    x = np.asarray(L0_attn_x, f)
    consts = _consts()

    if "all" not in _CACHE:
        _CACHE["all"] = build_all()
    nc = _CACHE["all"]

    sgT = np.asarray(L0_ffn_shared_gate_w, f).T
    suT = np.asarray(L0_ffn_shared_up_w, f).T
    sdT = np.asarray(L0_ffn_shared_down_w, f).T
    eg = np.asarray(L0_ffn_expert_gate_w, f)
    eu = np.asarray(L0_ffn_expert_up_w, f)
    ed = np.asarray(L0_ffn_expert_down_w, f)
    wg = np.empty((10, D, GRP), f); wu = np.empty((10, D, GRP), f)
    wd = np.empty((10, GRP, D), f)
    wg[0] = sgT[:, 0:GRP]; wg[1] = sgT[:, GRP:2 * GRP]
    wu[0] = suT[:, 0:GRP]; wu[1] = suT[:, GRP:2 * GRP]
    wd[0] = sdT[0:GRP, :]; wd[1] = sdT[GRP:2 * GRP, :]
    for e_i in range(8):
        wg[2 + e_i] = eg[e_i].T
        wu[2 + e_i] = eu[e_i].T
        wd[2 + e_i] = ed[e_i].T

    common = dict(
        WkT=asn(np.asarray(L0_attn_Wk, f).T), WvT=asn(np.asarray(L0_attn_Wv, f).T),
        WqT=asn(np.asarray(L0_attn_Wq, f).T), ir1T=asn(np.asarray(L0_attn_ir_w1, f).T),
        ir2T=asn(np.asarray(L0_attn_ir_w2, f).T), pwT=asn(np.asarray(L0_attn_pw, f).T),
        nm1T=asn(np.asarray(L0_attn_nm_w1, f).T), nm2T=asn(np.asarray(L0_attn_nm_w2, f).T),
        nmb1=np.asarray(L0_attn_nm_b1, f).reshape(-1, 1),
        nmb2=np.asarray(L0_attn_nm_b2, f).reshape(-1, 1),
        irb1=np.asarray(L0_attn_ir_b1, f).reshape(-1, 1),
        irb2=np.asarray(L0_attn_ir_b2, f).reshape(1, 1),
        WoT=asn(np.asarray(L0_attn_Wo, f).T),
        normw=np.asarray(L0_attn_norm_w, f).reshape(1, -1),
        gwT=asn(np.asarray(L0_ffn_gate_w, f).T),
        lb=np.asarray(L0_ffn_lb_bias, f).reshape(1, -1),
        fa=np.asarray(L0_attn_fast_alpha, f).reshape(1, 1),
        sa=np.asarray(L0_attn_slow_alpha, f).reshape(1, 1),
        wgT=wg, wuT=wu, wdT=wd,
        **consts,
    )
    nonce = _fresh_nonce()
    in_maps = []
    for c in range(8):
        parity = c % 2
        p = c // 2
        b0 = 2 * parity
        m = dict(common)
        xpair = np.concatenate([x[b0], x[b0 + 1]], axis=0)
        m["xC"] = asn(xpair.reshape(128, 128, 8, 128).transpose(0, 3, 2, 1)
                      .reshape(16384, D))
        m["fq"] = np.array([[p]], np.int32)
        m["relay"] = np.array([[parity, 1 - parity, 1, nonce + c // 2]],
                              np.int32)
        in_maps.append(m)
    res = run_bass_kernel_spmd(nc, in_maps, core_ids=list(range(8)))
    r = res.results

    out = np.empty((4, 8192, D), f)
    for c in range(8):
        parity = c % 2
        p = c // 2
        b = 2 * parity + p // 2
        h = p % 2
        a = np.asarray(r[c]["outT"], f)
        out[b, h * TOK:(h + 1) * TOK, :] = (
            a.reshape(32, 128, 8, 128).transpose(0, 3, 2, 1).reshape(TOK, D))
    return out


# version 15
# speedup vs baseline: 1.0957x; 1.0594x over previous
import contextlib
import dataclasses
import os
import time
import numpy as np
import ml_dtypes

import concourse.bass as bass
import concourse.mybir as mybir
from concourse.bass_utils import run_bass_kernel_spmd

F32 = mybir.dt.float32
BF16 = mybir.dt.bfloat16
I32 = mybir.dt.int32
U8 = mybir.dt.uint8
AF = mybir.ActivationFunctionType
ALU = mybir.AluOpType
AX = mybir.AxisListType
nbf = ml_dtypes.bfloat16

D = 1024
DS = 64
TOK = 4096
NSEG = 4
C = 128
NCH = TOK // C
GRP = 512
NG = TOK // GRP
NPASS = 4
GPP = NG // NPASS
QTOK = TOK // NPASS
TH_ERR = float((1.0 + 1e-6) * np.log(np.float64(np.float32(0.7)) /
                                     np.float64(np.float32(0.3))))
BIG = 100.0


def _fresh_nonce():
    return ((int(time.time() * 1e6) ^ (os.getpid() << 16)) & 0x7FFFFF00) | 0x1100

ENGINES = ("PE", "DVE", "ACT", "POOL", "SP")


class _Op:
    __slots__ = ("idx", "engine", "emit", "reads", "writes", "is_dma",
                 "deps", "marked", "count", "dma_val", "tag",
                 "self_drain", "drain_val")

    def __init__(self, idx, engine, emit, reads, writes, is_dma, tag=""):
        self.idx = idx; self.engine = engine; self.emit = emit
        self.reads = tuple(reads); self.writes = tuple(writes)
        self.is_dma = is_dma; self.deps = []
        self.marked = False; self.count = None; self.dma_val = None
        self.tag = tag; self.self_drain = 0; self.drain_val = 0


class Sched:

    def __init__(self, nc):
        self.nc = nc
        self.ops = []
        self.last_writer = {}
        self.readers = {}
        self.bar = ()

    def add(self, engine, emit, reads=(), writes=(), is_dma=False, tag="",
            dep_on=()):
        reads = tuple(reads)
        writes = tuple(writes)
        psr = tuple(b for b in reads if b.startswith("ps") and b not in writes)
        if psr:
            writes = writes + psr
            reads = tuple(b for b in reads if not b.startswith("ps"))
        op = _Op(len(self.ops), engine, emit, reads, writes, is_dma, tag)
        deps = {}
        for d in tuple(dep_on) + tuple(self.bar):
            if d is not None:
                deps[d.idx] = d
        for b in op.reads:
            w = self.last_writer.get(b)
            if w is not None:
                deps[w.idx] = w
        for b in op.writes:
            w = self.last_writer.get(b)
            if w is not None:
                deps[w.idx] = w
            for rr in self.readers.get(b, ()):
                deps[rr.idx] = rr
        deps.pop(op.idx, None)
        op.deps = list(deps.values())
        for b in op.reads:
            self.readers.setdefault(b, []).append(op)
        for b in op.writes:
            self.last_writer[b] = op
            self.readers[b] = []
        self.ops.append(op)
        return op

    def set_barrier(self):
        last_op = {}
        last_dma = {}
        for op in self.ops:
            last_op[op.engine] = op
            if op.is_dma:
                last_dma[op.engine] = op
        bars = []
        for e in ENGINES:
            deps = []
            if e in last_op:
                deps.append(last_op[e])
            if e in last_dma and last_dma[e] is not last_op.get(e):
                deps.append(last_dma[e])
            b = self.add(e, (lambda eng: eng.nop()), tag="phase_bar",
                         dep_on=tuple(deps))
            bars.append(b)
        self.bar = tuple(bars)

    def prepare(self):
        issued = {e: 0 for e in ENGINES}
        drains = {e: None for e in ENGINES}
        new_ops = []
        nidx = 10 ** 7
        for op in self.ops:
            ndeps = []
            self_drain = 0
            for d in op.deps:
                if not d.is_dma:
                    ndeps.append(d)
                    continue
                q = d.engine
                if q == op.engine:
                    self_drain = max(self_drain, 16 * issued[q])
                else:
                    dr = drains[q]
                    if dr is None or dr[0] < d.dma_val:
                        dop = _Op(nidx, q, (lambda e: e.nop()), (), (),
                                  False, tag="drain")
                        nidx += 1
                        dop.deps = []
                        dop.drain_val = 16 * issued[q]
                        dop.self_drain = 0
                        new_ops.append(dop)
                        dr = (16 * issued[q], dop)
                        drains[q] = dr
                    ndeps.append(dr[1])
            op.deps = ndeps
            op.self_drain = self_drain
            new_ops.append(op)
            if op.is_dma:
                issued[op.engine] += 1
                op.dma_val = 16 * issued[op.engine]
        self.ops = new_ops
        for op in self.ops:
            for d in op.deps:
                d.marked = True
        counts = {e: 0 for e in ENGINES}
        for op in self.ops:
            if not op.is_dma and op.marked:
                counts[op.engine] += 1
                op.count = counts[op.engine]

    def emit_engine(self, ename, eng, sems, dsems):
        waited = {e: 0 for e in ENGINES}
        dwaited = 0
        for op in self.ops:
            if op.engine != ename:
                continue
            sd = max(getattr(op, "self_drain", 0),
                     getattr(op, "drain_val", 0))
            if sd > dwaited:
                eng.wait_ge(dsems[ename], sd)
                dwaited = sd
            for d in op.deps:
                if d.count is not None and d.count > waited[d.engine]:
                    eng.wait_ge(sems[d.engine], d.count)
                    waited[d.engine] = d.count
            inst = op.emit(eng)
            if op.is_dma:
                inst.then_inc(dsems[ename], 16)
            elif op.marked:
                inst.then_inc(sems[ename], 1)

    def run(self):
        nc = self.nc
        self.prepare()
        stack = contextlib.ExitStack()
        sems = {e: stack.enter_context(nc.semaphore("S" + e)) for e in ENGINES}
        dsems = {e: stack.enter_context(nc.semaphore("D" + e)) for e in ENGINES}
        self._sem_stack = stack
        with nc.Block() as block:
            block.tensor(lambda eng: self.emit_engine("PE", eng, sems, dsems))
            block.vector(lambda eng: self.emit_engine("DVE", eng, sems, dsems))
            block.scalar(lambda eng: self.emit_engine("ACT", eng, sems, dsems))
            block.gpsimd(lambda eng: self.emit_engine("POOL", eng, sems, dsems))
            block.sync(lambda eng: self.emit_engine("SP", eng, sems, dsems))


def _bpar(dram_ap, n=128):
    return dataclasses.replace(dram_ap, ap=[[0, n]] + list(dram_ap.ap[1:]))


def _rep(ap, n):
    return dataclasses.replace(ap, ap=list(ap.ap[:1]) + [[0, n]] + list(ap.ap[1:]))


def _repf(ap, n):
    return dataclasses.replace(ap, ap=list(ap.ap[:1]) + [[0, n]])


def build_all():
    nc = bass.Bass("TRN2", target_bir_lowering=False, detect_race_conditions=False)
    ctx = contextlib.ExitStack()
    ctxA = contextlib.ExitStack()

    def din(name, shape, dtype=F32):
        return nc.dram_tensor(name, shape, dtype, kind="ExternalInput").ap()

    def sbuf(name, shape, dtype=F32):
        return ctx.enter_context(nc.sbuf_tensor(name, shape, dtype)).ap()

    def sbufA(name, shape, dtype=F32):
        return ctxA.enter_context(nc.sbuf_tensor(name, shape, dtype)).ap()

    def psumA(name, shape, dtype=F32):
        return ctxA.enter_context(nc.psum_tensor(name, shape, dtype)).ap()

    dXC = din("xC", [NSEG * TOK // C * 128, D])
    dWkT = din("WkT", [D, DS]); dWvT = din("WvT", [D, DS])
    dWqT = din("WqT", [D, DS]); dIR1T = din("ir1T", [D, DS])
    dIR2T = din("ir2T", [DS, 1]); dPWT = din("pwT", [D, 3])
    dNM1T = din("nm1T", [DS, 2 * DS]); dNM2T = din("nm2T", [2 * DS, DS])
    dNMB1 = din("nmb1", [2 * DS, 1]); dNMB2 = din("nmb2", [DS, 1])
    dIRB1 = din("irb1", [DS, 1]); dIRB2 = din("irb2", [1, 1])
    dWoT = din("WoT", [DS, D]); dNW = din("normw", [1, DS])
    dGWT = din("gwT", [D, 8]); dLB = din("lb", [1, 8])
    dFA = din("fa", [1, 1]); dSA = din("sa", [1, 1])
    dID = din("identb", [128, 128], BF16)
    dEX = din("exTS", [128, 128]); dEXT = din("exST", [128, 128])
    dMSL = din("mskSL", [128, 128]); dMSU = din("mskSU", [128, 128])
    dIOR = din("iotaRow", [1, 128]); dIOC = din("iotaCol", [128, 1])
    dIOCR = din("iotaColRev", [128, 1])
    dRELAY = din("relay", [1, 4], I32)
    dFQ = din("fq", [1, 1], I32)
    dWG = din("wgT", [10, D, GRP])
    dWU = din("wuT", [10, D, GRP])
    dWD = din("wdT", [10, GRP, D])

    oOut = nc.dram_tensor("outT", [TOK // C * 128, D], F32,
                          kind="ExternalOutput").ap()

    x1Scr = nc.dram_tensor("x1Scr", [NSEG * TOK // C * 128, D], F32,
                           kind="Internal").ap()
    statsScr = nc.dram_tensor("statsScr", [NSEG, 2 * TOK], F32,
                              kind="Internal").ap()
    selsScr = nc.dram_tensor("selsScr", [NSEG, 8 * TOK], BF16,
                             kind="Internal").ap()
    scrWG = nc.dram_tensor("scrWG", [10, 128, 8 * GRP], BF16, kind="Internal").ap()
    scrWU = nc.dram_tensor("scrWU", [10, 128, 8 * GRP], BF16, kind="Internal").ap()
    scrWD = nc.dram_tensor("scrWD", [10, 128, 4 * D], BF16, kind="Internal").ap()
    wmScr = nc.dram_tensor("wmScr", [1, 16], F32, kind="Internal").ap()
    x1Own = nc.dram_tensor("x1Own", [TOK // C * 128, D], F32,
                           kind="Internal").ap()
    statsOwn = nc.dram_tensor("statsOwn", [1, 2 * TOK], F32,
                              kind="Internal").ap()
    selsOwn = nc.dram_tensor("selsOwn", [1, 8 * TOK], BF16,
                             kind="Internal").ap()
    shPart = nc.dram_tensor("shPart", [2, 16], F32, kind="Internal",
                            addr_space="Shared").ap()
    shFlag = nc.dram_tensor("shFlag", [2, 2], I32, kind="Internal",
                            addr_space="Shared").ap()

    def xchunk(gc):
        return dXC[gc * 128:(gc + 1) * 128, :]

    def x1w2(seg, i):
        gc = seg * NCH + i
        return x1Scr[gc * 128:(gc + 1) * 128, :]

    def x1rd3(p_i):
        return dataclasses.replace(
            x1Own, offset=x1Own.offset + p_i * 8 * 128 * D,
            ap=[[D, 128], [128 * D, 8], [1, D]])

    def outw3(p_i):
        return dataclasses.replace(
            oOut, offset=oOut.offset + p_i * 8 * 128 * D,
            ap=[[D, 128], [128 * D, 8], [1, D]])

    def selsw3(seg, g4):
        return dataclasses.replace(
            selsScr, offset=selsScr.offset + seg * (8 * TOK) + g4 * 512,
            ap=[[128, 4], [TOK, 8], [1, 128]])

    sRelay = sbuf("sRelay", [1, 4], I32)
    sNonceS = sbuf("sNonceS", [1, 1], I32)
    sFlagRx = sbuf("sFlagRx", [1, 1], I32)
    sPart = sbuf("sPart", [1, 16])
    sTxPart = sbuf("sTxPart", [1, 16])
    sRxPart = sbuf("sRxPart", [1, 16])
    sWm = sbuf("sWm", [1, 16])
    sWmbc = sbuf("sWmbc", [128, 16])
    sFq = sbuf("sFq", [1, 1], I32)

    sWall = sbufA("sWall", [128, 8 * 195], BF16)
    sIR1T = sbufA("sIR1T", [128, 8 * DS], BF16)
    sGWT = sbufA("sGWT", [128, 8 * 8], BF16)
    sNM1T = sbufA("sNM1T", [DS, 2 * DS], BF16)
    sNM2T = sbufA("sNM2T", [2 * DS, DS], BF16)
    sIR2T = sbufA("sIR2T", [DS, 1], BF16)
    sWoT = sbufA("sWoT", [DS, D], BF16)
    sNW = sbufA("sNW", [128, DS])
    sLB = sbufA("sLB", [128, 8])
    sID = sbufA("sID", [128, 128], BF16)
    sIDF = sbufA("sIDF", [128, 128])
    sEX = sbufA("sEX", [128, 128]); sEXT = sbufA("sEXT", [128, 128])
    sMSL = sbufA("sMSL", [128, 128]); sMSU = sbufA("sMSU", [128, 128])
    sIOR = sbufA("sIOR", [1, 128]); sIOC = sbufA("sIOC", [128, 1])
    sIOCR = sbufA("sIOCR", [128, 1]); sIORB = sbufA("sIORB", [128, 128])
    sONE1 = sbufA("sONE1", [1, 128], BF16)
    sONE128 = sbufA("sONE128", [128, 1], BF16)
    sFA = sbufA("sFA", [128, 1]); sSA = sbufA("sSA", [128, 1])
    sLNFA = sbufA("sLNFA", [128, 1]); sLNSA = sbufA("sLNSA", [128, 1])
    sBF = sbufA("sBF", [128, 1]); sBS = sbufA("sBS", [128, 1])
    sNBF = sbufA("sNBF", [128, 1]); sNBS = sbufA("sNBS", [128, 1])
    sFAC = sbufA("sFAC", [128, 1]); sSAC = sbufA("sSAC", [128, 1])
    sDQFC = sbufA("sDQFC", [128, 1]); sDQSC = sbufA("sDQSC", [128, 1])
    sDKFC = sbufA("sDKFC", [128, 1]); sDKSC = sbufA("sDKSC", [128, 1])
    sNMB1 = sbufA("sNMB1", [2 * DS, 1]); sNMB2 = sbufA("sNMB2", [DS, 1])
    sIRB1 = sbufA("sIRB1", [DS, 1]); sIRB2 = sbufA("sIRB2", [128, 1])
    sDQF = sbufA("sDQF", [128, 128]); sDQS = sbufA("sDQS", [128, 128])
    sEt = sbufA("sEt", [128, 128])
    sNegE_f = sbufA("sNegE_f", [128, 128], BF16)
    sNegET_f = sbufA("sNegET_f", [128, 128], BF16)
    sPosET_f = sbufA("sPosET_f", [128, 128])
    sNegE_s = sbufA("sNegE_s", [128, 128], BF16)
    sNegET_s = sbufA("sNegET_s", [128, 128], BF16)
    sPosET_s = sbufA("sPosET_s", [128, 128])
    sHH = sbufA("sHH", [DS, 2 * DS], BF16)
    sNS0 = sbufA("sNS0", [DS, 1])
    sZeroHH = sbufA("sZeroHH", [DS, 2 * DS])
    sOHSG = sbufA("sOHSG", [128, 32], BF16)
    sSelT = sbufA("sSelT", [32, 128], BF16)
    sIOTA8 = sbufA("sIOTA8", [128, 8])
    sIOTA8B = sbufA("sIOTA8B", [128, 8])
    sNWrowF = sbufA("sNWrowF", [1, DS])
    sEPS5 = sbufA("sEPS5", [1, 1])
    sNWrowB = sbufA("sNWrowB", [1, DS], BF16)

    aMhT = sbufA("aMhT", [DS, NCH * 2 * C], BF16)
    aD0 = sbufA("aD0", [128, NCH * 128])
    aGqTf = sbufA("aGqTf", [128, NCH * C], BF16)
    aGqTs = sbufA("aGqTs", [128, NCH * C], BF16)
    aQhTf = sbufA("aQhTf", [DS, NCH * C], BF16)
    aQhTs = sbufA("aQhTs", [DS, NCH * C], BF16)
    aKpf = sbufA("aKpf", [128, NCH * DS], BF16)
    aKps = sbufA("aKps", [128, NCH * DS], BF16)
    aVtok = sbufA("aVtok", [128, NCH * DS])
    aNuT = sbufA("aNuT", [DS, NCH * C])
    aPwt = sbufA("aPwt", [128, NCH * 4])
    aQkf = sbufA("aQkf", [128, NCH])
    aQks = sbufA("aQks", [128, NCH])
    aHH = sbufA("aHH", [DS, NCH * 2 * DS], BF16)
    aDb = sbufA("aDb", [128, NCH * 128], BF16)
    aRbf = sbufA("aRbf", [128, NCH * DS], BF16)
    aRbs = sbufA("aRbs", [128, NCH * DS], BF16)
    aUnb = sbufA("aUnb", [128, NCH], BF16)
    aNsTok = sbufA("aNsTok", [128, NCH * DS])

    R = 2
    rg = {}
    def ring(nm, shape, dtype=F32):
        rg[nm] = [sbufA(f"r{r}_{nm}", shape, dtype) for r in range(R)]
    ring("xch", [128, 8 * C])
    ring("xtl", [128, 8 * C])
    ring("xhat", [128, 8 * C], BF16)
    ring("lnt", [128, 8 * C])
    ring("xbq", [128, 16 * C], BF16)
    ring("mrow", [1, C]); ring("rrow", [1, C])
    ring("mrr", [1, 2 * C], BF16)
    ring("kT", [DS, C], BF16); ring("qT", [DS, C], BF16)
    ring("ktT", [DS, C], BF16); ring("vT", [DS, C], BF16)
    ring("ktok", [C, DS]); ring("qtok", [C, DS])
    ring("kttok", [C, DS])
    ring("kb", [C, DS], BF16); ring("qb", [C, DS], BF16)
    ring("ktb", [C, DS], BF16); ring("vb", [C, DS], BF16)
    ring("mb", [C, DS], BF16)
    ring("Rf", [C, 128], BF16); ring("Rs", [C, 128], BF16)
    ring("Ntf", [128, 128], BF16); ring("Nnf", [128, 128], BF16)
    ring("Nts", [128, 128], BF16); ring("Nns", [128, 128], BF16)
    ring("Sp1", [128, 256], BF16); ring("Sp2", [128, 256], BF16)
    ring("Sp3", [128, 256], BF16); ring("Sp4", [128, 256], BF16)
    ring("Xa", [C, 128], BF16); ring("Xb", [C, 128], BF16)
    ring("Xc", [C, 128], BF16); ring("Xd", [C, 128], BF16)
    ring("nu1b", [128, C], BF16)
    ring("z1b", [DS, C], BF16)
    ring("ucol", [C, 1])
    ring("sc1", [C, DS]); ring("sc2", [C, DS])
    ring("errc", [C, 1])
    ring("urow", [1, C], BF16)
    ring("cbc", [DS, C]); ring("wns", [DS, C]); ring("nsT", [DS, C])
    ring("rf", [C, DS]); ring("rs", [C, DS])
    ring("o1", [C, DS]); ring("o2", [C, DS]); ring("msc", [C, 1])
    ring("ohat", [C, DS], BF16); ring("ohT", [DS, C], BF16)
    ring("scr", [128, 8]); ring("sc8a", [128, 8]); ring("sc8b", [128, 8])
    ring("c1", [128, 1]); ring("c2", [128, 1]); ring("c3", [128, 1])
    ring("ohw", [128, 8], BF16); ring("ohs", [128, 8])

    psTok = psumA("psTok", [128, 512])
    psFeat = psumA("psFeat", [128, 384])
    psSq = psumA("psSq", [128, 256])
    psSq2 = psumA("psSq2", [128, 256])
    psX = psumA("psX", [C, 128])
    psX2 = psumA("psX2", [C, 128])
    psChO = psumA("psChO", [128, 512])
    psTr = psumA("psTr", [128, 512], BF16)

    spin_sem = ctx.enter_context(nc.semaphore("SPIN"))

    s = Sched(nc)
    A = s.add
    out_dmas = []

    def N(ap_):
        return ap_.tensor.name

    for dst, src in [(sIR1T, dIR1T), (sGWT, dGWT)]:
        A("POOL", (lambda d_, s_: lambda e: e.dma_start(
            out=d_, in_=s_.rearrange("(g p) i -> p g i", p=128)))(dst, src),
          writes=[N(dst)], is_dma=True)
    for src, off, w in [(dWkT, 0, DS), (dWvT, DS, DS), (dWqT, 2 * DS, DS),
                        (dPWT, 192, 3)]:
        dstv = dataclasses.replace(
            sWall, offset=sWall.offset + off,
            ap=[list(sWall.ap[0]), [195, 8], [1, w]])
        A("POOL", (lambda d_, s_: lambda e: e.dma_start(
            out=d_, in_=s_.rearrange("(g p) i -> p g i", p=128)))(dstv, src),
          writes=["sWall"], is_dma=True)
    for dst, src in [(sNM1T, dNM1T), (sNM2T, dNM2T), (sIR2T, dIR2T),
                     (sWoT, dWoT)]:
        A("POOL", (lambda d_, s_: lambda e: e.dma_start(out=d_, in_=s_))(dst, src),
          writes=[N(dst)], is_dma=True)
    for dst, src in [(sID, dID), (sEX, dEX), (sEXT, dEXT), (sMSL, dMSL),
                     (sMSU, dMSU), (sIOR, dIOR), (sIOC, dIOC), (sIOCR, dIOCR),
                     (sNMB1, dNMB1), (sNMB2, dNMB2), (sIRB1, dIRB1),
                     (sRelay, dRELAY), (sNWrowF, dNW), (sFq, dFQ)]:
        A("SP", (lambda d_, s_: lambda e: e.dma_start(out=d_, in_=s_))(dst, src),
          writes=[N(dst)], is_dma=True)
    A("SP", lambda e: e.dma_start(out=sFA, in_=_bpar(dFA)), writes=["sFA"], is_dma=True)
    A("SP", lambda e: e.dma_start(out=sSA, in_=_bpar(dSA)), writes=["sSA"], is_dma=True)
    A("SP", lambda e: e.dma_start(out=sIRB2, in_=_bpar(dIRB2)), writes=["sIRB2"], is_dma=True)
    A("SP", lambda e: e.dma_start(out=sLB, in_=_bpar(dLB)), writes=["sLB"], is_dma=True)
    A("SP", lambda e: e.dma_start(out=sNonceS, in_=dRELAY[0:1, 3:4]), writes=["sNonceS"], is_dma=True)
    A("SP", lambda e: e.dma_start(out=sIORB, in_=_bpar(dIOR)), writes=["sIORB"], is_dma=True)
    A("SP", lambda e: e.dma_start(out=sIOTA8, in_=dataclasses.replace(
        dIOR, ap=[[0, 128], [1, 8]])), writes=["sIOTA8"], is_dma=True)

    fqreg = {}

    def ld_fq(e):
        fqreg[0] = e.value_load(sFq)
        return e.nop()
    A("SP", ld_fq, reads=["sFq"], writes=["fqreg"])

    for e_i in range(10):
        for kind, src, dst, pat in [
                ("g", dWG, scrWG, "(g p) i -> p g i"),
                ("u", dWU, scrWU, "(g p) i -> p g i"),
                ("d", dWD, scrWD, "(c p) h -> p c h")]:
            A("POOL", (lambda s_, d_, ei_, pat_: lambda e: e.dma_start(
                out=d_[ei_], in_=s_[ei_].rearrange(pat_, p=128)))(
                    src, dst, e_i, pat),
              writes=[f"cast{kind}{e_i}"], is_dma=True)

    A("DVE", lambda e: e.memset(sONE1, 1.0), writes=["sONE1"])
    A("DVE", lambda e: e.memset(sONE128, 1.0), writes=["sONE128"])
    A("DVE", lambda e: e.memset(sPart, 0.0), writes=["sPart"])
    A("DVE", lambda e: e.memset(sEPS5, 1e-5), writes=["sEPS5"])
    A("DVE", lambda e: e.memset(sZeroHH, 0.0), writes=["sZeroHH"])
    A("DVE", lambda e: e.tensor_scalar(sIOTA8B, sIOTA8, BIG, None, ALU.add),
      reads=["sIOTA8"], writes=["sIOTA8B"])
    A("DVE", lambda e: e.tensor_copy(out=sNWrowB, in_=sNWrowF),
      reads=["sNWrowF"], writes=["sNWrowB"])
    A("PE", lambda e: e.matmul(psFeat[:, 0:DS], sONE1, sNWrowB, start=True, stop=True),
      reads=["sONE1", "sNWrowB"], writes=["psFeat"])
    A("DVE", lambda e: e.tensor_copy(out=sNW, in_=psFeat[:, 0:DS]),
      reads=["psFeat"], writes=["sNW"])
    A("DVE", lambda e: e.tensor_copy(out=sIDF, in_=sID),
      reads=["sID"], writes=["sIDF"])

    A("ACT", lambda e: e.activation(sLNFA, sFA, AF.Ln), reads=["sFA"], writes=["sLNFA"])
    A("ACT", lambda e: e.activation(sLNSA, sSA, AF.Ln), reads=["sSA"], writes=["sLNSA"])
    A("DVE", lambda e: e.tensor_scalar(sBF, sFA, -1.0, 1.0, ALU.mult, ALU.add),
      reads=["sFA"], writes=["sBF"])
    A("DVE", lambda e: e.tensor_scalar(sBS, sSA, -1.0, 1.0, ALU.mult, ALU.add),
      reads=["sSA"], writes=["sBS"])
    A("DVE", lambda e: e.tensor_scalar(sNBF, sBF, -1.0, None, ALU.mult),
      reads=["sBF"], writes=["sNBF"])
    A("DVE", lambda e: e.tensor_scalar(sNBS, sBS, -1.0, None, ALU.mult),
      reads=["sBS"], writes=["sNBS"])
    A("ACT", lambda e: e.activation(sFAC, sLNFA, AF.Exp, scale=float(C)),
      reads=["sLNFA"], writes=["sFAC"])
    A("ACT", lambda e: e.activation(sSAC, sLNSA, AF.Exp, scale=float(C)),
      reads=["sLNSA"], writes=["sSAC"])
    A("ACT", lambda e: e.activation(sDQFC, sIOC, AF.Exp, scale=sLNFA),
      reads=["sIOC", "sLNFA"], writes=["sDQFC"])
    A("ACT", lambda e: e.activation(sDQSC, sIOC, AF.Exp, scale=sLNSA),
      reads=["sIOC", "sLNSA"], writes=["sDQSC"])
    A("ACT", lambda e: e.activation(sDKFC, sIOCR, AF.Exp, scale=sLNFA),
      reads=["sIOCR", "sLNFA"], writes=["sDKFC"])
    A("ACT", lambda e: e.activation(sDKSC, sIOCR, AF.Exp, scale=sLNSA),
      reads=["sIOCR", "sLNSA"], writes=["sDKSC"])
    A("DVE", lambda e: e.tensor_scalar(sDKFC, sDKFC, sBF, None, ALU.mult),
      reads=["sBF"], writes=["sDKFC"])
    A("DVE", lambda e: e.tensor_scalar(sDKSC, sDKSC, sBS, None, ALU.mult),
      reads=["sBS"], writes=["sDKSC"])
    A("ACT", lambda e: e.activation(sDQF, sIORB, AF.Exp, scale=sLNFA),
      reads=["sIORB", "sLNFA"], writes=["sDQF"])
    A("ACT", lambda e: e.activation(sDQS, sIORB, AF.Exp, scale=sLNSA),
      reads=["sIORB", "sLNSA"], writes=["sDQS"])
    for tag, lncol, bcol, nbcol, negE, negET, posET in [
        ("f", sLNFA, sBF, sNBF, sNegE_f, sNegET_f, sPosET_f),
        ("s", sLNSA, sBS, sNBS, sNegE_s, sNegET_s, sPosET_s),
    ]:
        A("ACT", (lambda ln_: lambda e: e.activation(sEt, sEX, AF.Exp, scale=ln_))(lncol),
          reads=["sEX", N(lncol)], writes=["sEt"])
        A("DVE", (lambda nb_, d_: lambda e: e.scalar_tensor_tensor(
            d_, sEt, nb_, sMSL, ALU.mult, ALU.mult))(nbcol, negE),
          reads=["sEt", N(nbcol), "sMSL"], writes=[N(negE)])
        A("ACT", (lambda ln_: lambda e: e.activation(sEt, sEXT, AF.Exp, scale=ln_))(lncol),
          reads=["sEXT", N(lncol)], writes=["sEt"])
        A("DVE", (lambda nb_, d_: lambda e: e.scalar_tensor_tensor(
            d_, sEt, nb_, sMSU, ALU.mult, ALU.mult))(nbcol, negET),
          reads=["sEt", N(nbcol), "sMSU"], writes=[N(negET)])
        A("DVE", (lambda b_, d_: lambda e: e.scalar_tensor_tensor(
            d_, sEt, b_, sMSU, ALU.mult, ALU.mult))(bcol, posET),
          reads=["sEt", N(bcol), "sMSU"], writes=[N(posET)])

    def emit_ln(xsrc, xkey, r, i, stats_seg):
        xv = xsrc.rearrange("p (g t) -> p g t", g=8)
        xbq, lnt = rg["xbq"][r], rg["lnt"][r]
        mrow, rrow = rg["mrow"][r], rg["rrow"][r]
        mrr = rg["mrr"][r]
        mrowb, rrowb = mrr[:, 0:C], mrr[:, C:2 * C]
        xhat = rg["xhat"][r]
        bfv = dataclasses.replace(xbq, ap=[list(xbq.ap[0]), [2 * C, 8], [1, C]])
        sqv = dataclasses.replace(xbq, offset=xbq.offset + C,
                                  ap=[list(xbq.ap[0]), [2 * C, 8], [1, C]])
        A("POOL", lambda e: e.tensor_copy(out=bfv, in_=xv),
          reads=[xkey], writes=[N(xbq)])
        A("ACT", lambda e: e.activation(sqv, xv, AF.Square),
          reads=[xkey], writes=[N(xbq)])
        def mm_sums(e):
            last = None
            for g in range(8):
                last = e.matmul(psFeat[0:1, 128:384], sONE128,
                                xbq[:, g * 2 * C:(g + 1) * 2 * C],
                                start=(g == 0), stop=(g == 7))
            return last
        A("PE", mm_sums, reads=[N(xbq), "sONE128"], writes=["psFeat"])
        A("DVE", lambda e: e.tensor_scalar(mrow, psFeat[0:1, 128:128 + C],
                                           1.0 / D, None, ALU.mult),
          reads=["psFeat"], writes=[N(mrow)])
        A("DVE", lambda e: e.scalar_tensor_tensor(
            rrow, mrow, -1.0, mrow, ALU.mult, ALU.mult),
          reads=[N(mrow)], writes=[N(rrow)])
        A("DVE", lambda e: e.scalar_tensor_tensor(
            rrow, psFeat[0:1, 256:256 + C], 1.0 / D, rrow, ALU.mult, ALU.add),
          reads=["psFeat", N(rrow)], writes=[N(rrow)])
        A("ACT", lambda e: e.activation(rrow, rrow, AF.Ln, bias=sEPS5),
          reads=[N(rrow), "sEPS5"], writes=[N(rrow)])
        A("ACT", lambda e: e.activation(rrow, rrow, AF.Exp, scale=-0.5),
          reads=[N(rrow)], writes=[N(rrow)])
        if stats_seg is not None:
            cs = i * C
            sg = stats_seg
            op = A("SP", (lambda cs_, sg_, m_: lambda e: e.dma_start(
                out=statsScr[sg_:sg_ + 1, cs_:cs_ + C], in_=m_))(cs, sg, mrow),
                reads=[N(mrow)], is_dma=True)
            out_dmas.append(op)
            op = A("SP", (lambda cs_, sg_, r_: lambda e: e.dma_start(
                out=statsScr[sg_:sg_ + 1, TOK + cs_:TOK + cs_ + C],
                in_=r_))(cs, sg, rrow),
                reads=[N(rrow)], writes=[f"dstats@{i}"], is_dma=True)
            out_dmas.append(op)
        A("POOL", lambda e: e.tensor_copy(out=mrowb, in_=mrow),
          reads=[N(mrow)], writes=[N(mrr)])
        A("POOL", lambda e: e.tensor_copy(out=rrowb, in_=rrow),
          reads=[N(rrow)], writes=[N(mrr)])
        A("PE", lambda e: e.matmul(psFeat[:, 128:384], sONE1, mrr,
                                   start=True, stop=True),
          reads=[N(mrr), "sONE1"], writes=["psFeat"])
        A("DVE", lambda e: e.tensor_tensor(
            lnt.rearrange("p (g t) -> p g t", g=8), xv,
            _rep(psFeat[:, 128:128 + C], 8), ALU.subtract),
          reads=[xkey, "psFeat"], writes=[N(lnt)])
        A("DVE", lambda e: e.tensor_tensor(
            xhat.rearrange("p (g t) -> p g t", g=8),
            lnt.rearrange("p (g t) -> p g t", g=8),
            _rep(psFeat[:, 256:256 + C], 8), ALU.mult),
          reads=[N(lnt), "psFeat"], writes=[N(xhat)])
        return xhat

    def emit_prep(seg, i):
        r = i % R
        cs = i * C
        xch = rg["xch"][r]
        gc = seg * NCH + i
        A("POOL", (lambda gc_, x_: lambda e: e.dma_start(
            out=x_, in_=xchunk(gc_)))(gc, xch),
          writes=[N(xch)], is_dma=True)
        xhat = emit_ln(xch, N(xch), r, i, stats_seg=None)
        def mm_proj(e):
            last = None
            for g in range(8):
                last = e.matmul(psTok[:, 0:195], xhat[:, g * C:(g + 1) * C],
                                sWall[:, g * 195:(g + 1) * 195],
                                start=(g == 0), stop=(g == 7))
            return last
        A("PE", mm_proj, reads=[N(xhat), "sWall"], writes=["psTok"])
        def mm_imp(e):
            last = None
            for g in range(8):
                last = e.matmul(psFeat[0:DS, 0:C],
                                sIR1T[:, g * DS:(g + 1) * DS],
                                xhat[:, g * C:(g + 1) * C],
                                start=(g == 0), stop=(g == 7))
            return last
        A("PE", mm_imp, reads=[N(xhat), "sIR1T"], writes=["psFeat"])
        ktok, qtok = rg["ktok"][r], rg["qtok"][r]
        vtok = aVtok[:, i * DS:(i + 1) * DS]
        for nm, off, dst in [("k", 0, ktok), ("q", 2 * DS, qtok)]:
            c1, c2 = rg["c1"][r], rg["c2"][r]
            A("ACT", (lambda o_, c_: lambda e: e.activation(
                rg["sc1"][r], psTok[:, o_:o_ + DS], AF.Square, accum_out=c_))(off, c1),
              reads=["psTok"], writes=[N(rg["sc1"][r]), N(c1)])
            A("ACT", (lambda c_: lambda e: e.activation(c_, c_, AF.Ln))(c1),
              reads=[], writes=[N(c1)])
            A("ACT", (lambda c_: lambda e: e.activation(c_, c_, AF.Exp, scale=0.5))(c1),
              reads=[], writes=[N(c1)])
            A("DVE", (lambda c_: lambda e: e.tensor_scalar(
                c_, c_, 1e-12, None, ALU.max))(c1),
              reads=[], writes=[N(c1)])
            A("DVE", (lambda a_, b_: lambda e: e.reciprocal(b_, a_))(c1, c2),
              reads=[N(c1)], writes=[N(c2)])
            A("DVE", (lambda o_, d_, c_: lambda e: e.tensor_scalar(
                d_, psTok[:, o_:o_ + DS], c_, None, ALU.mult))(off, dst, c2),
              reads=["psTok", N(c2)], writes=[N(dst)])
        A("ACT", (lambda v_: lambda e: e.activation(
            v_, psTok[:, DS:2 * DS], AF.Copy))(vtok),
          reads=["psTok"], writes=[f"aVtok@{i}"])
        pwt = aPwt[:, i * 4:i * 4 + 4]
        c3 = rg["c3"][r]
        A("DVE", lambda e: e.reduce_max(c3, psTok[:, 192:195], AX.X),
          reads=["psTok"], writes=[N(c3)])
        A("DVE", lambda e: e.tensor_scalar(rg["c1"][r], c3, -1.0, None, ALU.mult),
          reads=[N(c3)], writes=[N(rg["c1"][r])])
        A("ACT", (lambda p_: lambda e: e.activation(
            p_[:, 0:3], psTok[:, 192:195], AF.Exp, bias=rg["c1"][r]))(pwt),
          reads=["psTok", N(rg["c1"][r])], writes=[f"aPwt@{i}"])
        A("DVE", (lambda p_: lambda e: e.reduce_sum(c3, p_[:, 0:3], AX.X))(pwt),
          reads=[f"aPwt@{i}"], writes=[N(c3)])
        A("DVE", lambda e: e.reciprocal(c3, c3), reads=[], writes=[N(c3)])
        A("DVE", (lambda p_: lambda e: e.tensor_scalar(
            p_[:, 0:3], p_[:, 0:3], c3, None, ALU.mult))(pwt),
          reads=[N(c3)], writes=[f"aPwt@{i}"])
        ucol = rg["ucol"][r]
        A("ACT", lambda e: e.activation(rg["z1b"][r], psFeat[0:DS, 0:C], AF.Gelu,
                                        bias=sIRB1),
          reads=["psFeat", "sIRB1"], writes=[N(rg["z1b"][r])])
        A("PE", lambda e: e.matmul(psFeat[:, 380:381], rg["z1b"][r], sIR2T,
                                   start=True, stop=True),
          reads=[N(rg["z1b"][r]), "sIR2T"], writes=["psFeat"])
        A("DVE", lambda e: e.tensor_scalar(ucol, psFeat[:, 380:381], sIRB2, 0.0,
                                           ALU.add, ALU.is_gt),
          reads=["psFeat", "sIRB2"], writes=[N(ucol)])
        kttok = rg["kttok"][r]
        A("DVE", lambda e: e.tensor_scalar(kttok, ktok, ucol, None, ALU.mult),
          reads=[N(ktok), N(ucol)], writes=[N(kttok)])
        A("DVE", (lambda kp_: lambda e: e.tensor_scalar(
            kp_, ktok, sDKFC, None, ALU.mult))(aKpf[:, i * DS:(i + 1) * DS]),
          reads=[N(ktok), "sDKFC"], writes=[f"aKpf@{i}"])
        A("DVE", (lambda kp_: lambda e: e.tensor_scalar(
            kp_, kttok, sDKSC, None, ALU.mult))(aKps[:, i * DS:(i + 1) * DS]),
          reads=[N(kttok), "sDKSC"], writes=[f"aKps@{i}"])
        A("DVE", lambda e: e.tensor_tensor(rg["sc1"][r], qtok, ktok, ALU.mult),
          reads=[N(qtok), N(ktok)], writes=[N(rg["sc1"][r])])
        A("DVE", (lambda q_: lambda e: e.reduce_sum(q_, rg["sc1"][r], AX.X))(
            aQkf[:, i:i + 1]),
          reads=[N(rg["sc1"][r])], writes=[f"aQkf@{i}"])
        A("DVE", lambda e: e.tensor_tensor(rg["sc2"][r], qtok, kttok, ALU.mult),
          reads=[N(qtok), N(kttok)], writes=[N(rg["sc2"][r])])
        A("DVE", (lambda q_: lambda e: e.reduce_sum(q_, rg["sc2"][r], AX.X))(
            aQks[:, i:i + 1]),
          reads=[N(rg["sc2"][r])], writes=[f"aQks@{i}"])
        Rf, Rs = rg["Rf"][r], rg["Rs"][r]
        A("POOL", (lambda v_: lambda e: e.tensor_copy(out=Rf[:, 0:DS], in_=v_))(vtok),
          reads=[f"aVtok@{i}"], writes=[N(Rf)])
        A("DVE", lambda e: e.tensor_scalar(Rf[:, DS:128], ktok, sDQFC, None, ALU.mult),
          reads=[N(ktok), "sDQFC"], writes=[N(Rf)])
        A("POOL", (lambda v_: lambda e: e.tensor_copy(out=Rs[:, 0:DS], in_=v_))(vtok),
          reads=[f"aVtok@{i}"], writes=[N(Rs)])
        A("DVE", lambda e: e.tensor_scalar(Rs[:, DS:128], kttok, sDQSC, None, ALU.mult),
          reads=[N(kttok), "sDQSC"], writes=[N(Rs)])
        for nm, srcap, srck in [("kb", ktok, N(ktok)), ("qb", qtok, N(qtok)),
                                ("ktb", kttok, N(kttok)),
                                ("vb", vtok, f"aVtok@{i}")]:
            A("POOL", (lambda d_, s_: lambda e: e.tensor_copy(out=d_, in_=s_))(
                rg[nm][r], srcap),
              reads=[srck], writes=[N(rg[nm][r])])
        def mm_tr1(e):
            e.transpose(psTr[0:DS, 0:C], rg["kb"][r], sID)
            return e.transpose(psTr[0:DS, C:2 * C], rg["qb"][r], sID)
        A("PE", mm_tr1, reads=[N(rg["kb"][r]), N(rg["qb"][r]), "sID"],
          writes=["psTr"])
        A("ACT", lambda e: e.activation(rg["kT"][r], psTr[0:DS, 0:C], AF.Copy),
          reads=["psTr"], writes=[N(rg["kT"][r])])
        A("ACT", lambda e: e.activation(rg["qT"][r], psTr[0:DS, C:2 * C], AF.Copy),
          reads=["psTr"], writes=[N(rg["qT"][r])])
        def mm_tr2(e):
            e.transpose(psTr[0:DS, 2 * C:3 * C], rg["ktb"][r], sID)
            return e.transpose(psTr[0:DS, 3 * C:4 * C], rg["vb"][r], sID)
        A("PE", mm_tr2, reads=[N(rg["ktb"][r]), N(rg["vb"][r]), "sID"],
          writes=["psTr"])
        A("ACT", lambda e: e.activation(rg["ktT"][r], psTr[0:DS, 2 * C:3 * C], AF.Copy),
          reads=["psTr"], writes=[N(rg["ktT"][r])])
        A("ACT", lambda e: e.activation(rg["vT"][r], psTr[0:DS, 3 * C:4 * C], AF.Copy),
          reads=["psTr"], writes=[N(rg["vT"][r])])
        A("DVE", (lambda q_: lambda e: e.tensor_tensor(
            q_, rg["qT"][r], sDQF[0:DS, :], ALU.mult))(aQhTf[:, cs:cs + C]),
          reads=[N(rg["qT"][r]), "sDQF"], writes=[f"aQhTf@{i}"])
        A("DVE", (lambda q_: lambda e: e.tensor_tensor(
            q_, rg["qT"][r], sDQS[0:DS, :], ALU.mult))(aQhTs[:, cs:cs + C]),
          reads=[N(rg["qT"][r]), "sDQS"], writes=[f"aQhTs@{i}"])
        def mm_A(e):
            e.matmul(psSq[:, 0:128], rg["kT"][r], rg["kT"][r], start=True, stop=True)
            return e.matmul(psSq[:, 128:256], rg["kT"][r], rg["qT"][r],
                            start=True, stop=True)
        A("PE", mm_A, reads=[N(rg["kT"][r]), N(rg["qT"][r])], writes=["psSq"])
        A("PE", lambda e: e.matmul(psSq2[:, 0:128], rg["ktT"][r], rg["ktT"][r],
                                   start=True, stop=True),
          reads=[N(rg["ktT"][r])], writes=["psSq2"])
        A("DVE", lambda e: e.tensor_tensor(rg["Ntf"][r], psSq[:, 0:128],
                                           sNegET_f, ALU.mult),
          reads=["psSq", "sNegET_f"], writes=[N(rg["Ntf"][r])])
        A("DVE", lambda e: e.tensor_tensor(rg["Nnf"][r], psSq[:, 0:128],
                                           sNegE_f, ALU.mult),
          reads=["psSq", "sNegE_f"], writes=[N(rg["Nnf"][r])])
        A("DVE", (lambda g_: lambda e: e.tensor_tensor(
            g_, psSq[:, 128:256], sPosET_f, ALU.mult))(aGqTf[:, cs:cs + C]),
          reads=["psSq", "sPosET_f"], writes=[f"aGqTf@{i}"])
        A("DVE", (lambda g_: lambda e: e.scalar_tensor_tensor(
            g_, psSq[:, 128:256], rg["ucol"][r], sPosET_s,
            ALU.mult, ALU.mult))(aGqTs[:, cs:cs + C]),
          reads=["psSq", N(rg["ucol"][r]), "sPosET_s"], writes=[f"aGqTs@{i}"])
        A("DVE", lambda e: e.tensor_tensor(rg["Nts"][r], psSq2[:, 0:128],
                                           sNegET_s, ALU.mult),
          reads=["psSq2", "sNegET_s"], writes=[N(rg["Nts"][r])])
        A("DVE", lambda e: e.tensor_tensor(rg["Nns"][r], psSq2[:, 0:128],
                                           sNegE_s, ALU.mult),
          reads=["psSq2", "sNegE_s"], writes=[N(rg["Nns"][r])])
        A("PE", lambda e: e.matmul(psFeat[:, 0:C], sNM1T, rg["vT"][r],
                                   start=True, stop=True),
          reads=[N(rg["vT"][r]), "sNM1T"], writes=["psFeat"])
        A("ACT", lambda e: e.activation(rg["nu1b"][r], psFeat[:, 0:C], AF.Gelu,
                                        bias=sNMB1),
          reads=["psFeat", "sNMB1"], writes=[N(rg["nu1b"][r])])
        A("PE", lambda e: e.matmul(psFeat[0:DS, 128:128 + C], sNM2T,
                                   rg["nu1b"][r], start=True, stop=True),
          reads=[N(rg["nu1b"][r]), "sNM2T"], writes=["psFeat"])
        A("DVE", (lambda n_: lambda e: e.tensor_scalar(
            n_, psFeat[0:DS, 128:128 + C], sNMB2, None, ALU.add))(
            aNuT[:, cs:cs + C]),
          reads=["psFeat", "sNMB2"], writes=[f"aNuT@{i}"])
        wy = {}
        for st in ("f", "s"):
            wy[st] = dict(
                Sp=([rg["Sp1"][r], rg["Sp2"][r]] if st == "f"
                    else [rg["Sp3"][r], rg["Sp4"][r]]),
                Xp=([rg["Xa"][r], rg["Xb"][r]] if st == "f"
                    else [rg["Xc"][r], rg["Xd"][r]]),
                cur_t=rg["Ntf" if st == "f" else "Nts"][r],
                cur_n=rg["Nnf" if st == "f" else "Nns"][r],
                psX=(psX if st == "f" else psX2),
                pxk=("psX" if st == "f" else "psX2"),
                psSq=(psSq if st == "f" else psSq2),
                pqk=("psSq" if st == "f" else "psSq2"),
                src=rg["Rf" if st == "f" else "Rs"][r],
            )
            wy[st]["ct_key"] = N(wy[st]["cur_t"])
            wy[st]["cn_key"] = N(wy[st]["cur_n"])
            wy[st]["src_key"] = N(wy[st]["src"])
        for lev in range(7):
            for st in ("f", "s"):
                w = wy[st]
                A("PE", (lambda t_, s_, p_: lambda e: e.matmul(
                    p_, t_, s_, start=True, stop=True))(
                        w["cur_t"], w["src"], w["psX"]),
                  reads=[w["ct_key"], w["src_key"]], writes=[w["pxk"]])
                dst = w["Xp"][lev % 2]
                A("DVE", (lambda d_, s_, p_: lambda e: e.tensor_tensor(
                    d_, p_, s_, ALU.add))(dst, w["src"], w["psX"]),
                  reads=[w["pxk"], w["src_key"]], writes=[N(dst)])
                w["src"], w["src_key"] = dst, N(dst)
                if lev < 6:
                    so = w["Sp"][lev % 2]
                    def mm_sq(e, t_=w["cur_t"], n_=w["cur_n"], ps_=w["psSq"]):
                        e.matmul(ps_[:, 0:128], t_, n_, start=True, stop=True)
                        return e.matmul(ps_[:, 128:256], n_, t_,
                                        start=True, stop=True)
                    A("PE", mm_sq, reads=[w["ct_key"], w["cn_key"]],
                      writes=[w["pqk"]])
                    A("ACT", (lambda o_, ps_: lambda e: e.activation(
                        o_, ps_, AF.Copy))(so, w["psSq"]),
                      reads=[w["pqk"]], writes=[N(so)])
                    w["cur_t"] = so[:, 0:128]
                    w["cur_n"] = so[:, 128:256]
                    w["ct_key"] = w["cn_key"] = N(so)
        for st in ("f", "s"):
            w = wy[st]
            src, src_key = w["src"], w["src_key"]
            off = 0 if st == "f" else DS
            A("DVE", (lambda s_, o_: lambda e: e.tensor_copy(
                out=aD0[:, i * 128 + o_:i * 128 + o_ + DS], in_=s_[:, 0:DS]))(
                src, off),
              reads=[src_key], writes=[f"aD0@{i}"])
            mb = rg["mb"][r]
            A("DVE", (lambda s_: lambda e: e.tensor_copy(
                out=mb, in_=s_[:, DS:128]))(src),
              reads=[src_key], writes=[N(mb)])
            poff = 0 if st == "f" else C
            A("PE", (lambda po_: lambda e: e.transpose(
                psTr[0:DS, po_:po_ + C], rg["mb"][r], sID))(poff),
              reads=[N(mb), "sID"], writes=["psTr"])
            moff = i * 2 * C + (0 if st == "f" else C)
            A("DVE", (lambda mo_, po_: lambda e: e.tensor_scalar(
                aMhT[:, mo_:mo_ + C], psTr[0:DS, po_:po_ + C],
                -1.0, None, ALU.mult))(moff, poff),
              reads=["psTr"], writes=[f"aMhT@{i}"])

    def emit_B1(seg, i):
        if True:
            A("POOL", (lambda i_: lambda e: e.tensor_copy(
                out=aHH[:, i_ * 2 * DS:(i_ + 1) * 2 * DS], in_=sHH))(i),
              reads=["sHH"], writes=[f"aHH@{i}"])
            def mm_d(e, i=i):
                e.matmul(psChO[0:C, 0:DS], aMhT[:, i * 2 * C:i * 2 * C + C],
                         sHH[:, 0:DS], start=True, stop=True)
                return e.matmul(psChO[0:C, DS:2 * DS],
                                aMhT[:, i * 2 * C + C:i * 2 * C + 2 * C],
                                sHH[:, DS:2 * DS], start=True, stop=True)
            A("PE", mm_d, reads=[f"aMhT@{i}", "sHH"], writes=["psChO"])
            A("DVE", (lambda i_: lambda e: e.tensor_tensor(
                aDb[:, i_ * 128:(i_ + 1) * 128], psChO[0:C, 0:2 * DS],
                aD0[:, i_ * 128:(i_ + 1) * 128], ALU.add))(i),
              reads=["psChO", f"aD0@{i}"], writes=[f"aDb@{i}"])
            def mm_H(e, i=i):
                e.matmul(psChO[0:DS, 2 * DS:3 * DS], aKpf[:, i * DS:(i + 1) * DS],
                         aDb[:, i * 128:i * 128 + DS], start=True, stop=True)
                return e.matmul(psChO[0:DS, 3 * DS:4 * DS],
                                aKps[:, i * DS:(i + 1) * DS],
                                aDb[:, i * 128 + DS:(i + 1) * 128],
                                start=True, stop=True)
            A("PE", mm_H, reads=[f"aKpf@{i}", f"aKps@{i}", f"aDb@{i}"],
              writes=["psChO"])
            A("DVE", lambda e: e.scalar_tensor_tensor(
                sHH[:, 0:DS], sHH[:, 0:DS], sFAC[0:DS, :],
                psChO[0:DS, 2 * DS:3 * DS], ALU.mult, ALU.add),
              reads=["psChO", "sFAC"], writes=["sHH"])
            A("DVE", lambda e: e.scalar_tensor_tensor(
                sHH[:, DS:2 * DS], sHH[:, DS:2 * DS], sSAC[0:DS, :],
                psChO[0:DS, 3 * DS:4 * DS], ALU.mult, ALU.add),
              reads=["psChO", "sSAC"], writes=["sHH"])

    def emit_B3a(seg, i):
        if True:
            def mm_rd(e, i=i):
                e.matmul(psTok[:, 256:256 + DS], aQhTf[:, i * C:(i + 1) * C],
                         aHH[:, i * 2 * DS:i * 2 * DS + DS], start=True, stop=False)
                e.matmul(psTok[:, 256:256 + DS], aGqTf[:, i * C:(i + 1) * C],
                         aDb[:, i * 128:i * 128 + DS], start=False, stop=True)
                e.matmul(psTok[:, 320:320 + DS], aQhTs[:, i * C:(i + 1) * C],
                         aHH[:, i * 2 * DS + DS:(i + 1) * 2 * DS],
                         start=True, stop=False)
                return e.matmul(psTok[:, 320:320 + DS], aGqTs[:, i * C:(i + 1) * C],
                                aDb[:, i * 128 + DS:(i + 1) * 128],
                                start=False, stop=True)
            A("PE", mm_rd, reads=[f"aQhTf@{i}", f"aQhTs@{i}", f"aGqTf@{i}",
                                  f"aGqTs@{i}", f"aDb@{i}", f"aHH@{i}"],
              writes=["psTok"])
            A("ACT", (lambda i_: lambda e: e.activation(
                aRbf[:, i_ * DS:(i_ + 1) * DS], psTok[:, 256:256 + DS], AF.Copy))(i),
              reads=["psTok"], writes=[f"aRbf@{i}"])
            A("ACT", (lambda i_: lambda e: e.activation(
                aRbs[:, i_ * DS:(i_ + 1) * DS], psTok[:, 320:320 + DS], AF.Copy))(i),
              reads=["psTok"], writes=[f"aRbs@{i}"])
            r = i % R
            A("DVE", (lambda i_, r_: lambda e: e.scalar_tensor_tensor(
                rg["sc1"][r_], aRbf[:, i_ * DS:(i_ + 1) * DS], -0.5,
                aVtok[:, i_ * DS:(i_ + 1) * DS], ALU.mult, ALU.add))(i, r),
              reads=[f"aRbf@{i}", f"aVtok@{i}"], writes=[N(rg["sc1"][r])])
            A("DVE", (lambda i_, r_: lambda e: e.scalar_tensor_tensor(
                rg["sc2"][r_], aRbs[:, i_ * DS:(i_ + 1) * DS], -0.5,
                rg["sc1"][r_], ALU.mult, ALU.add))(i, r),
              reads=[f"aRbs@{i}", N(rg["sc1"][r])], writes=[N(rg["sc2"][r])])
            A("DVE", (lambda r_: lambda e: e.tensor_tensor(
                rg["sc1"][r_], rg["sc2"][r_], rg["sc2"][r_], ALU.mult))(r),
              reads=[N(rg["sc2"][r])], writes=[N(rg["sc1"][r])])
            A("DVE", (lambda r_: lambda e: e.reduce_sum(
                rg["errc"][r_], rg["sc1"][r_], AX.X))(r),
              reads=[N(rg["sc1"][r])], writes=[N(rg["errc"][r])])
            A("DVE", (lambda i_, r_: lambda e: e.tensor_scalar(
                aUnb[:, i_:i_ + 1], rg["errc"][r_], TH_ERR, None, ALU.is_gt))(i, r),
              reads=[N(rg["errc"][r])], writes=[f"aUnb@{i}"])

    def emit_ns(seg, i):
        if True:
            r = i % R
            A("PE", (lambda i_: lambda e: e.matmul(
                psChO[0:1, 0:C], aUnb[:, i_:i_ + 1], sID,
                start=True, stop=True))(i),
              reads=[f"aUnb@{i}", "sID"], writes=["psChO"])
            A("DVE", lambda e: e.tensor_copy(out=rg["urow"][r],
                                             in_=psChO[0:1, 0:C]),
              reads=["psChO"], writes=[N(rg["urow"][r])])
            A("PE", lambda e: e.matmul(psChO[0:DS, 4 * DS:4 * DS + C],
                                       sONE1[:, 0:DS], rg["urow"][r],
                                       start=True, stop=True),
              reads=[N(rg["urow"][r]), "sONE1"], writes=["psChO"])
            A("DVE", lambda e: e.tensor_scalar(rg["cbc"][r],
                                               psChO[0:DS, 4 * DS:4 * DS + C],
                                               -0.1, 1.0, ALU.mult, ALU.add),
              reads=["psChO"], writes=[N(rg["cbc"][r])])
            A("DVE", (lambda i_: lambda e: e.scalar_tensor_tensor(
                rg["wns"][r], aNuT[:, i_ * C:(i_ + 1) * C], 0.1,
                psChO[0:DS, 4 * DS:4 * DS + C], ALU.mult, ALU.mult))(i),
              reads=[f"aNuT@{i}", "psChO"], writes=[N(rg["wns"][r])])
            A("DVE", lambda e: e.tensor_tensor_scan(
                rg["nsT"][r], rg["cbc"][r], rg["wns"][r], sNS0, ALU.mult, ALU.add),
              reads=[N(rg["cbc"][r]), N(rg["wns"][r]), "sNS0"],
              writes=[N(rg["nsT"][r])])
            A("DVE", lambda e: e.tensor_copy(out=sNS0, in_=rg["nsT"][r][:, C - 1:C]),
              reads=[N(rg["nsT"][r])], writes=["sNS0"])
            A("PE", lambda e: e.transpose(psChO[0:C, 6 * DS:7 * DS], rg["nsT"][r],
                                          sIDF[0:DS, 0:DS]),
              reads=[N(rg["nsT"][r]), "sIDF"], writes=["psChO"])
            A("DVE", (lambda i_: lambda e: e.tensor_copy(
                out=aNsTok[:, i_ * DS:(i_ + 1) * DS],
                in_=psChO[0:C, 6 * DS:7 * DS]))(i),
              reads=["psChO"], writes=[f"aNsTok@{i}"])

    def emit_tailc(seg, i):
        r = i % R
        cs = i * C
        xtl = rg["xtl"][r]
        gc = seg * NCH + i
        A("SP", (lambda gc_, x_: lambda e: e.dma_start(
            out=x_, in_=xchunk(gc_)))(gc, xtl),
          writes=[N(xtl)], is_dma=True)
        A("DVE", (lambda i_: lambda e: e.tensor_scalar(
            rg["sc1"][r], aDb[:, i_ * 128:i_ * 128 + DS], aQkf[:, i_:i_ + 1],
            None, ALU.mult))(i),
          reads=[f"aDb@{i}", f"aQkf@{i}"], writes=[N(rg["sc1"][r])])
        A("DVE", lambda e: e.tensor_scalar(rg["sc1"][r], rg["sc1"][r], sBF,
                                           None, ALU.mult),
          reads=["sBF"], writes=[N(rg["sc1"][r])])
        A("DVE", (lambda i_: lambda e: e.scalar_tensor_tensor(
            rg["rf"][r], aRbf[:, i_ * DS:(i_ + 1) * DS], sFA, rg["sc1"][r],
            ALU.mult, ALU.add))(i),
          reads=[f"aRbf@{i}", N(rg["sc1"][r]), "sFA"], writes=[N(rg["rf"][r])])
        A("DVE", (lambda i_: lambda e: e.tensor_scalar(
            rg["sc2"][r], aDb[:, i_ * 128 + DS:(i_ + 1) * 128],
            aQks[:, i_:i_ + 1], None, ALU.mult))(i),
          reads=[f"aDb@{i}", f"aQks@{i}"], writes=[N(rg["sc2"][r])])
        A("DVE", lambda e: e.tensor_scalar(rg["sc2"][r], rg["sc2"][r], sBS,
                                           None, ALU.mult),
          reads=["sBS"], writes=[N(rg["sc2"][r])])
        A("DVE", (lambda i_: lambda e: e.scalar_tensor_tensor(
            rg["rs"][r], aRbs[:, i_ * DS:(i_ + 1) * DS], sSA, rg["sc2"][r],
            ALU.mult, ALU.add))(i),
          reads=[f"aRbs@{i}", N(rg["sc2"][r]), "sSA"], writes=[N(rg["rs"][r])])
        pwt = aPwt[:, i * 4:i * 4 + 4]
        A("DVE", (lambda p_: lambda e: e.tensor_scalar(
            rg["o1"][r], rg["rf"][r], p_[:, 0:1], None, ALU.mult))(pwt),
          reads=[N(rg["rf"][r]), f"aPwt@{i}"], writes=[N(rg["o1"][r])])
        A("DVE", (lambda p_: lambda e: e.scalar_tensor_tensor(
            rg["o2"][r], rg["rs"][r], p_[:, 1:2], rg["o1"][r],
            ALU.mult, ALU.add))(pwt),
          reads=[N(rg["rs"][r]), f"aPwt@{i}", N(rg["o1"][r])],
          writes=[N(rg["o2"][r])])
        A("DVE", (lambda p_, i_: lambda e: e.scalar_tensor_tensor(
            rg["o1"][r], aNsTok[:, i_ * DS:(i_ + 1) * DS], p_[:, 2:3],
            rg["o2"][r], ALU.mult, ALU.add))(pwt, i),
          reads=[f"aNsTok@{i}", f"aPwt@{i}", N(rg["o2"][r])],
          writes=[N(rg["o1"][r])])
        A("ACT", lambda e: e.activation(rg["sc1"][r], rg["o1"][r], AF.Square,
                                        accum_out=rg["msc"][r]),
          reads=[N(rg["o1"][r])], writes=[N(rg["sc1"][r]), N(rg["msc"][r])])
        A("DVE", lambda e: e.tensor_scalar(rg["msc"][r], rg["msc"][r],
                                           1.0 / DS, 1e-6, ALU.mult, ALU.add),
          reads=[], writes=[N(rg["msc"][r])])
        A("ACT", lambda e: e.activation(rg["msc"][r], rg["msc"][r], AF.Ln),
          reads=[], writes=[N(rg["msc"][r])])
        A("ACT", lambda e: e.activation(rg["msc"][r], rg["msc"][r], AF.Exp,
                                        scale=-0.5),
          reads=[], writes=[N(rg["msc"][r])])
        A("DVE", lambda e: e.scalar_tensor_tensor(
            rg["ohat"][r], rg["o1"][r], rg["msc"][r], sNW, ALU.mult, ALU.mult),
          reads=[N(rg["o1"][r]), N(rg["msc"][r]), "sNW"],
          writes=[N(rg["ohat"][r])])
        A("PE", lambda e: e.transpose(psTr[0:DS, 3 * C:4 * C],
                                      rg["ohat"][r], sID),
          reads=[N(rg["ohat"][r]), "sID"], writes=["psTr"])
        A("DVE", lambda e: e.tensor_copy(out=rg["ohT"][r],
                                         in_=psTr[0:DS, 3 * C:4 * C]),
          reads=["psTr"], writes=[N(rg["ohT"][r])])
        for hf in range(2):
            def mm_wo(e, hf=hf, r=r):
                last = None
                for g in range(4):
                    gg = hf * 4 + g
                    last = e.matmul(psChO[:, g * C:(g + 1) * C],
                                    sWoT[:, gg * C:(gg + 1) * C], rg["ohT"][r],
                                    start=True, stop=True)
                return last
            A("PE", mm_wo, reads=[N(rg["ohT"][r]), "sWoT"], writes=["psChO"])
            xvh = xtl.rearrange("p (g t) -> p g t", g=8)[:, hf * 4:(hf + 1) * 4, :]
            A("DVE", (lambda xv_: lambda e: e.tensor_tensor(
                xv_, xv_, psChO.rearrange("p (g t) -> p g t", g=4), ALU.add))(xvh),
              reads=["psChO"], writes=[N(xtl)])
        op = A("SP", (lambda seg_, i_, x_: lambda e: e.dma_start(
            out=x1w2(seg_, i_), in_=x_))(seg, i, xtl),
            reads=[N(xtl)], writes=[f"dx1@{i}"], is_dma=True)
        out_dmas.append(op)
        xhat = emit_ln(xtl, N(xtl), r, i, stats_seg=seg)
        def mm_rt(e):
            last = None
            for g in range(8):
                last = e.matmul(psTok[:, 200:208], xhat[:, g * C:(g + 1) * C],
                                sGWT[:, g * 8:(g + 1) * 8],
                                start=(g == 0), stop=(g == 7))
            return last
        A("PE", mm_rt, reads=[N(xhat), "sGWT"], writes=["psTok"])
        scr, a8, b8 = rg["scr"][r], rg["sc8a"][r], rg["sc8b"][r]
        c1, c2, c3 = rg["c1"][r], rg["c2"][r], rg["c3"][r]
        A("DVE", lambda e: e.tensor_tensor(scr, psTok[:, 200:208], sLB, ALU.add),
          reads=["psTok", "sLB"], writes=[N(scr)])
        A("ACT", lambda e: e.activation(scr, scr, AF.Sigmoid),
          reads=[], writes=[N(scr)])
        A("DVE", lambda e: e.reduce_max(c1, scr, AX.X),
          reads=[N(scr)], writes=[N(c1)])
        A("DVE", lambda e: e.tensor_scalar(a8, scr, c1, None, ALU.is_ge),
          reads=[N(scr), N(c1)], writes=[N(a8)])
        A("DVE", lambda e: e.scalar_tensor_tensor(a8, a8, -BIG, sIOTA8B,
                                                  ALU.mult, ALU.add),
          reads=["sIOTA8B"], writes=[N(a8)])
        A("DVE", lambda e: e.tensor_reduce(c2, a8, AX.X, ALU.min),
          reads=[N(a8)], writes=[N(c2)])
        A("DVE", lambda e: e.tensor_scalar(a8, sIOTA8, c2, None, ALU.is_equal),
          reads=[N(c2), "sIOTA8"], writes=[N(a8)])
        A("DVE", lambda e: e.scalar_tensor_tensor(b8, a8, -BIG, scr,
                                                  ALU.mult, ALU.add),
          reads=[N(a8), N(scr)], writes=[N(b8)])
        A("DVE", lambda e: e.reduce_max(c2, b8, AX.X),
          reads=[N(b8)], writes=[N(c2)])
        A("DVE", lambda e: e.tensor_scalar(b8, b8, c2, None, ALU.is_ge),
          reads=[N(c2)], writes=[N(b8)])
        A("DVE", lambda e: e.scalar_tensor_tensor(b8, b8, -BIG, sIOTA8B,
                                                  ALU.mult, ALU.add),
          reads=["sIOTA8B"], writes=[N(b8)])
        A("DVE", lambda e: e.tensor_reduce(c3, b8, AX.X, ALU.min),
          reads=[N(b8)], writes=[N(c3)])
        A("DVE", lambda e: e.tensor_scalar(b8, sIOTA8, c3, None, ALU.is_equal),
          reads=[N(c3), "sIOTA8"], writes=[N(b8)])
        A("DVE", lambda e: e.tensor_tensor(c3, c1, c2, ALU.add),
          reads=[N(c1), N(c2)], writes=[N(c3)])
        A("DVE", lambda e: e.tensor_scalar(c3, c3, 1e-8, None, ALU.add),
          reads=[], writes=[N(c3)])
        A("DVE", lambda e: e.reciprocal(c3, c3), reads=[], writes=[N(c3)])
        A("DVE", lambda e: e.tensor_scalar(c1, c1, c3, None, ALU.mult),
          reads=[N(c3)], writes=[N(c1)])
        A("DVE", lambda e: e.tensor_scalar(c2, c2, c3, None, ALU.mult),
          reads=[N(c3)], writes=[N(c2)])
        A("DVE", lambda e: e.tensor_tensor(rg["ohs"][r], a8, b8, ALU.add),
          reads=[N(a8), N(b8)], writes=[N(rg["ohs"][r])])
        A("DVE", lambda e: e.tensor_scalar(b8, b8, c2, None, ALU.mult),
          reads=[N(c2)], writes=[N(b8)])
        A("DVE", lambda e: e.scalar_tensor_tensor(rg["ohw"][r], a8, c1, b8,
                                                  ALU.mult, ALU.add),
          reads=[N(a8), N(c1), N(b8)], writes=[N(rg["ohw"][r])])
        A("DVE", (lambda i_: lambda e: e.tensor_copy(
            out=sOHSG[:, (i_ % 4) * 8:(i_ % 4 + 1) * 8], in_=rg["ohs"][r]))(i),
          reads=[N(rg["ohs"][r])], writes=[f"sOHSG@{i % 4}"])
        def mm_part(e, i=i, r=r):
            e.matmul(psTok[0:1, 208:216], sONE128, rg["ohw"][r],
                     start=True, stop=True)
            return e.matmul(psTok[0:1, 216:224], sONE128,
                            sOHSG[:, (i % 4) * 8:(i % 4 + 1) * 8],
                            start=True, stop=True)
        A("PE", mm_part, reads=[N(rg["ohw"][r]), f"sOHSG@{i % 4}", "sONE128"],
          writes=["psTok"])
        A("DVE", lambda e: e.tensor_tensor(sPart, sPart, psTok[0:1, 208:224],
                                           ALU.add),
          reads=["psTok"], writes=["sPart"])
        if i % 4 == 3:
            g4 = i // 4
            A("PE", lambda e: e.transpose(psTr[0:32, 2 * C:3 * C], sOHSG, sID),
              reads=["sOHSG@0", "sOHSG@1", "sOHSG@2", "sOHSG@3", "sID"],
              writes=["psTr"])
            A("DVE", lambda e: e.tensor_copy(out=sSelT,
                                             in_=psTr[0:32, 2 * C:3 * C]),
              reads=["psTr"], writes=["sSelT"])
            op = A("SP", (lambda seg_, g4_: lambda e: e.dma_start(
                out=selsw3(seg_, g4_), in_=sSelT))(seg, g4),
                reads=["sSelT"], writes=[f"dsels@{g4}"], is_dma=True)
            out_dmas.append(op)

    for seg in range(NSEG):
        if seg % 2 == 0:
            A("DVE", lambda e: e.tensor_copy(out=sHH, in_=sZeroHH),
              reads=["sZeroHH"], writes=["sHH"])
            A("DVE", lambda e: e.tensor_copy(out=sNS0, in_=sZeroHH[:, 0:1]),
              reads=["sZeroHH"], writes=["sNS0"])
        for i in range(NCH):
            emit_prep(seg, i)
        for i in range(NCH + 4):
            if i < NCH:
                emit_B1(seg, i)
            if 2 <= i < NCH + 2:
                emit_B3a(seg, i - 2)
            if i >= 4:
                emit_ns(seg, i - 4)
        for i in range(NCH):
            emit_tailc(seg, i)

    A("DVE", lambda e: e.tensor_copy(out=sTxPart, in_=sPart),
      reads=["sPart"], writes=["sTxPart"])
    def emit_txp(e):
        slot = e.value_load(sRelay[0:1, 0:1])
        return e.dma_start(out=shPart[bass.ts(slot, 1), :], in_=sTxPart)
    A("SP", emit_txp, reads=["sTxPart", "sRelay"], writes=["txp"], is_dma=True)
    def emit_txf(e):
        slot = e.value_load(sRelay[0:1, 0:1])
        return e.dma_start(out=shFlag[bass.ts(slot, 1), 0:1], in_=sNonceS)
    A("SP", emit_txf, reads=["txp", "sNonceS", "sRelay"], writes=["txf"],
      is_dma=True)
    def emit_spin(e):
        part = e.value_load(sRelay[0:1, 1:2])
        nonce = e.value_load(sRelay[0:1, 3:4])
        with e.register("r_cnt") as r_cnt, e.register("r_cond") as r_cond, \
             e.register("r_del") as r_del:
            e.reg_mov(r_cnt, 0)
            e.reg_mov(r_cond, 1)
            e.reg_mov(r_del, 0)
            with e.While(lambda: r_cond):
                for _ in range(400):
                    e.reg_add(r_del, r_del, 1)
                e.dma_start(out=sFlagRx,
                            in_=shFlag[bass.ts(part, 1), 0:1]
                            ).then_inc(spin_sem, 16)
                e.reg_add(r_cnt, r_cnt, 16)
                e.wait_ge(spin_sem, r_cnt)
                v = e.value_load(sFlagRx)
                e.reg_alu(r_cond, v, nonce, ALU.not_equal)
                with e.If_cmp(r_cnt, 16 * 4000000, "IS_GE"):
                    e.reg_mov(r_cond, 0)
        return e.nop()
    A("SP", emit_spin, reads=["sRelay", "txf"], writes=["spun", "sFlagRx"],
      tag="spin")
    def emit_rxp(e):
        part = e.value_load(sRelay[0:1, 1:2])
        return e.dma_start(out=sRxPart, in_=shPart[bass.ts(part, 1), :])
    A("SP", emit_rxp, reads=["spun", "sRelay"], writes=["sRxPart"], is_dma=True)
    A("DVE", lambda e: e.tensor_tensor(sWm, sPart, sRxPart, ALU.add),
      reads=["sPart", "sRxPart"], writes=["sWm"])
    A("DVE", lambda e: e.tensor_scalar(sWm[:, 8:16], sWm[:, 8:16], 1.0, None,
                                       ALU.max), reads=[], writes=["sWm"])
    A("DVE", lambda e: e.reciprocal(sWm[:, 8:16], sWm[:, 8:16]),
      reads=[], writes=["sWm"])
    A("DVE", lambda e: e.tensor_tensor(sWm[:, 0:8], sWm[:, 0:8], sWm[:, 8:16],
                                       ALU.mult), reads=[], writes=["sWm"])
    A("SP", lambda e: e.dma_start(out=wmScr, in_=sWm), reads=["sWm"],
      writes=["wmScr"], is_dma=True)
    A("SP", lambda e: e.dma_start(out=sWmbc, in_=_bpar(wmScr)),
      reads=["wmScr"], writes=["sWmbc"], is_dma=True)

    QC = D * TOK // 4
    for j in range(4):
        def cp_x1(e, j_=j):
            fq = fqreg[0]
            return e.dma_start(
                out=dataclasses.replace(
                    x1Own, offset=x1Own.offset + j_ * QC, ap=[[1, QC]]),
                in_=dataclasses.replace(
                    x1Scr, offset=x1Scr.offset + j_ * QC,
                    ap=[[D * TOK, NSEG], [1, QC]])[bass.ts(fq, 1), :])
        op = A("SP", cp_x1, reads=["fqreg"] + [f"dx1@{i}" for i in range(NCH)],
               writes=["x1Own"], is_dma=True)
        out_dmas.append(op)
    def cp_stats(e):
        fq = fqreg[0]
        return e.dma_start(out=statsOwn,
                           in_=statsScr[bass.ts(fq, 1), :])
    op = A("SP", cp_stats,
           reads=["fqreg"] + [f"dstats@{i}" for i in range(NCH)],
           writes=["statsOwn"], is_dma=True)
    out_dmas.append(op)
    def cp_sels(e):
        fq = fqreg[0]
        return e.dma_start(out=selsOwn,
                           in_=selsScr[bass.ts(fq, 1), :])
    op = A("SP", cp_sels,
           reads=["fqreg"] + [f"dsels@{i}" for i in range(NG)],
           writes=["selsOwn"], is_dma=True)
    out_dmas.append(op)

    s.set_barrier()
    ctxA.pop_all().close()

    x_q = sbuf("x_q", [128, 8 * QTOK])
    xhatF = sbuf("xhatF", [128, 8 * QTOK], BF16)
    slWG = [sbuf(f"slWG{j}", [128, 8 * GRP], BF16) for j in range(2)]
    slWU = [sbuf(f"slWU{j}", [128, 8 * GRP], BF16) for j in range(2)]
    slWD = [sbuf(f"slWD{j}", [128, 4 * D], BF16) for j in range(2)]
    sSilu = sbuf("sSilu", [128, 4 * GRP], BF16)
    sT1 = sbuf("sT1", [128, 4 * GRP], BF16)
    sSel = sbuf("sSel", [128, GRP], BF16)
    sDn = sbuf("sDn", [128, 4 * GRP], BF16)
    sMbc = sbuf("sMbc", [128, GRP])
    sRbc = sbuf("sRbc", [128, GRP])
    sLnT = sbuf("sLnT", [128, 8 * GRP])
    psG = ctx.enter_context(nc.psum_tensor("psG", [128, 2048], F32)).ap()
    psU = ctx.enter_context(nc.psum_tensor("psU", [128, 2048], F32)).ap()

    for p_i in range(NPASS):
        q0 = p_i * QTOK
        A("SP", (lambda pi_: lambda e: e.dma_start(
            out=x_q, in_=x1rd3(pi_)))(p_i),
          reads=["x1Own"],
          writes=["x_q"], is_dma=True)
        for gq in range(GPP):
            g = p_i * GPP + gq
            cs = g * GRP
            xv = x_q.rearrange("p (c g t) -> p c g t", c=8, g=8, t=128)[
                :, gq * 4:(gq + 1) * 4, :, :]
            A("SP", (lambda cs_: lambda e: e.dma_start(
                out=sMbc, in_=_bpar(statsOwn[0:1, cs_:cs_ + GRP])))(cs),
              reads=["statsOwn"],
              writes=["sMbc"], is_dma=True)
            A("SP", (lambda cs_: lambda e: e.dma_start(
                out=sRbc, in_=_bpar(statsOwn[0:1, TOK + cs_:TOK + cs_ + GRP])))(cs),
              reads=["statsOwn"],
              writes=["sRbc"], is_dma=True)
            xho = xhatF.rearrange("p (g c t) -> p c g t", g=8, c=8, t=128)[
                :, gq * 4:(gq + 1) * 4, :, :]
            ln4 = sLnT.rearrange("p (g c t) -> p c g t", g=8, c=4, t=128)
            def bc4(bc):
                return dataclasses.replace(
                    bc, ap=[list(bc.ap[0]), [128, 4], [0, 8], [1, 128]])
            A("DVE", (lambda xv_: lambda e: e.tensor_tensor(
                ln4, xv_, bc4(sMbc), ALU.subtract))(xv),
              reads=["x_q", "sMbc"], writes=["sLnT"])
            A("DVE", (lambda xho_: lambda e: e.tensor_tensor(
                xho_, ln4, bc4(sRbc), ALU.mult))(xho),
              reads=["sLnT", "sRbc"], writes=[f"xhatF@{gq}"])
        for e_i in range(10):
            sl = e_i % 2
            A("SP", (lambda ei_, sl_: lambda e: e.dma_start(
                out=slWG[sl_], in_=scrWG[ei_]))(e_i, sl),
              reads=[f"castg{e_i}"], writes=[f"slWG{sl}"], is_dma=True)
            A("SP", (lambda ei_, sl_: lambda e: e.dma_start(
                out=slWU[sl_], in_=scrWU[ei_]))(e_i, sl),
              reads=[f"castu{e_i}"], writes=[f"slWU{sl}"], is_dma=True)
            A("SP", (lambda ei_, sl_: lambda e: e.dma_start(
                out=slWD[sl_], in_=scrWD[ei_]))(e_i, sl),
              reads=[f"castd{e_i}"], writes=[f"slWD{sl}"], is_dma=True)
            for gq in range(GPP):
                g = p_i * GPP + gq
                cs = g * GRP
                if e_i >= 2:
                    A("SP", (lambda ei_, cs_: lambda e: e.dma_start(
                        out=sSel, in_=_bpar(selsOwn[0:1, (ei_ - 2) * TOK + cs_:
                                                    (ei_ - 2) * TOK + cs_ + GRP])))(e_i, cs),
                      reads=["selsOwn"],
                      writes=["sSel"], is_dma=True)
                def mm_gate(e, sl=sl, gq=gq):
                    last = None
                    for it in range(4):
                        for g8 in range(8):
                            last = e.matmul(
                                psG[:, it * GRP:(it + 1) * GRP],
                                slWG[sl][:, g8 * GRP + it * 128:
                                         g8 * GRP + (it + 1) * 128],
                                xhatF[:, g8 * QTOK + gq * GRP:
                                      g8 * QTOK + (gq + 1) * GRP],
                                start=(g8 == 0), stop=(g8 == 7))
                    return last
                A("PE", mm_gate, reads=[f"slWG{sl}", f"xhatF@{gq}"], writes=["psG"])
                A("ACT", lambda e: e.activation(sSilu, psG, AF.Silu),
                  reads=["psG"], writes=["sSilu"])
                def mm_up(e, sl=sl, gq=gq):
                    last = None
                    for it in range(4):
                        for g8 in range(8):
                            last = e.matmul(
                                psU[:, it * GRP:(it + 1) * GRP],
                                slWU[sl][:, g8 * GRP + it * 128:
                                         g8 * GRP + (it + 1) * 128],
                                xhatF[:, g8 * QTOK + gq * GRP:
                                      g8 * QTOK + (gq + 1) * GRP],
                                start=(g8 == 0), stop=(g8 == 7))
                    return last
                A("PE", mm_up, reads=[f"slWU{sl}", f"xhatF@{gq}"], writes=["psU"])
                A("DVE", lambda e: e.tensor_tensor(sT1, psU, sSilu, ALU.mult),
                  reads=["psU", "sSilu"], writes=["sT1"])
                if e_i >= 2:
                    A("DVE", (lambda ei_: lambda e: e.scalar_tensor_tensor(
                        sT1.rearrange("p (i t) -> p i t", i=4),
                        sT1.rearrange("p (i t) -> p i t", i=4),
                        sWmbc[:, ei_ - 2:ei_ - 1], _rep(sSel, 4),
                        ALU.mult, ALU.mult))(e_i),
                      reads=["sWmbc", "sSel"], writes=["sT1"])
                for hf in range(2):
                    ps = psG if hf == 0 else psU
                    pk = "psG" if hf == 0 else "psU"
                    def mm_dn(e, sl=sl, hf=hf, ps=ps):
                        last = None
                        for ht in range(4):
                            gg = hf * 4 + ht
                            for ic in range(4):
                                last = e.matmul(
                                    ps[:, ht * GRP:(ht + 1) * GRP],
                                    slWD[sl][:, ic * D + gg * 128:
                                             ic * D + (gg + 1) * 128],
                                    sT1[:, ic * GRP:(ic + 1) * GRP],
                                    start=(ic == 0), stop=(ic == 3))
                        return last
                    A("PE", mm_dn, reads=[f"slWD{sl}", "sT1"], writes=[pk])
                    xo = x_q.rearrange("p (c g t) -> p g c t",
                                       c=8, g=8, t=128)[
                        :, hf * 4:(hf + 1) * 4, gq * 4:(gq + 1) * 4, :]
                    if hf == 0:
                        A("DVE", (lambda xo_, ps_: lambda e: e.tensor_tensor(
                            xo_, xo_,
                            ps_.rearrange("p (g c t) -> p g c t",
                                          g=4, c=4, t=128),
                            ALU.add))(xo, ps),
                          reads=[pk], writes=["x_q"])
                    else:
                        A("ACT", (lambda ps_: lambda e: e.activation(
                            sDn, ps_, AF.Copy))(ps),
                          reads=[pk], writes=["sDn"])
                        A("POOL", (lambda xo_: lambda e: e.tensor_tensor(
                            xo_, xo_,
                            sDn.rearrange("p (g c t) -> p g c t",
                                          g=4, c=4, t=128),
                            ALU.add))(xo),
                          reads=["sDn"], writes=["x_q"])
        op = A("SP", (lambda pi_: lambda e: e.dma_start(
            out=outw3(pi_), in_=x_q))(p_i),
            reads=["x_q"], is_dma=True)
        out_dmas.append(op)
    A("SP", lambda e: e.nop(), dep_on=[o for o in out_dmas if o is not None])
    if os.environ.get("MK_PRCNT"):
        print("opcount:", len(s.ops))
    s.run()
    ctx.pop_all()
    return nc


_CACHE = {}


def _consts():
    t = np.arange(128, dtype=np.float32)
    TT, SS = np.meshgrid(t, t, indexing="ij")
    exTS = (TT - 1 - SS).astype(np.float32)
    return dict(
        identb=np.eye(128, dtype=nbf),
        exTS=exTS,
        exST=np.ascontiguousarray(exTS.T),
        mskSL=(SS < TT).astype(np.float32),
        mskSU=np.ascontiguousarray((SS < TT).astype(np.float32).T),
        iotaRow=np.ascontiguousarray(t.reshape(1, 128)),
        iotaCol=np.ascontiguousarray(t.reshape(128, 1)),
        iotaColRev=np.ascontiguousarray((127.0 - t).reshape(128, 1)),
    )


def kernel(L0_attn_x, L0_attn_Wk, L0_attn_Wv, L0_attn_Wq, L0_attn_fast_alpha,
           L0_attn_slow_alpha, L0_attn_nm_w1, L0_attn_nm_b1, L0_attn_nm_w2,
           L0_attn_nm_b2, L0_attn_ir_w1, L0_attn_ir_b1, L0_attn_ir_w2,
           L0_attn_ir_b2, L0_attn_pw, L0_attn_Wo, L0_attn_norm_w,
           L0_ffn_gate_w, L0_ffn_lb_bias, L0_ffn_expert_gate_w,
           L0_ffn_expert_up_w, L0_ffn_expert_down_w, L0_ffn_shared_gate_w,
           L0_ffn_shared_up_w, L0_ffn_shared_down_w):
    f = np.float32
    asn = np.ascontiguousarray
    x = np.asarray(L0_attn_x, f)
    consts = _consts()

    if "all" not in _CACHE:
        _CACHE["all"] = build_all()
    nc = _CACHE["all"]

    sgT = np.asarray(L0_ffn_shared_gate_w, f).T
    suT = np.asarray(L0_ffn_shared_up_w, f).T
    sdT = np.asarray(L0_ffn_shared_down_w, f).T
    eg = np.asarray(L0_ffn_expert_gate_w, f)
    eu = np.asarray(L0_ffn_expert_up_w, f)
    ed = np.asarray(L0_ffn_expert_down_w, f)
    wg = np.empty((10, D, GRP), f); wu = np.empty((10, D, GRP), f)
    wd = np.empty((10, GRP, D), f)
    wg[0] = sgT[:, 0:GRP]; wg[1] = sgT[:, GRP:2 * GRP]
    wu[0] = suT[:, 0:GRP]; wu[1] = suT[:, GRP:2 * GRP]
    wd[0] = sdT[0:GRP, :]; wd[1] = sdT[GRP:2 * GRP, :]
    for e_i in range(8):
        wg[2 + e_i] = eg[e_i].T
        wu[2 + e_i] = eu[e_i].T
        wd[2 + e_i] = ed[e_i].T

    common = dict(
        WkT=asn(np.asarray(L0_attn_Wk, f).T), WvT=asn(np.asarray(L0_attn_Wv, f).T),
        WqT=asn(np.asarray(L0_attn_Wq, f).T), ir1T=asn(np.asarray(L0_attn_ir_w1, f).T),
        ir2T=asn(np.asarray(L0_attn_ir_w2, f).T), pwT=asn(np.asarray(L0_attn_pw, f).T),
        nm1T=asn(np.asarray(L0_attn_nm_w1, f).T), nm2T=asn(np.asarray(L0_attn_nm_w2, f).T),
        nmb1=np.asarray(L0_attn_nm_b1, f).reshape(-1, 1),
        nmb2=np.asarray(L0_attn_nm_b2, f).reshape(-1, 1),
        irb1=np.asarray(L0_attn_ir_b1, f).reshape(-1, 1),
        irb2=np.asarray(L0_attn_ir_b2, f).reshape(1, 1),
        WoT=asn(np.asarray(L0_attn_Wo, f).T),
        normw=np.asarray(L0_attn_norm_w, f).reshape(1, -1),
        gwT=asn(np.asarray(L0_ffn_gate_w, f).T),
        lb=np.asarray(L0_ffn_lb_bias, f).reshape(1, -1),
        fa=np.asarray(L0_attn_fast_alpha, f).reshape(1, 1),
        sa=np.asarray(L0_attn_slow_alpha, f).reshape(1, 1),
        wgT=wg, wuT=wu, wdT=wd,
        **consts,
    )
    nonce = _fresh_nonce()
    in_maps = []
    for c in range(8):
        parity = c % 2
        p = c // 2
        b0 = 2 * parity
        m = dict(common)
        xpair = np.concatenate([x[b0], x[b0 + 1]], axis=0)
        m["xC"] = asn(xpair.reshape(128, 128, 8, 128).transpose(0, 3, 2, 1)
                      .reshape(16384, D))
        m["fq"] = np.array([[p]], np.int32)
        m["relay"] = np.array([[parity, 1 - parity, 1, nonce + c // 2]],
                              np.int32)
        in_maps.append(m)
    res = run_bass_kernel_spmd(nc, in_maps, core_ids=list(range(8)))
    r = res.results

    out = np.empty((4, 8192, D), f)
    for c in range(8):
        parity = c % 2
        p = c // 2
        b = 2 * parity + p // 2
        h = p % 2
        a = np.asarray(r[c]["outT"], f)
        out[b, h * TOK:(h + 1) * TOK, :] = (
            a.reshape(32, 128, 8, 128).transpose(0, 3, 2, 1).reshape(TOK, D))
    return out
